# version 10
# baseline (speedup 1.0000x reference)
import numpy as np
import ml_dtypes
import concourse.bass as bass
import concourse.mybir as mybir
from concourse.bass_utils import run_bass_kernel_spmd
from concourse.replica_groups import maybe_share_collective_output_space

F32 = mybir.dt.float32
BF16 = mybir.dt.bfloat16
I32 = mybir.dt.int32
AF = mybir.ActivationFunctionType
ALU = mybir.AluOpType

NCORE = 8
B, L, D = 2, 16384, 1024
CPS = NCORE // B
C = L // CPS
SUB = 512
NSUB = C // SUB
NCH = D // 128
NE, DH = 6, 512
NKH = DH // 128
TWO_PI = float(np.float32(2 * np.pi))
INV_2PI = float(np.float32(1.0 / (2 * np.pi)))
TP_HI = float(np.float32(2 * np.pi))
TP_LO = float(np.float32(2 * np.pi - np.float64(np.float32(2 * np.pi))))
PI_CL = float(np.float32(3.1415925))

_CACHE = {}


class Prog:
    ENGS = ("sync", "tensor", "vector", "scalar", "gpsimd")
    SEMS = ("L", "S", "WD", "MM", "VE", "SC", "GP", "CC", "PH", "CO")
    DMA_SEMS = ("L", "S", "WD", "CC")

    def __init__(self):
        self.q = {e: [] for e in self.ENGS}
        self.cnt = {s: 0 for s in self.SEMS}
        self.sems = {}
        self.lastwait = {}

    def w(self, eng, sem, val):
        if val is None or val <= 0:
            return
        k = (eng, sem)
        if self.lastwait.get(k, -1) >= val:
            return
        self.lastwait[k] = val

        def f(e, sem=sem, val=val):
            e.wait_ge(self.sems[sem], val)
        self.q[eng].append(f)

    def do(self, eng, build, inc=None, waits=(), attach=False):
        for wt in waits:
            if wt is None:
                continue
            sem, val = wt
            self.w(eng, sem, val)
        sem = None
        amt = 0
        if inc is not None:
            if isinstance(inc, tuple):
                sem, amt = inc
            else:
                sem = inc
                amt = 16 if sem in self.DMA_SEMS else 1
            self.cnt[sem] += amt

        def f(e, build=build, sem=sem, amt=amt, eng=eng, attach=attach):
            ins = build(e)
            if sem is not None:
                if attach or amt == 16 or eng == "sync":
                    ins.then_inc(self.sems[sem], amt)
                else:
                    e.drain().then_inc(self.sems[sem], amt)
        self.q[eng].append(f)
        return self.cnt[sem] if sem is not None else None

    def barrier(self):
        comp = ("tensor", "vector", "scalar", "gpsimd")
        target = self.cnt["PH"] + len(comp)
        for eng in comp:
            def f(e):
                e.drain().then_inc(self.sems["PH"], 1)
            self.q[eng].append(f)
        self.cnt["PH"] = target
        for eng in self.ENGS:
            for sem in ("S", "L", "WD", "CC"):
                self.w(eng, sem, self.cnt[sem])
            self.w(eng, "PH", target)
        return target


class Alo:
    def __init__(self, t, total, name):
        self.t, self.total, self.name = t, total, name
        self.off = 0
        self.marks = []

    def take(self, n):
        o = self.off
        assert o + n <= self.total, f"{self.name} overflow {o}+{n}>{self.total}"
        self.off += n
        return self.t[:, o:o + n]

    def push(self):
        self.marks.append(self.off)

    def pop(self):
        self.off = self.marks.pop()


def build_program():
    nc = bass.Bass("TRN2", target_bir_lowering=False, debug=False, detect_race_conditions=False)

    dI = {}
    def din(name, shape, dt=F32):
        dI[name] = nc.dram_tensor(name, shape, dt, kind="ExternalInput").ap()
    din("x", [C, D])
    din("wpsiT", [D, 2 * D]); din("wphiT", [D, 2 * D]); din("wgateT", [D, D])
    din("bpsi", [2 * D]); din("bphi", [2 * D]); din("bgate", [D])
    din("omega", [D]); din("loggam", [D]); din("dtv", [1]); din("rtemp", [1])
    din("projwT", [D, NE]); din("anchT", [NE, NE])
    din("w1", [NE, D, DH]); din("w2", [NE, DH, D])
    din("identf", [128, 128]); din("identb", [128, 128], BF16)
    din("ident6", [6, 6])
    din("onesrow_b", [1, 128], BF16)
    din("ehot", [128, 16], BF16)
    din("bsel", [6, 768], BF16)
    din("sel", [1, 24])

    out_d = nc.dram_tensor("out", [C, D], F32, kind="ExternalOutput").ap()

    j_re = nc.dram_tensor("j_re", [NSUB, 128, NCH * SUB], BF16, kind="Internal").ap()
    j_im = nc.dram_tensor("j_im", [NSUB, 128, NCH * SUB], BF16, kind="Internal").ap()
    xn_sc = nc.dram_tensor("xn_sc", [NSUB, 128, NCH * SUB], BF16, kind="Internal").ap()
    w1sc = nc.dram_tensor("w1sc", [NE, D, DH], BF16, kind="Internal").ap()
    w2sc = nc.dram_tensor("w2sc", [NE, DH, D], BF16, kind="Internal").ap()
    cc_in = nc.dram_tensor("cc_in", [128 * 16], F32, kind="Internal").ap()
    rg = [list(range(NCORE))]
    cc_out = nc.dram_tensor(
        "cc_out", [NCORE, 128 * 16], F32, kind="Internal",
        addr_space=maybe_share_collective_output_space("AllGather", rg)).ap()

    AB_N = 75264
    AF_N = 11776

    import contextlib
    with contextlib.ExitStack() as _ctx:
        ec = _ctx.enter_context
        AB_t = ec(nc.sbuf_tensor("AB", [128, AB_N], BF16))
        AF_t = ec(nc.sbuf_tensor("AF", [128, AF_N], F32))
        PRM = ec(nc.sbuf_tensor("PRM", [128, 336], F32))
        PBt = [ec(nc.psum_tensor("PB%d" % i, [128, 512], F32)) for i in range(8)]
        (PB0, PB1, PB2, PB3, PB4, PB5, PB6, PB7) = PBt
        sems = {n: ec(nc.semaphore(n))
                for n in ("L", "S", "WD", "MM", "VE", "SC", "GP", "CC", "PH", "CO")}
        (sem_L, sem_S, sem_WD, sem_MM, sem_VE, sem_SC, sem_GP, sem_CC,
         sem_PH, sem_CO) = (sems[n] for n in ("L", "S", "WD", "MM", "VE", "SC",
                                              "GP", "CC", "PH", "CO"))
        block = ec(nc.Block())
        p = Prog()
        p.sems = dict(L=sem_L, S=sem_S, WD=sem_WD, MM=sem_MM, VE=sem_VE,
                      SC=sem_SC, GP=sem_GP, CC=sem_CC, PH=sem_PH, CO=sem_CO)
        PB = [PB0, PB1, PB2, PB3, PB4, PB5, PB6, PB7]
        PBb = [b[:].bitcast(BF16) for b in PB]

        AB = Alo(AB_t[:], AB_N, "AB")
        AFa = Alo(AF_t[:], AF_N, "AF")

        lg_c = PRM[:, 0:8]; om_c = PRM[:, 8:16]
        del_c = PRM[:, 16:24]; rho_c = PRM[:, 24:32]; th_c = PRM[:, 32:40]
        bpsi_c = PRM[:, 40:56]; bphi_c = PRM[:, 56:72]; bgate_c = PRM[:, 72:80]
        gc_c = PRM[:, 80:88]; gs_c = PRM[:, 88:96]
        ec_c = PRM[:, 96:104]; es_c = PRM[:, 104:112]
        seam_re = PRM[:, 112:120]; seam_im = PRM[:, 120:128]
        scols = PRM[:, 128:144]
        sin_re = PRM[:, 144:152]; sin_im = PRM[:, 152:160]
        P_b = PRM[:, 160:192]
        mcol = PRM[:, 192:200]
        selR = PRM[:, 200:208]
        colA = PRM[:, 208:216]; colB = PRM[:, 216:224]
        icol = PRM[:, 224:232].bitcast(I32)
        dcol = PRM[:, 232:240]
        accR = PRM[:, 240:248]; accI = PRM[:, 248:256]
        fr_c = PRM[:, 256:264]; fr_s = PRM[:, 264:272]
        selI = PRM[:, 272:280]
        ms0 = PRM[:, 280:281]; ms1 = PRM[:, 281:282]
        ms2 = PRM[:, 282:283]
        SFAC = PRM[:, 284:285]; NB6 = PRM[:, 285:286]
        rowstg = PRM[0:2, 286:318]
        DT_COL = P_b[:, 2:3]; RT_COL = P_b[:, 3:4]
        rsA = PRM[:, 320:328]

        def SEL_COL(m, r):
            return P_b[:, 4 + m * 8 + r:5 + m * 8 + r]

        identb_s = AB.take(128)
        onesr_b = AB.take(128)
        ehot_s = AB.take(16)
        bsel_s = AB.take(768)
        ew_tm = AB.take(NSUB * 24)
        zre_s = AB.take(64); zim_s = AB.take(64)

        identf_s = AFa.take(128)
        ident6_s = AFa.take(6)
        anch_s = AFa.take(6)
        s_all = AFa.take(128)
        ones_rf = AFa.take(128)

        def dma(eng, sem, out, in_, waits=(), slow=False):
            if slow:
                return p.do(eng, lambda e, o=out, i=in_: e.dma_start(
                    out=o, in_=i, allow_slow_non_contiguous=True),
                    inc=sem, waits=waits)
            return p.do(eng, lambda e, o=out, i=in_: e.dma_start(out=o, in_=i),
                        inc=sem, waits=waits)

        def chunk(ap, c, w=SUB):
            return ap[:, c * w:(c + 1) * w]

        def rr_sin(dst, ang, shift, sc_extra=()):
            p.do("vector", lambda e, o=colA, i=ang, s=float(shift):
                 e.tensor_scalar(o, i, INV_2PI, s, ALU.mult, ALU.add), inc="VE",
                 waits=[("SC", p.cnt["SC"])])
            p.do("vector", lambda e, o=icol, i=colA: e.tensor_copy(o, i), inc="VE")
            p.do("vector", lambda e, o=colB, i=icol: e.tensor_copy(o, i), inc="VE")
            p.do("vector", lambda e, o=colA, a=colA, b=colB: e.tensor_sub(o, a, b),
                 inc="VE")
            p.do("vector", lambda e, o=rsA, i=colA: e.tensor_single_scalar(
                o, i, 0.5, ALU.is_gt), inc="VE")
            p.do("vector", lambda e, o=colB, a=colB, b=rsA: e.tensor_add(o, a, b),
                 inc="VE")
            p.do("vector", lambda e, o=rsA, i=colA: e.tensor_single_scalar(
                o, i, -0.5, ALU.is_lt), inc="VE")
            p.do("vector", lambda e, o=colB, a=colB, b=rsA: e.tensor_sub(o, a, b),
                 inc="VE")
            if float(shift) != 0.0:
                p.do("vector", lambda e, o=colB, i=colB, s=-float(shift):
                     e.tensor_scalar(o, i, s, None, ALU.add), inc="VE")
            p.do("vector", lambda e, o=colA, k=colB, a=ang:
                 e.scalar_tensor_tensor(o, k, -TP_HI, a, ALU.mult, ALU.add), inc="VE")
            p.do("vector", lambda e, o=colA, k=colB, a=colA:
                 e.scalar_tensor_tensor(o, k, -TP_LO, a, ALU.mult, ALU.add), inc="VE")
            p.do("vector", lambda e, o=colA: e.tensor_single_scalar(
                o, colA, PI_CL, ALU.min), inc="VE")
            ve = p.do("vector", lambda e, o=colA: e.tensor_single_scalar(
                o, colA, -PI_CL, ALU.max), inc="VE")
            return p.do("scalar", lambda e, o=dst, i=colA: e.activation(
                o, i, AF.Sin), inc="SC",
                waits=[("VE", ve)] + list(sc_extra))

        def cs_cols(dst_c, dst_s, pimm):
            p.do("vector", lambda e, o=dcol, s=float(pimm): e.tensor_scalar(
                o, th_c, s, None, ALU.mult), inc="VE")
            rr_sin(dst_c, dcol, 0.25)
            return rr_sin(dst_s, dcol, 0.0)

        AB.push()
        tabC = AB.take(NCH * SUB)
        tabS = AB.take(NCH * SUB)
        tabR = AB.take(NCH * SUB)
        AB.push()
        w_psi = AB.take(NCH * 2048)
        w_gate = AB.take(NCH * 1024)
        pw_hi = AB.take(NCH * 8)
        pw_lo = AB.take(NCH * 8)

        AFa.push()
        stage = AFa.take(4096)
        sc_a = AFa.take(512); sc_b = AFa.take(512)
        sc_i = AFa.take(512)
        tau_row = AFa.take(SUB)
        assert AFa.off <= AF_N

        wd = 0
        for (dst, src) in (
            (lg_c, dI["loggam"].rearrange("(c p) -> p c", p=128)),
            (om_c, dI["omega"].rearrange("(c p) -> p c", p=128)),
            (bpsi_c, dI["bpsi"].rearrange("(c p) -> p c", p=128)),
            (bphi_c, dI["bphi"].rearrange("(c p) -> p c", p=128)),
            (bgate_c, dI["bgate"].rearrange("(c p) -> p c", p=128)),
            (rowstg[0:1, 0:1], dI["dtv"].rearrange("(a b) -> a b", a=1)),
            (rowstg[0:1, 1:2], dI["rtemp"].rearrange("(a b) -> a b", a=1)),
            (rowstg[0:1, 4:28], dI["sel"]),
            (identf_s, dI["identf"]),
            (identb_s, dI["identb"]),
            (ident6_s[0:6, :], dI["ident6"]),
            (anch_s[0:6, :], dI["anchT"]),
            (onesr_b[0:1, :], dI["onesrow_b"]),
            (ehot_s, dI["ehot"]),
            (bsel_s[0:6, :], dI["bsel"]),
        ):
            wd = dma("scalar", "WD", dst, src, slow=True)

        sc = p.do("scalar", lambda e: e.activation(
            rowstg[0:1, 2:3], rowstg[0:1, 0:1], AF.Abs), inc="SC",
            waits=[("WD", wd)])
        ve = p.do("vector", lambda e: e.tensor_copy(
            rowstg[0:1, 3:4], rowstg[0:1, 1:2]), inc="VE", waits=[("SC", sc)])
        ve = p.do("vector", lambda e: e.memset(ones_rf[0:1, :], 1.0), inc="VE")
        mm = p.do("tensor", lambda e: e.matmul(
            PB7[:, 0:28], ones_rf[0:1, :], rowstg[0:1, 0:28],
            start=True, stop=True), inc="MM", waits=[("VE", ve)])
        ve = p.do("vector", lambda e: e.tensor_copy(P_b[:, 0:28], PB7[:, 0:28]), inc="VE",
                  waits=[("MM", mm)])

        sc = p.do("scalar", lambda e: e.activation(del_c, lg_c, AF.Exp), inc="SC",
                  waits=[("VE", ve)])
        ve = p.do("vector", lambda e: e.tensor_scalar(
            del_c, del_c, DT_COL, None, ALU.mult), inc="VE", waits=[("SC", sc)])
        sc = p.do("scalar", lambda e: e.activation(
            rho_c, del_c, AF.Exp, scale=np.float32(-1.0)), inc="SC", waits=[("VE", ve)])
        ve_th = p.do("vector", lambda e: e.tensor_scalar(
            th_c, om_c, DT_COL, None, ALU.mult), inc="VE")

        gp = p.do("gpsimd", lambda e: e.iota(
            tau_row, pattern=[[1, SUB]], base=0, channel_multiplier=0,
            allow_small_or_imprecise_dtypes=True), inc="GP")

        sc_iI = sc_i.bitcast(I32)
        sc_d = stage[:, 512:1024]

        def rr_sin_big(dst, ang, shift, waits=()):
            p.do("vector", lambda e, o=sc_a, i=ang, s=float(shift):
                 e.tensor_scalar(o, i, INV_2PI, s, ALU.mult, ALU.add), inc="VE",
                 waits=list(waits) + [("SC", p.cnt["SC"])])
            p.do("vector", lambda e, o=sc_iI, i=sc_a: e.tensor_copy(o, i), inc="VE")
            p.do("vector", lambda e, o=sc_b, i=sc_iI: e.tensor_copy(o, i), inc="VE")
            p.do("vector", lambda e, o=sc_a, a=sc_a, b=sc_b: e.tensor_sub(o, a, b),
                 inc="VE")
            p.do("vector", lambda e, o=sc_d, i=sc_a: e.tensor_single_scalar(
                o, i, 0.5, ALU.is_gt), inc="VE")
            p.do("vector", lambda e, o=sc_b, a=sc_b, b=sc_d: e.tensor_add(o, a, b),
                 inc="VE")
            p.do("vector", lambda e, o=sc_d, i=sc_a: e.tensor_single_scalar(
                o, i, -0.5, ALU.is_lt), inc="VE")
            p.do("vector", lambda e, o=sc_b, a=sc_b, b=sc_d: e.tensor_sub(o, a, b),
                 inc="VE")
            if float(shift) != 0.0:
                p.do("vector", lambda e, o=sc_b, i=sc_b, s=-float(shift):
                     e.tensor_scalar(o, i, s, None, ALU.add), inc="VE")
            p.do("vector", lambda e, o=sc_a, k=sc_b, a=ang:
                 e.scalar_tensor_tensor(o, k, -TP_HI, a, ALU.mult, ALU.add), inc="VE")
            p.do("vector", lambda e, o=sc_a, k=sc_b, a=sc_a:
                 e.scalar_tensor_tensor(o, k, -TP_LO, a, ALU.mult, ALU.add), inc="VE")
            p.do("vector", lambda e, o=sc_a: e.tensor_single_scalar(
                o, sc_a, PI_CL, ALU.min), inc="VE")
            ve2 = p.do("vector", lambda e, o=sc_a: e.tensor_single_scalar(
                o, sc_a, -PI_CL, ALU.max), inc="VE")
            return p.do("scalar", lambda e, o=dst, i=sc_a: e.activation(
                o, i, AF.Sin), inc="SC",
                waits=[("VE", ve2)])

        sc_tab = 0
        ang = stage[:, 0:SUB]
        for cidx in range(NCH):
            ve = p.do("vector", lambda e, o=ang, c=cidx: e.tensor_scalar(
                o, tau_row, th_c[:, c:c + 1], None, ALU.mult), inc="VE",
                waits=[("GP", gp), ("VE", ve_th), ("SC", p.cnt["SC"])])
            rr_sin_big(chunk(tabC, cidx), ang, 0.25)
            sc_tab = rr_sin_big(chunk(tabS, cidx), ang, 0.0)
            ve = p.do("vector", lambda e, o=ang, c=cidx: e.tensor_scalar(
                o, tau_row, del_c[:, c:c + 1], None, ALU.mult), inc="VE",
                waits=[("SC", p.cnt["SC"])])
            sc_tab = p.do("scalar", lambda e, o=chunk(tabR, cidx), i=ang:
                          e.activation(o, i, AF.Exp, scale=np.float32(-1.0)), inc="SC",
                          waits=[("VE", ve)])

        cs_cols(gc_c, gs_c, SUB)
        cs_cols(ec_c, es_c, SUB - 1)

        p.do("vector", lambda e: e.memset(seam_re, 0.0), inc="VE")
        p.do("vector", lambda e: e.memset(seam_im, 0.0), inc="VE")

        last_cast = ("VE", p.cnt["VE"])
        p.w("scalar", "SC", sc_tab)
        for k in range(NCH):
            src = dI["wpsiT"][k * 128:(k + 1) * 128, :]
            wdv = dma("scalar", "WD", stage[:, 0:2048], src, waits=[last_cast])
            cv = p.do("vector", lambda e, o=w_psi[:, k * 2048:(k + 1) * 2048],
                      i=stage[:, 0:2048]: e.tensor_copy(o, i), inc="VE",
                      waits=[("WD", wdv)])
            last_cast = ("VE", cv)
        for k in range(NCH):
            src = dI["wgateT"][k * 128:(k + 1) * 128, :]
            wdv = dma("scalar", "WD", stage[:, 0:1024], src, waits=[last_cast])
            cv = p.do("vector", lambda e, o=w_gate[:, k * 1024:(k + 1) * 1024],
                      i=stage[:, 0:1024]: e.tensor_copy(o, i), inc="VE",
                      waits=[("WD", wdv)])
            last_cast = ("VE", cv)
        for k in range(NCH):
            wdv = dma("scalar", "WD", stage[:, k * NE:(k + 1) * NE],
                      dI["projwT"][k * 128:(k + 1) * 128, :],
                      waits=[last_cast] if k == 0 else ())
        for k in range(NCH):
            sl = stage[:, k * NE:(k + 1) * NE]
            hi = pw_hi[:, k * 8:k * 8 + NE]
            lo = pw_lo[:, k * 8:k * 8 + NE]
            p.do("vector", lambda e, o=hi, i=sl: e.tensor_copy(o, i), inc="VE",
                 waits=[("WD", wdv)])
            p.do("vector", lambda e, o=lo, a=sl, b=hi: e.tensor_sub(o, a, b),
                 inc="VE")

        AFa.pop()
        p.w("sync", "S", p.cnt["S"]); p.w("sync", "L", p.cnt["L"])
        p.barrier()

        AFa.push()
        x_tm = [AFa.take(4096), AFa.take(4096)]
        sq_f32 = AFa.take(1024)
        rtile = AFa.take(512)
        lb_tm = AFa.take(24)
        mlm_f = AFa.take(24)
        m14 = AFa.take(16)
        ln_red = AFa.take(8)
        assert AFa.off <= AF_N

        AB.push()
        xn_bf = AB.take(NCH * SUB)
        xn_lo = AB.take(NCH * SUB)
        g_bf = AB.take(NCH * SUB)
        u_re = AB.take(NCH * SUB)
        u_im = AB.take(NCH * SUB)
        w_re_t = AB.take(NCH * SUB)
        w_im_t = AB.take(NCH * SUB)
        t14 = AB.take(4 * SUB)
        m24 = AB.take(48)
        stg32 = [AB.take(2048).bitcast(F32), AB.take(2048).bitcast(F32)]
        stg16 = [AB.take(1024), AB.take(1024)]
        assert AB.off <= AB_N, AB.off
        jt_re = u_re
        jt_im = u_im

        last_S_xn = 0
        last_S_j = 0
        sq_free = ("VE", 0)
        x_free = [("VE", 0), ("VE", 0)]
        psi_free = [("VE", 0), ("VE", 0)]
        gate_free = [("SC", 0), ("SC", 0)]
        tr_free = [("VE", 0), ("VE", 0)]
        rtr_free = ("VE", 0)
        pb6_free = ("SC", 0)
        rot_gp_prev = 0
        u_prev_ve = 0
        router_mm_prev = 0
        scans_prev = 0
        ln_last_sc = 0
        sfac_ve = None

        wjobs = []
        for eidx in range(NE):
            for j in range(NCH // 2):
                wjobs.append((
                    dI["w1"][eidx, 2 * j * 128:(2 * j + 2) * 128, :]
                    .rearrange("(k p) d -> p k d", p=128),
                    w1sc[eidx, 2 * j * 128:(2 * j + 2) * 128, :]
                    .rearrange("(k p) d -> p k d", p=128), True))
            for hk in range(NKH):
                wjobs.append((dI["w2"][eidx, hk * 128:(hk + 1) * 128, :],
                              w2sc[eidx, hk * 128:(hk + 1) * 128, :], False))
        wst_free = [("VE", 0), ("VE", 0)]
        wst16_S = [0, 0]

        def emit_wjob(i):
            src, dst, pair = wjobs[i]
            par = i % 2
            s32 = stg32[par]
            s16 = stg16[par]
            if pair:
                s32 = s32.rearrange("p (k d) -> p k d", d=512)
                s16 = s16.rearrange("p (k d) -> p k d", d=512)
            wd = dma("scalar", "WD", s32, src, waits=[wst_free[par]])
            cv = p.do("vector", lambda e, o=stg16[par], i2=stg32[par]:
                      e.tensor_copy(o, i2), inc="VE",
                      waits=[("WD", wd), ("S", wst16_S[par])])
            wst_free[par] = ("VE", cv)
            wst16_S[par] = dma("sync", "S", dst, s16,
                               waits=[("VE", cv)])

        for st in range(NSUB):
            for i in range(st * 6, min(st * 6 + 6, len(wjobs))):
                emit_wjob(i)
            xb = x_tm[st % 2]
            p.w("sync", x_free[st % 2][0], x_free[st % 2][1])
            lv = dma("sync", "L", xb.rearrange("p (t d) -> p t d", d=1024),
                     dI["x"][st * SUB:(st + 1) * SUB, :]
                     .rearrange("(t p) d -> p t d", p=128))

            ln_sc = []
            for t in range(4):
                tile = xb[:, t * 1024:(t + 1) * 1024]
                rcol = ln_red[:, 2 * t:2 * t + 1]
                scol = ln_red[:, 2 * t + 1:2 * t + 2]
                ve = p.do("vector", lambda e, o=rcol, i=tile: e.reduce_sum(
                    o, i, axis=mybir.AxisListType.X), inc="VE", waits=[("L", lv)])
                sc = p.do("scalar", lambda e, o=sq_f32, i=tile:
                          e.activation(o, i, AF.Square), inc="SC",
                          waits=[("L", lv), sq_free])
                ve_s2 = p.do("vector", lambda e, o=scol, i=sq_f32: e.reduce_sum(
                    o, i, axis=mybir.AxisListType.X), inc="VE",
                    waits=[("SC", sc)])
                sq_free = ("VE", ve_s2)
                mc = mcol[:, 2 * t:2 * t + 1]
                nmi = mcol[:, 2 * t + 1:2 * t + 2]
                ve = p.do("vector", lambda e, o=mc, i=rcol: e.tensor_scalar(
                    o, i, 1.0 / D, None, ALU.mult), inc="VE",
                    waits=[("SC", ln_last_sc)])
                ve = p.do("vector", lambda e, o=ms0, i=scol: e.tensor_scalar(
                    o, i, 1.0 / D, None, ALU.mult), inc="VE", waits=[("VE", ve_s2)])
                ve = p.do("vector", lambda e, o=ms1, a=mc, b=mc:
                          e.tensor_mul(o, a, b), inc="VE")
                ve = p.do("vector", lambda e, o=ms0, a=ms0, b=ms1:
                          e.tensor_sub(o, a, b), inc="VE")
                sc = p.do("scalar", lambda e, o=ms1, i=ms0:
                          e.activation(o, i, AF.Ln, bias=np.float32(1e-5)), inc="SC",
                          waits=[("VE", ve)])
                sc = p.do("scalar", lambda e, o=ms0, i=ms1:
                          e.activation(o, i, AF.Exp, scale=np.float32(-0.5)), inc="SC")
                ve = p.do("vector", lambda e, o=nmi, m=mc, i=ms0:
                          e.scalar_tensor_tensor(o, m, -1.0, i, ALU.mult, ALU.mult),
                          inc="VE", waits=[("SC", sc)])
                ve = p.do("vector", lambda e, o=mc, i=ms0: e.tensor_copy(o, i),
                          inc="VE")
                sc = p.do("scalar", lambda e, o=tile, i=tile, s=mc, b=nmi:
                          e.activation(o, i, AF.Identity, bias=b, scale=s),
                          inc="SC", waits=[("VE", ve)])
                ln_sc.append(sc)
            ln_last_sc = ln_sc[-1]

            for cidx in range(NCH):
                bank = PB[cidx % 2]
                mm = None
                for t in range(4):
                    wts = [("SC", ln_sc[t])]
                    if t == 0:
                        wts.append(tr_free[cidx % 2])
                    mm = p.do("tensor", lambda e, b=bank, t=t, c=cidx, xb=xb:
                              e.transpose(b[:, t * 128:(t + 1) * 128],
                                          xb[:, t * 1024 + c * 128:
                                             t * 1024 + (c + 1) * 128],
                                          identf_s), inc="MM", waits=wts)
                sc = p.do("scalar", lambda e, o=chunk(xn_bf, cidx), i=bank[:]:
                          e.activation(o, i, AF.Identity), inc="SC",
                          waits=[("MM", mm), ("S", last_S_xn),
                                 ("MM", router_mm_prev)])
                ve = p.do("vector", lambda e, o=chunk(xn_lo, cidx), a=bank[:],
                          b=chunk(xn_bf, cidx): e.tensor_sub(o, a, b), inc="VE",
                          waits=[("SC", sc), ("MM", router_mm_prev)])
                tr_free[cidx % 2] = ("VE", ve)
            xn_all_sc = p.cnt["SC"]
            xn_all_ve = p.cnt["VE"]

            sxn = dma("sync", "S", xn_sc[st], xn_bf, waits=[("SC", xn_all_sc)])
            last_S_xn = sxn

            g_sc = []
            for o in range(NCH):
                bank = PB[2 if o % 2 == 0 else 5]
                mm = None
                for k in range(NCH):
                    wts = []
                    if k == 0:
                        wts = [("SC", xn_all_sc), gate_free[o % 2]]
                    mm = p.do("tensor", lambda e, b=bank, o=o, k=k: e.matmul(
                        b[:], w_gate[:, k * 1024 + o * 128:k * 1024 + (o + 1) * 128],
                        chunk(xn_bf, k), start=(k == 0), stop=(k == NCH - 1)),
                        inc="MM", waits=wts)
                sc = p.do("scalar", lambda e, o2=chunk(g_bf, o), i=bank[:],
                          bc=bgate_c[:, o:o + 1]: e.activation(
                              o2, i, AF.Sigmoid, bias=bc), inc="SC",
                          waits=[("MM", mm), ("VE", u_prev_ve)])
                gate_free[o % 2] = ("SC", sc)
                g_sc.append(sc)

            for comp in range(2):
                udst = u_re if comp == 0 else u_im
                for cidx in range(NCH):
                    o = comp * NCH + cidx
                    bank = PB[3 + (o % 2)]
                    mm = None
                    for k in range(NCH):
                        wts = [psi_free[o % 2]] if k == 0 else []
                        mm = p.do("tensor", lambda e, b=bank, o=o, k=k: e.matmul(
                            b[:], w_psi[:, k * 2048 + o * 128:
                                        k * 2048 + (o + 1) * 128],
                            chunk(xn_bf, k), start=(k == 0), stop=(k == NCH - 1)),
                            inc="MM", waits=wts)
                    ve = p.do("vector", lambda e, o2=chunk(udst, cidx), i=bank[:],
                              bc=bpsi_c[:, o:o + 1], g=chunk(g_bf, cidx):
                              e.scalar_tensor_tensor(o2, i, bc, g, ALU.add,
                                                     ALU.mult),
                              inc="VE", waits=[("MM", mm), ("SC", g_sc[cidx]),
                                               ("GP", rot_gp_prev),
                                               ("S", last_S_j)])
                    psi_free[o % 2] = ("VE", ve)
            u_all_ve = p.cnt["VE"]
            u_prev_ve = u_all_ve

            t1 = t14[:, 0:SUB]; t3 = t14[:, 2 * SUB:3 * SUB]
            for cidx in range(NCH):
                cre = chunk(u_re, cidx); cim = chunk(u_im, cidx)
                ctab = chunk(tabC, cidx); stab = chunk(tabS, cidx)
                uw = [("VE", u_all_ve)] if cidx == 0 else []
                p.do("vector", lambda e, o=chunk(w_re_t, cidx), a=cre, b=ctab:
                     e.tensor_mul(o, a, b), inc="VE", waits=uw)
                p.do("vector", lambda e, o=t1, a=cim, b=stab: e.tensor_mul(o, a, b),
                     inc="VE")
                ve_wr = p.do("vector", lambda e, o=chunk(w_re_t, cidx), a=t1:
                             e.tensor_add(o, o, a), inc="VE")
                gw = [("VE", u_all_ve), ("VE", scans_prev)] if cidx == 0 else []
                p.do("gpsimd", lambda e, o=chunk(w_im_t, cidx), a=cim, b=ctab:
                     e.tensor_mul(o, a, b), inc="GP", waits=gw)
                p.do("gpsimd", lambda e, o=t3, a=cre, b=stab: e.tensor_mul(o, a, b),
                     inc="GP")
                gp_wi = p.do("gpsimd", lambda e, o=chunk(w_im_t, cidx), a=t3:
                             e.tensor_sub(o, o, a), inc="GP")
                p.do("vector", lambda e, o=chunk(jt_re, cidx),
                     d0=rho_c[:, cidx:cidx + 1].broadcast_to([128, SUB]),
                     d1=chunk(w_re_t, cidx), ini=seam_re[:, cidx:cidx + 1]:
                     e.tensor_tensor_scan(o, d0, d1, ini, ALU.mult, ALU.add),
                     inc="VE", waits=[("GP", gp_wi)])
                p.do("vector", lambda e, o=chunk(jt_im, cidx),
                     d0=rho_c[:, cidx:cidx + 1].broadcast_to([128, SUB]),
                     d1=chunk(w_im_t, cidx), ini=seam_im[:, cidx:cidx + 1]:
                     e.tensor_tensor_scan(o, d0, d1, ini, ALU.mult, ALU.add),
                     inc="VE")
            rot_gp_prev = p.cnt["GP"]
            scans_ve = p.cnt["VE"]
            scans_prev = scans_ve

            jre_end = jt_re[:, SUB - 1::SUB]
            jim_end = jt_im[:, SUB - 1::SUB]
            p.do("vector", lambda e: e.tensor_mul(accR, gc_c, jre_end), inc="VE")
            p.do("vector", lambda e: e.tensor_mul(accI, gs_c, jim_end), inc="VE")
            p.do("vector", lambda e: e.tensor_sub(seam_re, accR, accI), inc="VE")
            p.do("vector", lambda e: e.tensor_mul(accR, gc_c, jim_end), inc="VE")
            p.do("vector", lambda e: e.tensor_mul(accI, gs_c, jre_end), inc="VE")
            seam_ve = p.do("vector", lambda e: e.tensor_add(seam_im, accR, accI),
                           inc="VE")
            if st == NSUB - 1:
                p.do("vector", lambda e: e.tensor_mul(accR, ec_c, jre_end), inc="VE")
                p.do("vector", lambda e: e.tensor_mul(accI, es_c, jim_end), inc="VE")
                p.do("vector", lambda e: e.tensor_sub(scols[:, 0:8], accR, accI),
                     inc="VE")
                p.do("vector", lambda e: e.tensor_mul(accR, ec_c, jim_end), inc="VE")
                p.do("vector", lambda e: e.tensor_mul(accI, es_c, jre_end), inc="VE")
                seam_ve = p.do("vector", lambda e: e.tensor_add(
                    scols[:, 8:16], accR, accI), inc="VE")

            sj = dma("sync", "S", j_re[st], jt_re, waits=[("VE", scans_ve)])
            sj = dma("sync", "S", j_im[st], jt_im, waits=[("VE", seam_ve)])
            last_S_j = sj

            pairs = []
            for (wsta, mov) in ((pw_hi, xn_bf), (pw_hi, xn_lo), (pw_lo, xn_bf)):
                for k in range(NCH):
                    pairs.append((wsta[:, k * 8:k * 8 + NE], chunk(mov, k)))
            mm = None
            for i, (wst, mv) in enumerate(pairs):
                wts = [("VE", xn_all_ve), pb6_free] if i == 0 else []
                mm = p.do("tensor", lambda e, w=wst, m=mv, f=(i == 0),
                          l=(i == len(pairs) - 1): e.matmul(
                              PB6[0:NE, :], w, m, start=f, stop=l),
                          inc="MM", waits=wts)
            sc = p.do("scalar", lambda e: e.activation(
                rtile[0:NE, :], PB6[0:NE, :], AF.Tanh,
                scale=np.float32(1.0) / np.float32(0.3)), inc="SC",
                waits=[("MM", mm)])
            mm = p.do("tensor", lambda e: e.matmul(
                PB6[0:NE, :], anch_s[0:6, 0:6], rtile[0:NE, :],
                start=True, stop=True), inc="MM", waits=[("SC", sc)])
            if sfac_ve is None:
                ve = p.do("vector", lambda e: e.tensor_scalar(
                    ms2, RT_COL, 0.1, 2.0, ALU.max, ALU.mult), inc="VE")
                ve = p.do("vector", lambda e: e.reciprocal(SFAC, ms2), inc="VE")
                sfac_ve = p.do("vector", lambda e: e.tensor_scalar(
                    NB6, SFAC, -6.0, None, ALU.mult), inc="VE")
            sc_lg = p.do("scalar", lambda e: e.activation(
                rtile[0:NE, :], PB6[0:NE, :], AF.Identity,
                bias=NB6[0:NE, :], scale=SFAC[0:NE, :]), inc="SC",
                waits=[("MM", mm), ("VE", sfac_ve)])
            pb6_free = ("SC", sc_lg)
            router_mm_prev = p.cnt["MM"]
            mm = None
            for t in range(4):
                wts = [("SC", sc_lg)]
                if t == 0:
                    wts.append(rtr_free)
                mm = p.do("tensor", lambda e, t=t: e.transpose(
                    PB7[:, t * 6:(t + 1) * 6],
                    rtile[0:NE, t * 128:(t + 1) * 128], ident6_s[0:6, 0:6]),
                    inc="MM", waits=wts)
            ve = p.do("vector", lambda e: e.tensor_copy(lb_tm, PB7[:, 0:24]),
                      inc="VE", waits=[("MM", mm)])
            rtr_free = ("VE", ve)
            mask1 = m24[:, 0:24]; mask2 = m24[:, 24:48]; mlm = mlm_f
            m1p = m14[:, 0:4]; m2p = m14[:, 4:8]
            wa = m14[:, 8:12]; wb = m14[:, 12:16]

            def g6(a):
                return a.rearrange("p (g e) -> p g e", e=6)
            p.do("vector", lambda e: e.reduce_max(
                m1p.unsqueeze(2), g6(lb_tm), axis=mybir.AxisListType.X), inc="VE")
            p.do("vector", lambda e: e.tensor_tensor(
                g6(mask1), g6(lb_tm),
                m1p.unsqueeze(2).broadcast_to([128, 4, 6]), ALU.is_equal),
                inc="VE")
            p.do("vector", lambda e: e.scalar_tensor_tensor(
                mlm, mask1, -1e30, lb_tm, ALU.mult, ALU.add), inc="VE")
            p.do("vector", lambda e: e.reduce_max(
                m2p.unsqueeze(2), g6(mlm), axis=mybir.AxisListType.X), inc="VE")
            p.do("vector", lambda e: e.tensor_tensor(
                g6(mask2), g6(mlm),
                m2p.unsqueeze(2).broadcast_to([128, 4, 6]), ALU.is_equal),
                inc="VE")
            ve = p.do("vector", lambda e: e.tensor_sub(m1p, m1p, m2p), inc="VE")
            sc = p.do("scalar", lambda e: e.activation(wa, m1p, AF.Sigmoid),
                      inc="SC", waits=[("VE", ve)])
            p.do("vector", lambda e: e.tensor_scalar(
                wb, wa, -1.0, 1.0, ALU.mult, ALU.add), inc="VE",
                waits=[("SC", sc)])
            p.do("vector", lambda e: e.tensor_tensor(
                g6(mask1), g6(mask1),
                wa.unsqueeze(2).broadcast_to([128, 4, 6]), ALU.mult), inc="VE")
            p.do("vector", lambda e: e.tensor_tensor(
                g6(mask2), g6(mask2),
                wb.unsqueeze(2).broadcast_to([128, 4, 6]), ALU.mult), inc="VE")
            p.do("vector", lambda e, o=ew_tm[:, st * 24:(st + 1) * 24]:
                 e.tensor_add(o, mask1, mask2), inc="VE")

            x_free[st % 2] = ("MM", p.cnt["MM"])

        AFa.pop()
        AB.pop()
        AB.pop()
        AB.push()
        p.w("sync", "S", p.cnt["S"]); p.w("sync", "L", p.cnt["L"])
        p.barrier()

        AFa.push()
        stage2 = AFa.take(2048)
        x2 = AFa.take(4096)
        attn_acc = AFa.take(4096)
        stat4 = stage2[:, 0:512]
        rowf = stage2[:, 512:1024]
        rowg = stage2[:, 1024:1536]
        assert AFa.off <= AF_N

        w_phi = AB.take(NCH * 2048)
        hJ_re = AB.take(NCH * SUB)
        hJ_im = AB.take(NCH * SUB)
        xn2 = AB.take(NCH * SUB)
        phR = AB.take(NCH * SUB)
        phI = AB.take(NCH * SUB)
        hsq_b = phI
        he_all = AB.take(2 * NKH * SUB)
        he_b = [he_all[:, 0:NKH * SUB], he_all[:, NKH * SUB:2 * NKH * SUB]]
        w1b = [AB.take(NCH * DH), AB.take(NCH * DH)]
        w2b = [AB.take(NKH * 1024), AB.take(NKH * 1024)]
        rowsb = AB.take(4 * SUB)
        ewrow = AB.take(SUB)
        ta = AB.take(SUB)
        tb = AB.take(SUB)
        assert AB.off <= AB_N, AB.off
        mb_re = rowsb[:, 0:SUB]; ib_re = rowsb[:, SUB:2 * SUB]
        mb_im = rowsb[:, 2 * SUB:3 * SUB]; ib_im = rowsb[:, 3 * SUB:4 * SUB]
        rowm_b = ta
        rowi_b = tb

        half_free = [("VE", 0), ("VE", 0)]
        for k in range(NCH):
            for hh in range(2):
                par = hh
                sl = stage2[:, par * 1024:(par + 1) * 1024]
                wdv = dma("scalar", "WD", sl,
                          dI["wphiT"][k * 128:(k + 1) * 128,
                                      hh * 1024:(hh + 1) * 1024],
                          waits=[half_free[par]])
                cv = p.do("vector", lambda e,
                          o=w_phi[:, k * 2048 + hh * 1024:
                                  k * 2048 + (hh + 1) * 1024],
                          i=sl: e.tensor_copy(o, i), inc="VE",
                          waits=[("WD", wdv)])
                half_free[par] = ("VE", cv)
        wphi_ready = max(half_free[0][1], half_free[1][1])

        lv = dma("sync", "L", hJ_re, j_re[0])
        lv = dma("sync", "L", hJ_im, j_im[0])
        j_ready = lv
        xn2_ready = dma("sync", "L", xn2, xn_sc[0])
        x_rdy = dma("sync", "L", x2.rearrange("p (t d) -> p t d", d=1024),
                    dI["x"][0:SUB, :].rearrange("(t p) d -> p t d", p=128))

        w1_rdy = {}
        w2_rdy = {}

        def load_w1(g, wait):
            if g >= NSUB * NE:
                return
            w1_rdy[g] = dma("scalar", "WD",
                            w1b[g % 2].rearrange("p (k d) -> p k d", d=DH),
                            w1sc[g % NE].rearrange("(k p) d -> p k d", p=128),
                            waits=[wait])

        def load_w2(g, wait):
            if g >= NSUB * NE:
                return
            w2_rdy[g] = dma("scalar", "WD",
                            w2b[g % 2].rearrange("p (k d) -> p k d", d=1024),
                            w2sc[g % NE].rearrange("(k p) d -> p k d", p=128),
                            waits=[wait])

        load_w1(0, None)
        load_w1(1, None)
        load_w2(0, None)
        load_w2(1, None)

        cc = p.do("gpsimd", lambda e: e.dma_start(
            out=cc_in.rearrange("(p f) -> p f", p=128), in_=scols), inc="CC")
        co = p.do("gpsimd", lambda e: e.collective_compute(
            "AllGather", ALU.bypass, replica_groups=rg,
            ins=[cc_in], outs=[cc_out]), inc=("CO", 1), waits=[("CC", cc)],
            attach=True)
        for r in range(NCORE):
            cc = p.do("gpsimd", lambda e, r=r: e.dma_start(
                out=s_all[:, r * 16:(r + 1) * 16],
                in_=cc_out[r].rearrange("(p f) -> p f", p=128)),
                inc="CC", waits=[("CO", co)])
        CC_ALL = cc

        p.do("vector", lambda e: e.memset(sin_re, 0.0), inc="VE",
             waits=[("CC", CC_ALL)])
        p.do("vector", lambda e: e.memset(sin_im, 0.0), inc="VE")
        for m in range(3):
            p.do("vector", lambda e: e.memset(selR, 0.0), inc="VE")
            p.do("vector", lambda e: e.memset(selI, 0.0), inc="VE")
            for r in range(NCORE):
                p.do("vector", lambda e, m=m, r=r: e.scalar_tensor_tensor(
                    selR, s_all[:, r * 16:r * 16 + 8], SEL_COL(m, r), selR,
                    ALU.mult, ALU.add), inc="VE")
                p.do("vector", lambda e, m=m, r=r: e.scalar_tensor_tensor(
                    selI, s_all[:, r * 16 + 8:r * 16 + 16], SEL_COL(m, r), selI,
                    ALU.mult, ALU.add), inc="VE")
            if m == 0:
                p.do("vector", lambda e: e.tensor_add(sin_re, sin_re, selR),
                     inc="VE")
                p.do("vector", lambda e: e.tensor_add(sin_im, sin_im, selI),
                     inc="VE")
            else:
                sc = cs_cols(fr_c, fr_s, C * m)
                ve = p.do("vector", lambda e, pi=float(C * m): e.tensor_scalar(
                    accR, del_c, pi, None, ALU.mult), inc="VE",
                    waits=[("SC", sc)])
                sc = p.do("scalar", lambda e: e.activation(
                    accR, accR, AF.Exp, scale=np.float32(-1.0)), inc="SC",
                    waits=[("VE", ve)])
                p.do("vector", lambda e: e.tensor_mul(fr_c, fr_c, accR), inc="VE",
                     waits=[("SC", sc)])
                p.do("vector", lambda e: e.tensor_mul(fr_s, fr_s, accR), inc="VE")
                p.do("vector", lambda e: e.tensor_mul(accR, fr_c, selR), inc="VE")
                p.do("vector", lambda e: e.tensor_mul(accI, fr_s, selI), inc="VE")
                p.do("vector", lambda e: e.tensor_sub(accR, accR, accI), inc="VE")
                p.do("vector", lambda e: e.tensor_add(sin_re, sin_re, accR),
                     inc="VE")
                p.do("vector", lambda e: e.tensor_mul(accR, fr_c, selI), inc="VE")
                p.do("vector", lambda e: e.tensor_mul(accI, fr_s, selR), inc="VE")
                p.do("vector", lambda e: e.tensor_add(accR, accR, accI), inc="VE")
                p.do("vector", lambda e: e.tensor_add(sin_im, sin_im, accR),
                     inc="VE")

        def emit_Z(b):
            if b >= NSUB:
                return
            pimm = b * SUB + 1
            sc = cs_cols(fr_c, fr_s, pimm)
            ve = p.do("vector", lambda e, pi=float(pimm): e.tensor_scalar(
                accR, del_c, pi, None, ALU.mult), inc="VE", waits=[("SC", sc)])
            sc = p.do("scalar", lambda e: e.activation(
                accR, accR, AF.Exp, scale=np.float32(-1.0)), inc="SC",
                waits=[("VE", ve)])
            p.do("vector", lambda e: e.tensor_mul(fr_c, fr_c, accR), inc="VE",
                 waits=[("SC", sc)])
            p.do("vector", lambda e: e.tensor_mul(fr_s, fr_s, accR), inc="VE")
            zr = zre_s[:, b * 8:(b + 1) * 8]
            zi = zim_s[:, b * 8:(b + 1) * 8]
            p.do("vector", lambda e: e.tensor_mul(accR, fr_c, sin_re), inc="VE")
            p.do("vector", lambda e: e.tensor_mul(accI, fr_s, sin_im), inc="VE")
            p.do("vector", lambda e, o=zr: e.tensor_sub(o, accR, accI),
                 inc="VE")
            p.do("vector", lambda e: e.tensor_mul(accR, fr_c, sin_im), inc="VE")
            p.do("vector", lambda e: e.tensor_mul(accI, fr_s, sin_re), inc="VE")
            p.do("vector", lambda e, o=zi: e.tensor_add(o, accR, accI),
                 inc="VE")

        emit_Z(0)


        pb01_free = [("VE", 0), ("VE", 0)]
        pb23_free = [("VE", 0), ("VE", 0)]
        pb4_free = ("VE", 0)
        pb56_free = [("VE", 0), ("VE", 0)]
        pb7_free = ("VE", 0)
        he_free = [("MM", 0), ("MM", 0)]
        attn_gp_prev = 0
        attn_done_gp = 0
        res_mm_prev = 0
        ewb_mm_prev = 0
        rowcast_prev = 0
        stats2_last = 0
        h_mm_last = 0

        for st in range(NSUB):
            first = True
            for cidx in range(NCH):
                hr = chunk(hJ_re, cidx); hi = chunk(hJ_im, cidx)
                ctab = chunk(tabC, cidx); stab = chunk(tabS, cidx)
                rtab = chunk(tabR, cidx)
                zr = zre_s[:, st * 8 + cidx:st * 8 + cidx + 1]
                zi = zim_s[:, st * 8 + cidx:st * 8 + cidx + 1]
                wts = [("L", j_ready), ("MM", ewb_mm_prev),
                       ("GP", attn_gp_prev)] if first else []
                p.do("vector", lambda e, i=zr: e.tensor_copy(ms0, i), inc="VE",
                     waits=wts)
                p.do("vector", lambda e, i=zi: e.tensor_copy(ms1, i), inc="VE")
                p.do("vector", lambda e, o=hr, t=rtab:
                     e.scalar_tensor_tensor(o, t, ms0, o, ALU.mult, ALU.add),
                     inc="VE")
                p.do("vector", lambda e, o=hi, t=rtab:
                     e.scalar_tensor_tensor(o, t, ms1, o, ALU.mult, ALU.add),
                     inc="VE")
                p.do("vector", lambda e, a=hr, b=ctab: e.tensor_mul(ta, a, b),
                     inc="VE")
                p.do("vector", lambda e, a=hi, b=stab: e.tensor_mul(tb, a, b),
                     inc="VE")
                p.do("vector", lambda e: e.tensor_sub(ta, ta, tb), inc="VE")
                p.do("vector", lambda e, o=hi, b=ctab: e.tensor_mul(o, o, b),
                     inc="VE")
                p.do("vector", lambda e, a=hr, b=stab: e.tensor_mul(tb, a, b),
                     inc="VE")
                p.do("vector", lambda e, o=hi: e.tensor_add(o, o, tb),
                     inc="VE")
                p.do("vector", lambda e, o=hr: e.tensor_copy(o, ta), inc="VE")
                p.do("vector", lambda e, o=chunk(phI, cidx), a=hr:
                     e.tensor_mul(o, a, a), inc="VE")
                p.do("vector", lambda e, o=chunk(he_all, cidx), a=hi:
                     e.tensor_mul(o, a, a), inc="VE")
                first = False
            rot_ve = p.cnt["VE"]

            for o in range(NCH):
                b = o % 2
                bank = PB[b]
                mm = None
                for k in range(NCH):
                    wts = []
                    if k == 0:
                        wts = [("L", xn2_ready), ("VE", wphi_ready),
                               pb01_free[b]]
                    mm = p.do("tensor", lambda e, bk=bank, o=o, k=k: e.matmul(
                        bk[:], w_phi[:, k * 2048 + o * 128:
                                     k * 2048 + (o + 1) * 128],
                        chunk(xn2, k), start=(k == 0), stop=(k == NCH - 1)),
                        inc="MM", waits=wts)
                ve = p.do("vector", lambda e, o2=chunk(phR, o), i=bank[:],
                          bc=bphi_c[:, o:o + 1]: e.tensor_scalar(
                              o2, i, bc, None, ALU.add), inc="VE",
                          waits=[("MM", mm)])
                pb01_free[b] = ("VE", ve)

            mm_stats = None
            for cidx in range(NCH):
                wts = []
                if cidx == 0:
                    wts = [("VE", rot_ve), pb23_free[0], pb7_free]
                l = (cidx == NCH - 1)
                mm_stats = p.do("tensor", lambda e, f=(cidx == 0),
                                m=chunk(hJ_re, cidx): e.matmul(
                                    PB2[0:2, :], ehot_s[:, 0:2], m,
                                    start=f, stop=False), inc="MM", waits=wts)
                mm_stats = p.do("tensor", lambda e, f=(cidx == 0),
                                m=chunk(phI, cidx): e.matmul(
                                    PB7[0:2, :], ehot_s[:, 0:2], m,
                                    start=f, stop=False), inc="MM")
                mm_stats = p.do("tensor", lambda e, m=chunk(hJ_im, cidx),
                                l=l: e.matmul(
                                    PB2[0:2, :], ehot_s[:, 4:6], m,
                                    start=False, stop=l), inc="MM")
                mm_stats = p.do("tensor", lambda e, m=chunk(he_all, cidx),
                                l=l: e.matmul(
                                    PB7[0:2, :], ehot_s[:, 4:6], m,
                                    start=False, stop=l), inc="MM")
            stats2_mm = mm_stats
            stats2_last = stats2_mm

            phI_ve = {}
            for o in range(NCH):
                b = o % 2
                bank = PB[b]
                mm = None
                for k in range(NCH):
                    wts = [pb01_free[b]] if k == 0 else []
                    mm = p.do("tensor", lambda e, bk=bank, o=o, k=k: e.matmul(
                        bk[:], w_phi[:, k * 2048 + (NCH + o) * 128:
                                     k * 2048 + (NCH + o + 1) * 128],
                        chunk(xn2, k), start=(k == 0), stop=(k == NCH - 1)),
                        inc="MM", waits=wts)
                ve = p.do("vector", lambda e, o2=chunk(phI, o), i=bank[:],
                          bc=bphi_c[:, NCH + o:NCH + o + 1]: e.tensor_scalar(
                              o2, i, bc, None, ALU.add), inc="VE",
                          waits=[("MM", mm), ("MM", stats2_mm)])
                pb01_free[b] = ("VE", ve)
                phI_ve[o] = ve

            sc = p.do("scalar", lambda e: e.activation(
                stat4[0:2, :], PB2[0:2, :], AF.Identity,
                scale=np.float32(1.0 / D)), inc="SC",
                waits=[("MM", stats2_mm), ("VE", rowcast_prev)])
            sc = p.do("scalar", lambda e: e.activation(
                rowf[0:2, :], PB7[0:2, :], AF.Identity,
                scale=np.float32(1.0 / D)), inc="SC")
            ve = p.do("vector", lambda e: e.tensor_mul(
                rowg[0:2, :], stat4[0:2, :], stat4[0:2, :]), inc="VE",
                waits=[("SC", sc)])
            ve = p.do("vector", lambda e: e.tensor_sub(
                rowf[0:2, :], rowf[0:2, :], rowg[0:2, :]), inc="VE")
            ve = p.do("vector", lambda e: e.tensor_single_scalar(
                rowf[0:2, :], rowf[0:2, :], 1e-30, ALU.max), inc="VE")
            sc = p.do("scalar", lambda e: e.activation(
                rowg[0:2, :], rowf[0:2, :], AF.Ln), inc="SC",
                waits=[("VE", ve)])
            sc = p.do("scalar", lambda e: e.activation(
                rowf[0:2, :], rowg[0:2, :], AF.Exp, scale=np.float32(0.5)),
                inc="SC")
            rows_sc = sc
            ve = p.do("vector", lambda e: e.tensor_scalar(
                rowf[0:2, :], rowf[0:2, :], 1e-6, None, ALU.add), inc="VE",
                waits=[("SC", sc)])
            ve = p.do("vector", lambda e: e.reciprocal(
                rowg[0:2, :], rowf[0:2, :]), inc="VE")
            ve = p.do("vector", lambda e: e.tensor_copy(
                rowm_b[0:2, :], stat4[0:2, :]), inc="VE")
            ve = p.do("vector", lambda e: e.tensor_copy(
                rowi_b[0:2, :], rowg[0:2, :]), inc="VE")
            rowcast = ve
            rowcast_prev = rowcast

            mmprev = {4: None, 7: None}
            mb_mm = None
            for src, j, dstrow, bank, bid in (
                    (rowm_b, 0, mb_re, PB4, 4), (rowm_b, 1, mb_im, PB7, 7),
                    (rowi_b, 0, ib_re, PB4, 4), (rowi_b, 1, ib_im, PB7, 7)):
                wts = [("VE", rowcast)]
                if mmprev[bid] is None:
                    wts.append(pb4_free if bid == 4 else ("SC", rows_sc))
                else:
                    wts.append(("VE", mmprev[bid]))
                mb_mm = p.do("tensor", lambda e, bk=bank, s=src, j=j: e.matmul(
                    bk[:], bsel_s[0:2, j * 128:(j + 1) * 128], s[0:2, :],
                    start=True, stop=True), inc="MM", waits=wts)
                cv = p.do("vector", lambda e, o=dstrow, bk=bank:
                          e.tensor_copy(o, bk[:]), inc="VE",
                          waits=[("MM", mb_mm)])
                mmprev[bid] = cv
            mb_mm_last = mb_mm
            pb4_free = ("VE", mmprev[4])

            mm = None
            for t in range(4):
                wts = [("VE", mmprev[7])] if t == 0 else []
                mm = p.do("tensor", lambda e, t=t,
                          i=ew_tm[:, st * 24 + t * 6:st * 24 + (t + 1) * 6]:
                          e.transpose(PBb[7][0:6, t * 128:(t + 1) * 128], i,
                                      identb_s), inc="MM", waits=wts)
            ewrow_ve = p.do("vector", lambda e: e.tensor_copy(
                ewrow[0:6, :], PBb[7][0:6, 0:512]), inc="VE",
                waits=[("MM", mm), ("MM", ewb_mm_prev)])
            pb7_free = ("VE", ewrow_ve)

            for cidx in range(NCH):
                hr = chunk(hJ_re, cidx); hi = chunk(hJ_im, cidx)
                wts = []
                if cidx == 0:
                    wts = [("MM", mb_mm_last), ("MM", res_mm_prev),
                           ("VE", ewrow_ve)]
                p.do("gpsimd", lambda e, a=hr: e.tensor_sub(ta, a, mb_re),
                     inc="GP", waits=wts)
                p.do("gpsimd", lambda e: e.tensor_mul(ta, ta, ib_re),
                     inc="GP")
                p.do("gpsimd", lambda e, o=chunk(attn_acc, cidx),
                     b=chunk(phR, cidx): e.tensor_mul(o, ta, b), inc="GP")
                p.do("gpsimd", lambda e, a=hi: e.tensor_sub(tb, a, mb_im),
                     inc="GP")
                p.do("gpsimd", lambda e: e.tensor_mul(tb, tb, ib_im),
                     inc="GP")
                p.do("gpsimd", lambda e, b=chunk(phI, cidx):
                     e.tensor_mul(tb, tb, b), inc="GP")
                p.do("gpsimd", lambda e, o=chunk(attn_acc, cidx):
                     e.tensor_add(o, o, tb), inc="GP")
            attn_done_gp = p.cnt["GP"]
            attn_gp_prev = attn_done_gp

            for eidx in range(NE):
                g = st * NE + eidx
                par = g % 2
                wts = [("VE", ewrow_ve), pb4_free]
                mmb = p.do("tensor", lambda e, ei=eidx: e.matmul(
                    PB4[:], bsel_s[0:6, ei * 128:(ei + 1) * 128],
                    ewrow[0:6, :], start=True, stop=True), inc="MM", waits=wts)
                he = he_b[par]
                he_ve = None
                for hk in range(NKH):
                    b = hk % 2
                    bank = PB[b]
                    mm = None
                    for k in range(NCH):
                        wts = []
                        if k == 0:
                            wts = [("WD", w1_rdy[g]), ("L", xn2_ready),
                                   pb01_free[b]]
                        mm = p.do("tensor",
                                  lambda e, bk=bank, hk=hk, k=k, par=par:
                                  e.matmul(
                                      bk[:],
                                      w1b[par][:, k * DH + hk * 128:
                                               k * DH + (hk + 1) * 128],
                                      chunk(xn2, k), start=(k == 0),
                                      stop=(k == NCH - 1)),
                                  inc="MM", waits=wts)
                    sc = p.do("scalar", lambda e, hk=hk, he=he, bk=bank:
                              e.activation(he[:, hk * SUB:(hk + 1) * SUB],
                                           bk[:], AF.Gelu_apprx_tanh),
                              inc="SC", waits=[("MM", mm), he_free[par]])
                    he_ve = p.do("vector", lambda e, hk=hk, he=he:
                                 e.tensor_tensor(he[:, hk * SUB:(hk + 1) * SUB],
                                                 he[:, hk * SUB:(hk + 1) * SUB],
                                                 PB4[:], ALU.mult),
                                 inc="VE", waits=[("SC", sc), ("MM", mmb)])
                    pb01_free[b] = ("SC", sc)
                h_mm_last = p.cnt["MM"]
                load_w1(g + 2, ("MM", h_mm_last))
                pb4_free = ("VE", he_ve)

                for o in range(NCH):
                    b = o % 2
                    bank = PB[2 + b]
                    mm = None
                    for hk in range(NKH):
                        wts = []
                        if hk == 0:
                            wts = [("VE", he_ve), ("WD", w2_rdy[g]),
                                   pb23_free[b]]
                            if eidx == 0 and o < 2:
                                wts.append(("SC", rows_sc))
                        mm = p.do("tensor",
                                  lambda e, bk=bank, o=o, hk=hk, par=par,
                                  he=he: e.matmul(
                                      bk[:],
                                      w2b[par][:, hk * 1024 + o * 128:
                                               hk * 1024 + (o + 1) * 128],
                                      he[:, hk * SUB:(hk + 1) * SUB],
                                      start=(hk == 0), stop=(hk == NKH - 1)),
                                  inc="MM", waits=wts)
                    ywts = [("MM", mm)]
                    if eidx == 0:
                        ywts.append(("GP", attn_done_gp))
                    yv = p.do("vector", lambda e, o2=chunk(attn_acc, o),
                              bk=bank: e.tensor_add(o2, o2, bk[:]), inc="VE",
                              waits=ywts)
                    pb23_free[b] = ("VE", yv)
                y_mm = p.cnt["MM"]
                load_w2(g + 2, ("MM", y_mm))
                he_free[par] = ("MM", y_mm)
            y_all_ve = p.cnt["VE"]
            ewb_mm_prev = p.cnt["MM"]

            for t in range(4):
                for half in range(2):
                    bank = PB[5 + half]
                    mm = None
                    for ci in range(4):
                        cidx = half * 4 + ci
                        wts = [("VE", y_all_ve), ("GP", attn_done_gp),
                               pb56_free[half]] if ci == 0 else []
                        mm = p.do("tensor",
                                  lambda e, bi=5 + half, ci=ci, c=cidx, t=t:
                                  e.transpose(
                                      PB[bi][:, ci * 128:(ci + 1) * 128],
                                      chunk(attn_acc, c)[:, t * 128:
                                                         (t + 1) * 128],
                                      identf_s), inc="MM", waits=wts)
                    ve = p.do("vector", lambda e, bi=5 + half, t=t, h=half:
                              e.tensor_add(
                                  x2[:, t * 1024 + h * 512:
                                     t * 1024 + (h + 1) * 512],
                                  x2[:, t * 1024 + h * 512:
                                     t * 1024 + (h + 1) * 512],
                                  PB[bi][:, 0:512]), inc="VE",
                              waits=[("MM", mm), ("L", x_rdy)])
                    pb56_free[half] = ("VE", ve)
            res_ve = p.cnt["VE"]
            res_mm_prev = p.cnt["MM"]

            if st + 1 < NSUB:
                lv = dma("sync", "L", hJ_re, j_re[st + 1],
                         waits=[("GP", attn_done_gp),
                                ("MM", stats2_last)])
                lv = dma("sync", "L", hJ_im, j_im[st + 1])
                j_ready = lv
                xn2_ready = dma("sync", "L", xn2, xn_sc[st + 1],
                                waits=[("MM", h_mm_last)])

            sv = p.do("sync", lambda e, st=st: e.dma_start(
                out=out_d[st * SUB:(st + 1) * SUB, :]
                .rearrange("(t p) d -> p t d", p=128),
                in_=x2.rearrange("p (t d) -> p t d", d=1024)),
                inc="S", waits=[("VE", res_ve)])

            emit_Z(st + 1)
            if st + 1 < NSUB:
                x_rdy = dma("sync", "L",
                            x2.rearrange("p (t d) -> p t d", d=1024),
                            dI["x"][(st + 1) * SUB:(st + 2) * SUB, :]
                            .rearrange("(t p) d -> p t d", p=128),
                            waits=[("S", sv)])

        p.w("sync", "S", p.cnt["S"])
        p.w("sync", "L", p.cnt["L"])
        AFa.pop(); AB.pop()

        @block.sync
        def _(e):
            for f in p.q["sync"]:
                f(e)

        @block.tensor
        def _(e):
            for f in p.q["tensor"]:
                f(e)

        @block.vector
        def _(e):
            for f in p.q["vector"]:
                f(e)

        @block.scalar
        def _(e):
            for f in p.q["scalar"]:
                f(e)

        @block.gpsimd
        def _(e):
            for f in p.q["gpsimd"]:
                f(e)

    return nc


def _host_constants():
    ehot = np.zeros((128, 16), dtype=ml_dtypes.bfloat16)
    for j in range(4):
        ehot[:, 4 * j + j] = 1
    bsel = np.zeros((6, 768), dtype=ml_dtypes.bfloat16)
    for j in range(6):
        bsel[j, j * 128:(j + 1) * 128] = 1
    return dict(
        identf=np.eye(128, dtype=np.float32),
        identb=np.eye(128, dtype=ml_dtypes.bfloat16),
        ident6=np.eye(6, dtype=np.float32),
        onesrow_b=np.ones((1, 128), dtype=ml_dtypes.bfloat16),
        ehot=ehot,
        bsel=bsel,
    )


def kernel(L0_attn_x, L0_attn_omega, L0_attn_log_gamma, L0_attn_dt,
           L0_attn_W_psi, L0_attn_b_psi, L0_attn_W_phi, L0_attn_b_phi,
           L0_attn_W_gate, L0_attn_b_gate, L0_ffn_proj_w, L0_ffn_anchors,
           L0_ffn_routing_temp, L0_ffn_w1, L0_ffn_w2):
    if "nc" not in _CACHE:
        _CACHE["nc"] = build_program()
    nc = _CACHE["nc"]

    f32 = np.float32
    x = np.asarray(L0_attn_x, f32)
    shared = dict(
        wpsiT=np.ascontiguousarray(np.asarray(L0_attn_W_psi, f32).T),
        wphiT=np.ascontiguousarray(np.asarray(L0_attn_W_phi, f32).T),
        wgateT=np.ascontiguousarray(np.asarray(L0_attn_W_gate, f32).T),
        bpsi=np.asarray(L0_attn_b_psi, f32),
        bphi=np.asarray(L0_attn_b_phi, f32),
        bgate=np.asarray(L0_attn_b_gate, f32),
        omega=np.asarray(L0_attn_omega, f32),
        loggam=np.asarray(L0_attn_log_gamma, f32),
        dtv=np.asarray(L0_attn_dt, f32).reshape(1),
        rtemp=np.asarray(L0_ffn_routing_temp, f32).reshape(1),
        projwT=np.ascontiguousarray(np.asarray(L0_ffn_proj_w, f32).T),
        anchT=np.ascontiguousarray(np.asarray(L0_ffn_anchors, f32).T),
        w1=np.ascontiguousarray(np.asarray(L0_ffn_w1, f32)),
        w2=np.ascontiguousarray(np.asarray(L0_ffn_w2, f32)),
        **_host_constants(),
    )
    in_maps = []
    for core in range(NCORE):
        seq, ch = divmod(core, CPS)
        sel = np.zeros((3, NCORE), f32)
        for m in range(3):
            j = ch - 1 - m
            if j >= 0:
                sel[m, seq * CPS + j] = 1.0
        im = dict(shared)
        im["x"] = np.ascontiguousarray(x[seq, ch * C:(ch + 1) * C])
        im["sel"] = np.ascontiguousarray(sel.reshape(1, 24))
        in_maps.append(im)

    res = run_bass_kernel_spmd(nc, in_maps, core_ids=list(range(NCORE)))
    out = np.empty((B, L, D), np.float32)
    for core in range(NCORE):
        seq, ch = divmod(core, CPS)
        out[seq, ch * C:(ch + 1) * C] = res.results[core]["out"]
    return out


# version 11
# speedup vs baseline: 1.0574x; 1.0574x over previous
import numpy as np
import ml_dtypes
import concourse.bass as bass
import concourse.mybir as mybir
from concourse.bass_utils import run_bass_kernel_spmd
from concourse.replica_groups import maybe_share_collective_output_space

F32 = mybir.dt.float32
BF16 = mybir.dt.bfloat16
I32 = mybir.dt.int32
AF = mybir.ActivationFunctionType
ALU = mybir.AluOpType

NCORE = 8
B, L, D = 2, 16384, 1024
CPS = NCORE // B
C = L // CPS
SUB = 512
NSUB = C // SUB
NCH = D // 128
NE, DH = 6, 512
NKH = DH // 128
TWO_PI = float(np.float32(2 * np.pi))
INV_2PI = float(np.float32(1.0 / (2 * np.pi)))
TP_HI = float(np.float32(2 * np.pi))
TP_LO = float(np.float32(2 * np.pi - np.float64(np.float32(2 * np.pi))))
PI_CL = float(np.float32(3.1415925))

_CACHE = {}


class Prog:
    ENGS = ("sync", "tensor", "vector", "scalar", "gpsimd")
    SEMS = ("L", "S", "WD", "MM", "VE", "SC", "GP", "CC", "PH", "CO")
    DMA_SEMS = ("L", "S", "WD", "CC")

    def __init__(self):
        self.q = {e: [] for e in self.ENGS}
        self.cnt = {s: 0 for s in self.SEMS}
        self.sems = {}
        self.lastwait = {}

    def w(self, eng, sem, val):
        if val is None or val <= 0:
            return
        k = (eng, sem)
        if self.lastwait.get(k, -1) >= val:
            return
        self.lastwait[k] = val

        def f(e, sem=sem, val=val):
            e.wait_ge(self.sems[sem], val)
        self.q[eng].append(f)

    def do(self, eng, build, inc=None, waits=(), attach=False):
        for wt in waits:
            if wt is None:
                continue
            sem, val = wt
            self.w(eng, sem, val)
        sem = None
        amt = 0
        if inc is not None:
            if isinstance(inc, tuple):
                sem, amt = inc
            else:
                sem = inc
                amt = 16 if sem in self.DMA_SEMS else 1
            self.cnt[sem] += amt

        def f(e, build=build, sem=sem, amt=amt, eng=eng, attach=attach):
            ins = build(e)
            if sem is not None:
                if attach or amt == 16 or eng == "sync":
                    ins.then_inc(self.sems[sem], amt)
                else:
                    e.drain().then_inc(self.sems[sem], amt)
        self.q[eng].append(f)
        return self.cnt[sem] if sem is not None else None

    def barrier(self):
        comp = ("tensor", "vector", "scalar", "gpsimd")
        target = self.cnt["PH"] + len(comp)
        for eng in comp:
            def f(e):
                e.drain().then_inc(self.sems["PH"], 1)
            self.q[eng].append(f)
        self.cnt["PH"] = target
        for eng in self.ENGS:
            for sem in ("S", "L", "WD", "CC"):
                self.w(eng, sem, self.cnt[sem])
            self.w(eng, "PH", target)
        return target


class Alo:
    def __init__(self, t, total, name):
        self.t, self.total, self.name = t, total, name
        self.off = 0
        self.marks = []

    def take(self, n):
        o = self.off
        assert o + n <= self.total, f"{self.name} overflow {o}+{n}>{self.total}"
        self.off += n
        return self.t[:, o:o + n]

    def push(self):
        self.marks.append(self.off)

    def pop(self):
        self.off = self.marks.pop()


def build_program():
    nc = bass.Bass("TRN2", target_bir_lowering=False, debug=False, detect_race_conditions=False)

    dI = {}
    def din(name, shape, dt=F32):
        dI[name] = nc.dram_tensor(name, shape, dt, kind="ExternalInput").ap()
    din("x", [C, D])
    din("wpsiT", [D, 2 * D]); din("wphiT", [D, 2 * D]); din("wgateT", [D, D])
    din("bpsi", [2 * D]); din("bphi", [2 * D]); din("bgate", [D])
    din("omega", [D]); din("loggam", [D]); din("dtv", [1]); din("rtemp", [1])
    din("projwT", [D, NE]); din("anchT", [NE, NE])
    din("w1", [NE, D, DH]); din("w2", [NE, DH, D])
    din("identf", [128, 128]); din("identb", [128, 128], BF16)
    din("ident6", [6, 6])
    din("onesrow_b", [1, 128], BF16)
    din("ehot", [128, 16], BF16)
    din("bsel", [6, 768], BF16)
    din("sel", [1, 24])

    out_d = nc.dram_tensor("out", [C, D], F32, kind="ExternalOutput").ap()

    j_re = nc.dram_tensor("j_re", [NSUB, 128, NCH * SUB], BF16, kind="Internal").ap()
    j_im = nc.dram_tensor("j_im", [NSUB, 128, NCH * SUB], BF16, kind="Internal").ap()
    xn_sc = nc.dram_tensor("xn_sc", [NSUB, 128, NCH * SUB], BF16, kind="Internal").ap()
    w1sc = nc.dram_tensor("w1sc", [NE, D, DH], BF16, kind="Internal").ap()
    w2sc = nc.dram_tensor("w2sc", [NE, DH, D], BF16, kind="Internal").ap()
    cc_in = nc.dram_tensor("cc_in", [128 * 16], F32, kind="Internal").ap()
    rg = [list(range(NCORE))]
    cc_out = nc.dram_tensor(
        "cc_out", [NCORE, 128 * 16], F32, kind="Internal",
        addr_space=maybe_share_collective_output_space("AllGather", rg)).ap()

    AB_N = 75264
    AF_N = 11776

    import contextlib
    with contextlib.ExitStack() as _ctx:
        ec = _ctx.enter_context
        AB_t = ec(nc.sbuf_tensor("AB", [128, AB_N], BF16))
        AF_t = ec(nc.sbuf_tensor("AF", [128, AF_N], F32))
        PRM = ec(nc.sbuf_tensor("PRM", [128, 336], F32))
        PBt = [ec(nc.psum_tensor("PB%d" % i, [128, 512], F32)) for i in range(8)]
        (PB0, PB1, PB2, PB3, PB4, PB5, PB6, PB7) = PBt
        sems = {n: ec(nc.semaphore(n))
                for n in ("L", "S", "WD", "MM", "VE", "SC", "GP", "CC", "PH", "CO")}
        (sem_L, sem_S, sem_WD, sem_MM, sem_VE, sem_SC, sem_GP, sem_CC,
         sem_PH, sem_CO) = (sems[n] for n in ("L", "S", "WD", "MM", "VE", "SC",
                                              "GP", "CC", "PH", "CO"))
        block = ec(nc.Block())
        p = Prog()
        p.sems = dict(L=sem_L, S=sem_S, WD=sem_WD, MM=sem_MM, VE=sem_VE,
                      SC=sem_SC, GP=sem_GP, CC=sem_CC, PH=sem_PH, CO=sem_CO)
        PB = [PB0, PB1, PB2, PB3, PB4, PB5, PB6, PB7]
        PBb = [b[:].bitcast(BF16) for b in PB]

        AB = Alo(AB_t[:], AB_N, "AB")
        AFa = Alo(AF_t[:], AF_N, "AF")

        lg_c = PRM[:, 0:8]; om_c = PRM[:, 8:16]
        del_c = PRM[:, 16:24]; rho_c = PRM[:, 24:32]; th_c = PRM[:, 32:40]
        bpsi_c = PRM[:, 40:56]; bphi_c = PRM[:, 56:72]; bgate_c = PRM[:, 72:80]
        gc_c = PRM[:, 80:88]; gs_c = PRM[:, 88:96]
        ec_c = PRM[:, 96:104]; es_c = PRM[:, 104:112]
        seam_re = PRM[:, 112:120]; seam_im = PRM[:, 120:128]
        scols = PRM[:, 128:144]
        sin_re = PRM[:, 144:152]; sin_im = PRM[:, 152:160]
        P_b = PRM[:, 160:192]
        mcol = PRM[:, 192:200]
        selR = PRM[:, 200:208]
        colA = PRM[:, 208:216]; colB = PRM[:, 216:224]
        icol = PRM[:, 224:232].bitcast(I32)
        dcol = PRM[:, 232:240]
        accR = PRM[:, 240:248]; accI = PRM[:, 248:256]
        fr_c = PRM[:, 256:264]; fr_s = PRM[:, 264:272]
        selI = PRM[:, 272:280]
        ms0 = PRM[:, 280:281]; ms1 = PRM[:, 281:282]
        ms2 = PRM[:, 282:283]
        SFAC = PRM[:, 284:285]; NB6 = PRM[:, 285:286]
        rowstg = PRM[0:2, 286:318]
        DT_COL = P_b[:, 2:3]; RT_COL = P_b[:, 3:4]
        rsA = PRM[:, 320:328]

        def SEL_COL(m, r):
            return P_b[:, 4 + m * 8 + r:5 + m * 8 + r]

        identb_s = AB.take(128)
        onesr_b = AB.take(128)
        ehot_s = AB.take(16)
        bsel_s = AB.take(768)
        ew_tm = AB.take(NSUB * 24)
        zre_s = AB.take(64); zim_s = AB.take(64)

        identf_s = AFa.take(128)
        ident6_s = AFa.take(6)
        anch_s = AFa.take(6)
        s_all = AFa.take(128)
        ones_rf = AFa.take(128)

        def dma(eng, sem, out, in_, waits=(), slow=False):
            if slow:
                return p.do(eng, lambda e, o=out, i=in_: e.dma_start(
                    out=o, in_=i, allow_slow_non_contiguous=True),
                    inc=sem, waits=waits)
            return p.do(eng, lambda e, o=out, i=in_: e.dma_start(out=o, in_=i),
                        inc=sem, waits=waits)

        def chunk(ap, c, w=SUB):
            return ap[:, c * w:(c + 1) * w]

        def rr_sin(dst, ang, shift, sc_extra=()):
            p.do("vector", lambda e, o=colA, i=ang, s=float(shift):
                 e.tensor_scalar(o, i, INV_2PI, s, ALU.mult, ALU.add), inc="VE",
                 waits=[("SC", p.cnt["SC"])])
            p.do("vector", lambda e, o=icol, i=colA: e.tensor_copy(o, i), inc="VE")
            p.do("vector", lambda e, o=colB, i=icol: e.tensor_copy(o, i), inc="VE")
            p.do("vector", lambda e, o=colA, a=colA, b=colB: e.tensor_sub(o, a, b),
                 inc="VE")
            p.do("vector", lambda e, o=rsA, i=colA: e.tensor_single_scalar(
                o, i, 0.5, ALU.is_gt), inc="VE")
            p.do("vector", lambda e, o=colB, a=colB, b=rsA: e.tensor_add(o, a, b),
                 inc="VE")
            p.do("vector", lambda e, o=rsA, i=colA: e.tensor_single_scalar(
                o, i, -0.5, ALU.is_lt), inc="VE")
            p.do("vector", lambda e, o=colB, a=colB, b=rsA: e.tensor_sub(o, a, b),
                 inc="VE")
            if float(shift) != 0.0:
                p.do("vector", lambda e, o=colB, i=colB, s=-float(shift):
                     e.tensor_scalar(o, i, s, None, ALU.add), inc="VE")
            p.do("vector", lambda e, o=colA, k=colB, a=ang:
                 e.scalar_tensor_tensor(o, k, -TP_HI, a, ALU.mult, ALU.add), inc="VE")
            p.do("vector", lambda e, o=colA, k=colB, a=colA:
                 e.scalar_tensor_tensor(o, k, -TP_LO, a, ALU.mult, ALU.add), inc="VE")
            p.do("vector", lambda e, o=colA: e.tensor_single_scalar(
                o, colA, PI_CL, ALU.min), inc="VE")
            ve = p.do("vector", lambda e, o=colA: e.tensor_single_scalar(
                o, colA, -PI_CL, ALU.max), inc="VE")
            return p.do("scalar", lambda e, o=dst, i=colA: e.activation(
                o, i, AF.Sin), inc="SC",
                waits=[("VE", ve)] + list(sc_extra))

        def cs_cols(dst_c, dst_s, pimm):
            p.do("vector", lambda e, o=dcol, s=float(pimm): e.tensor_scalar(
                o, th_c, s, None, ALU.mult), inc="VE")
            rr_sin(dst_c, dcol, 0.25)
            return rr_sin(dst_s, dcol, 0.0)

        AB.push()
        tabC = AB.take(NCH * SUB)
        tabS = AB.take(NCH * SUB)
        tabR = AB.take(NCH * SUB)
        AB.push()
        w_psi = AB.take(NCH * 2048)
        w_gate = AB.take(NCH * 1024)
        pw_hi = AB.take(NCH * 8)
        pw_lo = AB.take(NCH * 8)

        AFa.push()
        stage = AFa.take(4096)
        sc_a = AFa.take(512); sc_b = AFa.take(512)
        sc_i = AFa.take(512)
        tau_row = AFa.take(SUB)
        assert AFa.off <= AF_N

        wd = 0
        for (dst, src) in (
            (lg_c, dI["loggam"].rearrange("(c p) -> p c", p=128)),
            (om_c, dI["omega"].rearrange("(c p) -> p c", p=128)),
            (bpsi_c, dI["bpsi"].rearrange("(c p) -> p c", p=128)),
            (bphi_c, dI["bphi"].rearrange("(c p) -> p c", p=128)),
            (bgate_c, dI["bgate"].rearrange("(c p) -> p c", p=128)),
            (rowstg[0:1, 0:1], dI["dtv"].rearrange("(a b) -> a b", a=1)),
            (rowstg[0:1, 1:2], dI["rtemp"].rearrange("(a b) -> a b", a=1)),
            (rowstg[0:1, 4:28], dI["sel"]),
            (identf_s, dI["identf"]),
            (identb_s, dI["identb"]),
            (ident6_s[0:6, :], dI["ident6"]),
            (anch_s[0:6, :], dI["anchT"]),
            (onesr_b[0:1, :], dI["onesrow_b"]),
            (ehot_s, dI["ehot"]),
            (bsel_s[0:6, :], dI["bsel"]),
        ):
            wd = dma("scalar", "WD", dst, src, slow=True)

        sc = p.do("scalar", lambda e: e.activation(
            rowstg[0:1, 2:3], rowstg[0:1, 0:1], AF.Abs), inc="SC",
            waits=[("WD", wd)])
        ve = p.do("vector", lambda e: e.tensor_copy(
            rowstg[0:1, 3:4], rowstg[0:1, 1:2]), inc="VE", waits=[("SC", sc)])
        ve = p.do("vector", lambda e: e.memset(ones_rf[0:1, :], 1.0), inc="VE")
        mm = p.do("tensor", lambda e: e.matmul(
            PB7[:, 0:28], ones_rf[0:1, :], rowstg[0:1, 0:28],
            start=True, stop=True), inc="MM", waits=[("VE", ve)])
        ve = p.do("vector", lambda e: e.tensor_copy(P_b[:, 0:28], PB7[:, 0:28]), inc="VE",
                  waits=[("MM", mm)])

        sc = p.do("scalar", lambda e: e.activation(del_c, lg_c, AF.Exp), inc="SC",
                  waits=[("VE", ve)])
        ve = p.do("vector", lambda e: e.tensor_scalar(
            del_c, del_c, DT_COL, None, ALU.mult), inc="VE", waits=[("SC", sc)])
        sc = p.do("scalar", lambda e: e.activation(
            rho_c, del_c, AF.Exp, scale=np.float32(-1.0)), inc="SC", waits=[("VE", ve)])
        ve_th = p.do("vector", lambda e: e.tensor_scalar(
            th_c, om_c, DT_COL, None, ALU.mult), inc="VE")

        gp = p.do("gpsimd", lambda e: e.iota(
            tau_row, pattern=[[1, SUB]], base=0, channel_multiplier=0,
            allow_small_or_imprecise_dtypes=True), inc="GP")

        sc_iI = sc_i.bitcast(I32)
        sc_d = stage[:, 512:1024]

        def rr_sin_big(dst, ang, shift, waits=()):
            p.do("vector", lambda e, o=sc_a, i=ang, s=float(shift):
                 e.tensor_scalar(o, i, INV_2PI, s, ALU.mult, ALU.add), inc="VE",
                 waits=list(waits) + [("SC", p.cnt["SC"])])
            p.do("vector", lambda e, o=sc_iI, i=sc_a: e.tensor_copy(o, i), inc="VE")
            p.do("vector", lambda e, o=sc_b, i=sc_iI: e.tensor_copy(o, i), inc="VE")
            p.do("vector", lambda e, o=sc_a, a=sc_a, b=sc_b: e.tensor_sub(o, a, b),
                 inc="VE")
            p.do("vector", lambda e, o=sc_d, i=sc_a: e.tensor_single_scalar(
                o, i, 0.5, ALU.is_gt), inc="VE")
            p.do("vector", lambda e, o=sc_b, a=sc_b, b=sc_d: e.tensor_add(o, a, b),
                 inc="VE")
            p.do("vector", lambda e, o=sc_d, i=sc_a: e.tensor_single_scalar(
                o, i, -0.5, ALU.is_lt), inc="VE")
            p.do("vector", lambda e, o=sc_b, a=sc_b, b=sc_d: e.tensor_sub(o, a, b),
                 inc="VE")
            if float(shift) != 0.0:
                p.do("vector", lambda e, o=sc_b, i=sc_b, s=-float(shift):
                     e.tensor_scalar(o, i, s, None, ALU.add), inc="VE")
            p.do("vector", lambda e, o=sc_a, k=sc_b, a=ang:
                 e.scalar_tensor_tensor(o, k, -TP_HI, a, ALU.mult, ALU.add), inc="VE")
            p.do("vector", lambda e, o=sc_a, k=sc_b, a=sc_a:
                 e.scalar_tensor_tensor(o, k, -TP_LO, a, ALU.mult, ALU.add), inc="VE")
            p.do("vector", lambda e, o=sc_a: e.tensor_single_scalar(
                o, sc_a, PI_CL, ALU.min), inc="VE")
            ve2 = p.do("vector", lambda e, o=sc_a: e.tensor_single_scalar(
                o, sc_a, -PI_CL, ALU.max), inc="VE")
            return p.do("scalar", lambda e, o=dst, i=sc_a: e.activation(
                o, i, AF.Sin), inc="SC",
                waits=[("VE", ve2)])

        sc_tab = 0
        ang = stage[:, 0:SUB]
        for cidx in range(NCH):
            ve = p.do("vector", lambda e, o=ang, c=cidx: e.tensor_scalar(
                o, tau_row, th_c[:, c:c + 1], None, ALU.mult), inc="VE",
                waits=[("GP", gp), ("VE", ve_th), ("SC", p.cnt["SC"])])
            rr_sin_big(chunk(tabC, cidx), ang, 0.25)
            sc_tab = rr_sin_big(chunk(tabS, cidx), ang, 0.0)
            ve = p.do("vector", lambda e, o=ang, c=cidx: e.tensor_scalar(
                o, tau_row, del_c[:, c:c + 1], None, ALU.mult), inc="VE",
                waits=[("SC", p.cnt["SC"])])
            sc_tab = p.do("scalar", lambda e, o=chunk(tabR, cidx), i=ang:
                          e.activation(o, i, AF.Exp, scale=np.float32(-1.0)), inc="SC",
                          waits=[("VE", ve)])

        cs_cols(gc_c, gs_c, SUB)
        cs_cols(ec_c, es_c, SUB - 1)

        p.do("vector", lambda e: e.memset(seam_re, 0.0), inc="VE")
        p.do("vector", lambda e: e.memset(seam_im, 0.0), inc="VE")

        last_cast = ("VE", p.cnt["VE"])
        p.w("scalar", "SC", sc_tab)
        for k in range(NCH):
            src = dI["wpsiT"][k * 128:(k + 1) * 128, :]
            wdv = dma("scalar", "WD", stage[:, 0:2048], src, waits=[last_cast])
            cv = p.do("vector", lambda e, o=w_psi[:, k * 2048:(k + 1) * 2048],
                      i=stage[:, 0:2048]: e.tensor_copy(o, i), inc="VE",
                      waits=[("WD", wdv)])
            last_cast = ("VE", cv)
        for k in range(NCH):
            src = dI["wgateT"][k * 128:(k + 1) * 128, :]
            wdv = dma("scalar", "WD", stage[:, 0:1024], src, waits=[last_cast])
            cv = p.do("vector", lambda e, o=w_gate[:, k * 1024:(k + 1) * 1024],
                      i=stage[:, 0:1024]: e.tensor_copy(o, i), inc="VE",
                      waits=[("WD", wdv)])
            last_cast = ("VE", cv)
        for k in range(NCH):
            wdv = dma("scalar", "WD", stage[:, k * NE:(k + 1) * NE],
                      dI["projwT"][k * 128:(k + 1) * 128, :],
                      waits=[last_cast] if k == 0 else ())
        for k in range(NCH):
            sl = stage[:, k * NE:(k + 1) * NE]
            hi = pw_hi[:, k * 8:k * 8 + NE]
            lo = pw_lo[:, k * 8:k * 8 + NE]
            p.do("vector", lambda e, o=hi, i=sl: e.tensor_copy(o, i), inc="VE",
                 waits=[("WD", wdv)])
            p.do("vector", lambda e, o=lo, a=sl, b=hi: e.tensor_sub(o, a, b),
                 inc="VE")

        AFa.pop()
        p.w("sync", "S", p.cnt["S"]); p.w("sync", "L", p.cnt["L"])
        p.barrier()

        AFa.push()
        x_tm = [AFa.take(4096), AFa.take(4096)]
        sq_f32 = AFa.take(1024)
        rtile = AFa.take(512)
        lb_tm = AFa.take(24)
        mlm_f = AFa.take(24)
        m14 = AFa.take(16)
        ln_red = AFa.take(8)
        assert AFa.off <= AF_N

        AB.push()
        xn_bf = AB.take(NCH * SUB)
        xn_lo = AB.take(NCH * SUB)
        g_bf = AB.take(NCH * SUB)
        u_re = AB.take(NCH * SUB)
        u_im = AB.take(NCH * SUB)
        w_re_t = AB.take(NCH * SUB)
        w_im_t = AB.take(NCH * SUB)
        t14 = AB.take(4 * SUB)
        m24 = AB.take(48)
        stg32 = [AB.take(2048).bitcast(F32), AB.take(2048).bitcast(F32)]
        stg16 = [AB.take(1024), AB.take(1024)]
        assert AB.off <= AB_N, AB.off
        jt_re = u_re
        jt_im = u_im

        last_S_xn = 0
        last_S_j = 0
        sq_free = ("VE", 0)
        x_free = [("VE", 0), ("VE", 0)]
        psi_free = [("VE", 0), ("VE", 0)]
        gate_free = [("SC", 0), ("SC", 0)]
        tr_free = [("VE", 0), ("VE", 0)]
        rtr_free = ("VE", 0)
        pb6_free = ("SC", 0)
        rot_gp_prev = 0
        u_prev_ve = 0
        router_mm_prev = 0
        scans_prev = 0
        ln_last_sc = 0
        sfac_ve = None

        wjobs = []
        for eidx in range(NE):
            for j in range(NCH // 2):
                wjobs.append((
                    dI["w1"][eidx, 2 * j * 128:(2 * j + 2) * 128, :]
                    .rearrange("(k p) d -> p k d", p=128),
                    w1sc[eidx, 2 * j * 128:(2 * j + 2) * 128, :]
                    .rearrange("(k p) d -> p k d", p=128), True))
            for hk in range(NKH):
                wjobs.append((dI["w2"][eidx, hk * 128:(hk + 1) * 128, :],
                              w2sc[eidx, hk * 128:(hk + 1) * 128, :], False))
        wst_free = [("VE", 0), ("VE", 0)]
        wst16_S = [0, 0]

        def emit_wjob(i):
            src, dst, pair = wjobs[i]
            par = i % 2
            s32 = stg32[par]
            s16 = stg16[par]
            if pair:
                s32 = s32.rearrange("p (k d) -> p k d", d=512)
                s16 = s16.rearrange("p (k d) -> p k d", d=512)
            wd = dma("scalar", "WD", s32, src, waits=[wst_free[par]])
            cv = p.do("vector", lambda e, o=stg16[par], i2=stg32[par]:
                      e.tensor_copy(o, i2), inc="VE",
                      waits=[("WD", wd), ("S", wst16_S[par])])
            wst_free[par] = ("VE", cv)
            wst16_S[par] = dma("sync", "S", dst, s16,
                               waits=[("VE", cv)])

        for st in range(NSUB):
            for i in range(st * 6, min(st * 6 + 6, len(wjobs))):
                emit_wjob(i)
            xb = x_tm[st % 2]
            p.w("sync", x_free[st % 2][0], x_free[st % 2][1])
            lv = dma("sync", "L", xb.rearrange("p (t d) -> p t d", d=1024),
                     dI["x"][st * SUB:(st + 1) * SUB, :]
                     .rearrange("(t p) d -> p t d", p=128))

            ln_sc = []
            for t in range(4):
                tile = xb[:, t * 1024:(t + 1) * 1024]
                rcol = ln_red[:, 2 * t:2 * t + 1]
                scol = ln_red[:, 2 * t + 1:2 * t + 2]
                ve = p.do("vector", lambda e, o=rcol, i=tile: e.reduce_sum(
                    o, i, axis=mybir.AxisListType.X), inc="VE", waits=[("L", lv)])
                sc = p.do("scalar", lambda e, o=sq_f32, i=tile:
                          e.activation(o, i, AF.Square), inc="SC",
                          waits=[("L", lv), sq_free])
                ve_s2 = p.do("vector", lambda e, o=scol, i=sq_f32: e.reduce_sum(
                    o, i, axis=mybir.AxisListType.X), inc="VE",
                    waits=[("SC", sc)])
                sq_free = ("VE", ve_s2)
                mc = mcol[:, 2 * t:2 * t + 1]
                nmi = mcol[:, 2 * t + 1:2 * t + 2]
                ve = p.do("vector", lambda e, o=mc, i=rcol: e.tensor_scalar(
                    o, i, 1.0 / D, None, ALU.mult), inc="VE",
                    waits=[("SC", ln_last_sc)])
                ve = p.do("vector", lambda e, o=ms0, i=scol: e.tensor_scalar(
                    o, i, 1.0 / D, None, ALU.mult), inc="VE", waits=[("VE", ve_s2)])
                ve = p.do("vector", lambda e, o=ms1, a=mc, b=mc:
                          e.tensor_mul(o, a, b), inc="VE")
                ve = p.do("vector", lambda e, o=ms0, a=ms0, b=ms1:
                          e.tensor_sub(o, a, b), inc="VE")
                sc = p.do("scalar", lambda e, o=ms1, i=ms0:
                          e.activation(o, i, AF.Ln, bias=np.float32(1e-5)), inc="SC",
                          waits=[("VE", ve)])
                sc = p.do("scalar", lambda e, o=ms0, i=ms1:
                          e.activation(o, i, AF.Exp, scale=np.float32(-0.5)), inc="SC")
                ve = p.do("vector", lambda e, o=nmi, m=mc, i=ms0:
                          e.scalar_tensor_tensor(o, m, -1.0, i, ALU.mult, ALU.mult),
                          inc="VE", waits=[("SC", sc)])
                ve = p.do("vector", lambda e, o=mc, i=ms0: e.tensor_copy(o, i),
                          inc="VE")
                sc = p.do("scalar", lambda e, o=tile, i=tile, s=mc, b=nmi:
                          e.activation(o, i, AF.Identity, bias=b, scale=s),
                          inc="SC", waits=[("VE", ve)])
                ln_sc.append(sc)
            ln_last_sc = ln_sc[-1]

            for cidx in range(NCH):
                bank = PB[cidx % 2]
                mm = None
                for t in range(4):
                    wts = [("SC", ln_sc[t])]
                    if t == 0:
                        wts.append(tr_free[cidx % 2])
                    mm = p.do("tensor", lambda e, b=bank, t=t, c=cidx, xb=xb:
                              e.transpose(b[:, t * 128:(t + 1) * 128],
                                          xb[:, t * 1024 + c * 128:
                                             t * 1024 + (c + 1) * 128],
                                          identf_s), inc="MM", waits=wts)
                sc = p.do("scalar", lambda e, o=chunk(xn_bf, cidx), i=bank[:]:
                          e.activation(o, i, AF.Identity), inc="SC",
                          waits=[("MM", mm), ("S", last_S_xn),
                                 ("MM", router_mm_prev)])
                ve = p.do("vector", lambda e, o=chunk(xn_lo, cidx), a=bank[:],
                          b=chunk(xn_bf, cidx): e.tensor_sub(o, a, b), inc="VE",
                          waits=[("SC", sc), ("MM", router_mm_prev)])
                tr_free[cidx % 2] = ("VE", ve)
            xn_all_sc = p.cnt["SC"]
            xn_all_ve = p.cnt["VE"]

            sxn = dma("sync", "S", xn_sc[st], xn_bf, waits=[("SC", xn_all_sc)])
            last_S_xn = sxn

            g_sc = []
            for o in range(NCH):
                bank = PB[2 if o % 2 == 0 else 5]
                mm = None
                for k in range(NCH):
                    wts = []
                    if k == 0:
                        wts = [("SC", xn_all_sc), gate_free[o % 2]]
                    mm = p.do("tensor", lambda e, b=bank, o=o, k=k: e.matmul(
                        b[:], w_gate[:, k * 1024 + o * 128:k * 1024 + (o + 1) * 128],
                        chunk(xn_bf, k), start=(k == 0), stop=(k == NCH - 1)),
                        inc="MM", waits=wts)
                sc = p.do("scalar", lambda e, o2=chunk(g_bf, o), i=bank[:],
                          bc=bgate_c[:, o:o + 1]: e.activation(
                              o2, i, AF.Sigmoid, bias=bc), inc="SC",
                          waits=[("MM", mm), ("VE", u_prev_ve)])
                gate_free[o % 2] = ("SC", sc)
                g_sc.append(sc)

            for comp in range(2):
                udst = u_re if comp == 0 else u_im
                for cidx in range(NCH):
                    o = comp * NCH + cidx
                    bank = PB[3 + (o % 2)]
                    mm = None
                    for k in range(NCH):
                        wts = [psi_free[o % 2]] if k == 0 else []
                        mm = p.do("tensor", lambda e, b=bank, o=o, k=k: e.matmul(
                            b[:], w_psi[:, k * 2048 + o * 128:
                                        k * 2048 + (o + 1) * 128],
                            chunk(xn_bf, k), start=(k == 0), stop=(k == NCH - 1)),
                            inc="MM", waits=wts)
                    ve = p.do("vector", lambda e, o2=chunk(udst, cidx), i=bank[:],
                              bc=bpsi_c[:, o:o + 1], g=chunk(g_bf, cidx):
                              e.scalar_tensor_tensor(o2, i, bc, g, ALU.add,
                                                     ALU.mult),
                              inc="VE", waits=[("MM", mm), ("SC", g_sc[cidx]),
                                               ("GP", rot_gp_prev),
                                               ("S", last_S_j)])
                    psi_free[o % 2] = ("VE", ve)
            u_all_ve = p.cnt["VE"]
            u_prev_ve = u_all_ve

            t1 = t14[:, 0:SUB]; t3 = t14[:, 2 * SUB:3 * SUB]
            for cidx in range(NCH):
                cre = chunk(u_re, cidx); cim = chunk(u_im, cidx)
                ctab = chunk(tabC, cidx); stab = chunk(tabS, cidx)
                uw = [("VE", u_all_ve)] if cidx == 0 else []
                p.do("vector", lambda e, o=chunk(w_re_t, cidx), a=cre, b=ctab:
                     e.tensor_mul(o, a, b), inc="VE", waits=uw)
                p.do("vector", lambda e, o=t1, a=cim, b=stab: e.tensor_mul(o, a, b),
                     inc="VE")
                ve_wr = p.do("vector", lambda e, o=chunk(w_re_t, cidx), a=t1:
                             e.tensor_add(o, o, a), inc="VE")
                gw = [("VE", u_all_ve), ("VE", scans_prev)] if cidx == 0 else []
                p.do("gpsimd", lambda e, o=chunk(w_im_t, cidx), a=cim, b=ctab:
                     e.tensor_mul(o, a, b), inc="GP", waits=gw)
                p.do("gpsimd", lambda e, o=t3, a=cre, b=stab: e.tensor_mul(o, a, b),
                     inc="GP")
                gp_wi = p.do("gpsimd", lambda e, o=chunk(w_im_t, cidx), a=t3:
                             e.tensor_sub(o, o, a), inc="GP")
                p.do("vector", lambda e, o=chunk(jt_re, cidx),
                     d0=rho_c[:, cidx:cidx + 1].broadcast_to([128, SUB]),
                     d1=chunk(w_re_t, cidx), ini=seam_re[:, cidx:cidx + 1]:
                     e.tensor_tensor_scan(o, d0, d1, ini, ALU.mult, ALU.add),
                     inc="VE", waits=[("GP", gp_wi)])
                p.do("vector", lambda e, o=chunk(jt_im, cidx),
                     d0=rho_c[:, cidx:cidx + 1].broadcast_to([128, SUB]),
                     d1=chunk(w_im_t, cidx), ini=seam_im[:, cidx:cidx + 1]:
                     e.tensor_tensor_scan(o, d0, d1, ini, ALU.mult, ALU.add),
                     inc="VE")
            rot_gp_prev = p.cnt["GP"]
            scans_ve = p.cnt["VE"]
            scans_prev = scans_ve

            jre_end = jt_re[:, SUB - 1::SUB]
            jim_end = jt_im[:, SUB - 1::SUB]
            p.do("vector", lambda e: e.tensor_mul(accR, gc_c, jre_end), inc="VE")
            p.do("vector", lambda e: e.tensor_mul(accI, gs_c, jim_end), inc="VE")
            p.do("vector", lambda e: e.tensor_sub(seam_re, accR, accI), inc="VE")
            p.do("vector", lambda e: e.tensor_mul(accR, gc_c, jim_end), inc="VE")
            p.do("vector", lambda e: e.tensor_mul(accI, gs_c, jre_end), inc="VE")
            seam_ve = p.do("vector", lambda e: e.tensor_add(seam_im, accR, accI),
                           inc="VE")
            if st == NSUB - 1:
                p.do("vector", lambda e: e.tensor_mul(accR, ec_c, jre_end), inc="VE")
                p.do("vector", lambda e: e.tensor_mul(accI, es_c, jim_end), inc="VE")
                p.do("vector", lambda e: e.tensor_sub(scols[:, 0:8], accR, accI),
                     inc="VE")
                p.do("vector", lambda e: e.tensor_mul(accR, ec_c, jim_end), inc="VE")
                p.do("vector", lambda e: e.tensor_mul(accI, es_c, jre_end), inc="VE")
                seam_ve = p.do("vector", lambda e: e.tensor_add(
                    scols[:, 8:16], accR, accI), inc="VE")

            sj = dma("sync", "S", j_re[st], jt_re, waits=[("VE", scans_ve)])
            sj = dma("sync", "S", j_im[st], jt_im, waits=[("VE", seam_ve)])
            last_S_j = sj

            pairs = []
            for (wsta, mov) in ((pw_hi, xn_bf), (pw_hi, xn_lo), (pw_lo, xn_bf)):
                for k in range(NCH):
                    pairs.append((wsta[:, k * 8:k * 8 + NE], chunk(mov, k)))
            mm = None
            for i, (wst, mv) in enumerate(pairs):
                wts = [("VE", xn_all_ve), pb6_free] if i == 0 else []
                mm = p.do("tensor", lambda e, w=wst, m=mv, f=(i == 0),
                          l=(i == len(pairs) - 1): e.matmul(
                              PB6[0:NE, :], w, m, start=f, stop=l),
                          inc="MM", waits=wts)
            sc = p.do("scalar", lambda e: e.activation(
                rtile[0:NE, :], PB6[0:NE, :], AF.Tanh,
                scale=np.float32(1.0) / np.float32(0.3)), inc="SC",
                waits=[("MM", mm)])
            mm = p.do("tensor", lambda e: e.matmul(
                PB6[0:NE, :], anch_s[0:6, 0:6], rtile[0:NE, :],
                start=True, stop=True), inc="MM", waits=[("SC", sc)])
            if sfac_ve is None:
                ve = p.do("vector", lambda e: e.tensor_scalar(
                    ms2, RT_COL, 0.1, 2.0, ALU.max, ALU.mult), inc="VE")
                ve = p.do("vector", lambda e: e.reciprocal(SFAC, ms2), inc="VE")
                sfac_ve = p.do("vector", lambda e: e.tensor_scalar(
                    NB6, SFAC, -6.0, None, ALU.mult), inc="VE")
            sc_lg = p.do("scalar", lambda e: e.activation(
                rtile[0:NE, :], PB6[0:NE, :], AF.Identity,
                bias=NB6[0:NE, :], scale=SFAC[0:NE, :]), inc="SC",
                waits=[("MM", mm), ("VE", sfac_ve)])
            pb6_free = ("SC", sc_lg)
            router_mm_prev = p.cnt["MM"]
            mm = None
            for t in range(4):
                wts = [("SC", sc_lg)]
                if t == 0:
                    wts.append(rtr_free)
                mm = p.do("tensor", lambda e, t=t: e.transpose(
                    PB7[:, t * 6:(t + 1) * 6],
                    rtile[0:NE, t * 128:(t + 1) * 128], ident6_s[0:6, 0:6]),
                    inc="MM", waits=wts)
            ve = p.do("vector", lambda e: e.tensor_copy(lb_tm, PB7[:, 0:24]),
                      inc="VE", waits=[("MM", mm)])
            rtr_free = ("VE", ve)
            mask1 = m24[:, 0:24]; mask2 = m24[:, 24:48]; mlm = mlm_f
            m1p = m14[:, 0:4]; m2p = m14[:, 4:8]
            wa = m14[:, 8:12]; wb = m14[:, 12:16]

            def g6(a):
                return a.rearrange("p (g e) -> p g e", e=6)
            p.do("vector", lambda e: e.reduce_max(
                m1p.unsqueeze(2), g6(lb_tm), axis=mybir.AxisListType.X), inc="VE")
            p.do("vector", lambda e: e.tensor_tensor(
                g6(mask1), g6(lb_tm),
                m1p.unsqueeze(2).broadcast_to([128, 4, 6]), ALU.is_equal),
                inc="VE")
            p.do("vector", lambda e: e.scalar_tensor_tensor(
                mlm, mask1, -1e30, lb_tm, ALU.mult, ALU.add), inc="VE")
            p.do("vector", lambda e: e.reduce_max(
                m2p.unsqueeze(2), g6(mlm), axis=mybir.AxisListType.X), inc="VE")
            p.do("vector", lambda e: e.tensor_tensor(
                g6(mask2), g6(mlm),
                m2p.unsqueeze(2).broadcast_to([128, 4, 6]), ALU.is_equal),
                inc="VE")
            ve = p.do("vector", lambda e: e.tensor_sub(m1p, m1p, m2p), inc="VE")
            sc = p.do("scalar", lambda e: e.activation(wa, m1p, AF.Sigmoid),
                      inc="SC", waits=[("VE", ve)])
            p.do("vector", lambda e: e.tensor_scalar(
                wb, wa, -1.0, 1.0, ALU.mult, ALU.add), inc="VE",
                waits=[("SC", sc)])
            p.do("vector", lambda e: e.tensor_tensor(
                g6(mask1), g6(mask1),
                wa.unsqueeze(2).broadcast_to([128, 4, 6]), ALU.mult), inc="VE")
            p.do("vector", lambda e: e.tensor_tensor(
                g6(mask2), g6(mask2),
                wb.unsqueeze(2).broadcast_to([128, 4, 6]), ALU.mult), inc="VE")
            p.do("vector", lambda e, o=ew_tm[:, st * 24:(st + 1) * 24]:
                 e.tensor_add(o, mask1, mask2), inc="VE")

            x_free[st % 2] = ("MM", p.cnt["MM"])

        AFa.pop()
        AB.pop()
        AB.pop()
        AB.push()
        p.w("sync", "S", p.cnt["S"]); p.w("sync", "L", p.cnt["L"])
        p.barrier()

        AFa.push()
        stage2 = AFa.take(2048)
        x2 = AFa.take(4096)
        attn_acc = AFa.take(4096)
        stat4 = stage2[:, 0:512]
        rowf = stage2[:, 512:1024]
        rowg = stage2[:, 1024:1536]
        assert AFa.off <= AF_N

        w_phi = AB.take(NCH * 2048)
        hJ_re = AB.take(NCH * SUB)
        hJ_im = AB.take(NCH * SUB)
        xn2 = AB.take(NCH * SUB)
        phR = AB.take(NCH * SUB)
        phI = AB.take(NCH * SUB)
        hsq_b = phI
        he_all = AB.take(2 * NKH * SUB)
        he_b = [he_all[:, 0:NKH * SUB], he_all[:, NKH * SUB:2 * NKH * SUB]]
        w1b = [AB.take(NCH * DH), AB.take(NCH * DH)]
        w2b = [AB.take(NKH * 1024), AB.take(NKH * 1024)]
        rowsb = AB.take(4 * SUB)
        ewrow = AB.take(SUB)
        ta = AB.take(SUB)
        tb = AB.take(SUB)
        assert AB.off <= AB_N, AB.off
        mb_re = rowsb[:, 0:SUB]; ib_re = rowsb[:, SUB:2 * SUB]
        mb_im = rowsb[:, 2 * SUB:3 * SUB]; ib_im = rowsb[:, 3 * SUB:4 * SUB]
        rowm_b = ta
        rowi_b = tb

        half_free = [("VE", 0), ("VE", 0)]
        for k in range(NCH):
            for hh in range(2):
                par = hh
                sl = stage2[:, par * 1024:(par + 1) * 1024]
                wdv = dma("scalar", "WD", sl,
                          dI["wphiT"][k * 128:(k + 1) * 128,
                                      hh * 1024:(hh + 1) * 1024],
                          waits=[half_free[par]])
                cv = p.do("vector", lambda e,
                          o=w_phi[:, k * 2048 + hh * 1024:
                                  k * 2048 + (hh + 1) * 1024],
                          i=sl: e.tensor_copy(o, i), inc="VE",
                          waits=[("WD", wdv)])
                half_free[par] = ("VE", cv)
        wphi_ready = max(half_free[0][1], half_free[1][1])

        lv = dma("sync", "L", hJ_re, j_re[0])
        lv = dma("sync", "L", hJ_im, j_im[0])
        j_ready = lv
        xn2_ready = dma("sync", "L", xn2, xn_sc[0])
        x_rdy = dma("sync", "L", x2.rearrange("p (t d) -> p t d", d=1024),
                    dI["x"][0:SUB, :].rearrange("(t p) d -> p t d", p=128))

        w1_rdy = {}
        w2_rdy = {}

        def load_w1(g, wait):
            if g >= NSUB * NE:
                return
            w1_rdy[g] = dma("scalar", "WD",
                            w1b[g % 2].rearrange("p (k d) -> p k d", d=DH),
                            w1sc[g % NE].rearrange("(k p) d -> p k d", p=128),
                            waits=[wait])

        def load_w2(g, wait):
            if g >= NSUB * NE:
                return
            w2_rdy[g] = dma("scalar", "WD",
                            w2b[g % 2].rearrange("p (k d) -> p k d", d=1024),
                            w2sc[g % NE].rearrange("(k p) d -> p k d", p=128),
                            waits=[wait])

        load_w1(0, None)
        load_w1(1, None)
        load_w2(0, None)
        load_w2(1, None)

        cc = p.do("gpsimd", lambda e: e.dma_start(
            out=cc_in.rearrange("(p f) -> p f", p=128), in_=scols), inc="CC")
        co = p.do("gpsimd", lambda e: e.collective_compute(
            "AllGather", ALU.bypass, replica_groups=rg,
            ins=[cc_in], outs=[cc_out]), inc=("CO", 1), waits=[("CC", cc)],
            attach=True)
        for r in range(NCORE):
            cc = p.do("gpsimd", lambda e, r=r: e.dma_start(
                out=s_all[:, r * 16:(r + 1) * 16],
                in_=cc_out[r].rearrange("(p f) -> p f", p=128)),
                inc="CC", waits=[("CO", co)])
        CC_ALL = cc

        p.do("vector", lambda e: e.memset(sin_re, 0.0), inc="VE",
             waits=[("CC", CC_ALL)])
        p.do("vector", lambda e: e.memset(sin_im, 0.0), inc="VE")
        for m in range(3):
            p.do("vector", lambda e: e.memset(selR, 0.0), inc="VE")
            p.do("vector", lambda e: e.memset(selI, 0.0), inc="VE")
            for r in range(NCORE):
                p.do("vector", lambda e, m=m, r=r: e.scalar_tensor_tensor(
                    selR, s_all[:, r * 16:r * 16 + 8], SEL_COL(m, r), selR,
                    ALU.mult, ALU.add), inc="VE")
                p.do("vector", lambda e, m=m, r=r: e.scalar_tensor_tensor(
                    selI, s_all[:, r * 16 + 8:r * 16 + 16], SEL_COL(m, r), selI,
                    ALU.mult, ALU.add), inc="VE")
            if m == 0:
                p.do("vector", lambda e: e.tensor_add(sin_re, sin_re, selR),
                     inc="VE")
                p.do("vector", lambda e: e.tensor_add(sin_im, sin_im, selI),
                     inc="VE")
            else:
                sc = cs_cols(fr_c, fr_s, C * m)
                ve = p.do("vector", lambda e, pi=float(C * m): e.tensor_scalar(
                    accR, del_c, pi, None, ALU.mult), inc="VE",
                    waits=[("SC", sc)])
                sc = p.do("scalar", lambda e: e.activation(
                    accR, accR, AF.Exp, scale=np.float32(-1.0)), inc="SC",
                    waits=[("VE", ve)])
                p.do("vector", lambda e: e.tensor_mul(fr_c, fr_c, accR), inc="VE",
                     waits=[("SC", sc)])
                p.do("vector", lambda e: e.tensor_mul(fr_s, fr_s, accR), inc="VE")
                p.do("vector", lambda e: e.tensor_mul(accR, fr_c, selR), inc="VE")
                p.do("vector", lambda e: e.tensor_mul(accI, fr_s, selI), inc="VE")
                p.do("vector", lambda e: e.tensor_sub(accR, accR, accI), inc="VE")
                p.do("vector", lambda e: e.tensor_add(sin_re, sin_re, accR),
                     inc="VE")
                p.do("vector", lambda e: e.tensor_mul(accR, fr_c, selI), inc="VE")
                p.do("vector", lambda e: e.tensor_mul(accI, fr_s, selR), inc="VE")
                p.do("vector", lambda e: e.tensor_add(accR, accR, accI), inc="VE")
                p.do("vector", lambda e: e.tensor_add(sin_im, sin_im, accR),
                     inc="VE")

        def emit_Z(b):
            if b >= NSUB:
                return
            pimm = b * SUB + 1
            sc = cs_cols(fr_c, fr_s, pimm)
            ve = p.do("vector", lambda e, pi=float(pimm): e.tensor_scalar(
                accR, del_c, pi, None, ALU.mult), inc="VE", waits=[("SC", sc)])
            sc = p.do("scalar", lambda e: e.activation(
                accR, accR, AF.Exp, scale=np.float32(-1.0)), inc="SC",
                waits=[("VE", ve)])
            p.do("vector", lambda e: e.tensor_mul(fr_c, fr_c, accR), inc="VE",
                 waits=[("SC", sc)])
            p.do("vector", lambda e: e.tensor_mul(fr_s, fr_s, accR), inc="VE")
            zr = zre_s[:, b * 8:(b + 1) * 8]
            zi = zim_s[:, b * 8:(b + 1) * 8]
            p.do("vector", lambda e: e.tensor_mul(accR, fr_c, sin_re), inc="VE")
            p.do("vector", lambda e: e.tensor_mul(accI, fr_s, sin_im), inc="VE")
            p.do("vector", lambda e, o=zr: e.tensor_sub(o, accR, accI),
                 inc="VE")
            p.do("vector", lambda e: e.tensor_mul(accR, fr_c, sin_im), inc="VE")
            p.do("vector", lambda e: e.tensor_mul(accI, fr_s, sin_re), inc="VE")
            p.do("vector", lambda e, o=zi: e.tensor_add(o, accR, accI),
                 inc="VE")

        emit_Z(0)


        pb01_free = [("VE", 0), ("VE", 0)]
        pb23_free = [("VE", 0), ("VE", 0)]
        pb4_free = ("VE", 0)
        pb56_free = [("VE", 0), ("VE", 0)]
        pb7_free = ("VE", 0)
        he_free = [("MM", 0), ("MM", 0)]
        attn_ve_prev = 0
        attn_gp_prev = 0
        res_mm_prev = 0
        ewb_mm_prev = 0
        rowcast_prev = 0
        stats2_last = 0
        h_mm_last = 0

        for st in range(NSUB):
            first = True
            for cidx in range(NCH):
                hr = chunk(hJ_re, cidx); hi = chunk(hJ_im, cidx)
                rtab = chunk(tabR, cidx)
                zr = zre_s[:, st * 8 + cidx:st * 8 + cidx + 1]
                zi = zim_s[:, st * 8 + cidx:st * 8 + cidx + 1]
                wts = [("L", j_ready), ("MM", ewb_mm_prev)] if first else []
                p.do("vector", lambda e, i=zr: e.tensor_copy(ms0, i), inc="VE",
                     waits=wts)
                p.do("vector", lambda e, i=zi: e.tensor_copy(ms1, i), inc="VE")
                p.do("vector", lambda e, o=hr, t=rtab:
                     e.scalar_tensor_tensor(o, t, ms0, o, ALU.mult, ALU.add),
                     inc="VE")
                p.do("vector", lambda e, o=hi, t=rtab:
                     e.scalar_tensor_tensor(o, t, ms1, o, ALU.mult, ALU.add),
                     inc="VE")
                first = False
            p.do("vector", lambda e: e.tensor_mul(phR, hJ_re, tabC), inc="VE")
            p.do("vector", lambda e: e.tensor_mul(he_all, hJ_im, tabS),
                 inc="VE")
            p.do("vector", lambda e: e.tensor_sub(phR, phR, he_all), inc="VE")
            p.do("vector", lambda e: e.tensor_mul(hJ_im, hJ_im, tabC),
                 inc="VE")
            p.do("vector", lambda e: e.tensor_mul(he_all, hJ_re, tabS),
                 inc="VE")
            p.do("vector", lambda e: e.tensor_add(hJ_im, hJ_im, he_all),
                 inc="VE")
            p.do("vector", lambda e: e.tensor_copy(hJ_re, phR), inc="VE")
            p.do("vector", lambda e: e.tensor_mul(phI, hJ_re, hJ_re),
                 inc="VE")
            p.do("vector", lambda e: e.tensor_mul(he_all, hJ_im, hJ_im),
                 inc="VE")
            rot_ve = p.cnt["VE"]

            for o in range(NCH):
                b = o % 2
                bank = PB[b]
                mm = None
                for k in range(NCH):
                    wts = []
                    if k == 0:
                        wts = [("L", xn2_ready), ("VE", wphi_ready),
                               pb01_free[b]]
                    mm = p.do("tensor", lambda e, bk=bank, o=o, k=k: e.matmul(
                        bk[:], w_phi[:, k * 2048 + o * 128:
                                     k * 2048 + (o + 1) * 128],
                        chunk(xn2, k), start=(k == 0), stop=(k == NCH - 1)),
                        inc="MM", waits=wts)
                ve = p.do("vector", lambda e, o2=chunk(phR, o), i=bank[:],
                          bc=bphi_c[:, o:o + 1]: e.tensor_scalar(
                              o2, i, bc, None, ALU.add), inc="VE",
                          waits=[("MM", mm)])
                pb01_free[b] = ("VE", ve)

            mm_stats = None
            for cidx in range(NCH):
                wts = []
                if cidx == 0:
                    wts = [("VE", rot_ve), pb23_free[0], pb7_free]
                l = (cidx == NCH - 1)
                mm_stats = p.do("tensor", lambda e, f=(cidx == 0),
                                m=chunk(hJ_re, cidx): e.matmul(
                                    PB2[0:2, :], ehot_s[:, 0:2], m,
                                    start=f, stop=False), inc="MM", waits=wts)
                mm_stats = p.do("tensor", lambda e, f=(cidx == 0),
                                m=chunk(phI, cidx): e.matmul(
                                    PB7[0:2, :], ehot_s[:, 0:2], m,
                                    start=f, stop=False), inc="MM")
                mm_stats = p.do("tensor", lambda e, m=chunk(hJ_im, cidx),
                                l=l: e.matmul(
                                    PB2[0:2, :], ehot_s[:, 4:6], m,
                                    start=False, stop=l), inc="MM")
                mm_stats = p.do("tensor", lambda e, m=chunk(he_all, cidx),
                                l=l: e.matmul(
                                    PB7[0:2, :], ehot_s[:, 4:6], m,
                                    start=False, stop=l), inc="MM")
            stats2_mm = mm_stats
            stats2_last = stats2_mm

            phI_ve = {}
            for o in range(NCH):
                b = o % 2
                bank = PB[b]
                mm = None
                for k in range(NCH):
                    wts = [pb01_free[b]] if k == 0 else []
                    mm = p.do("tensor", lambda e, bk=bank, o=o, k=k: e.matmul(
                        bk[:], w_phi[:, k * 2048 + (NCH + o) * 128:
                                     k * 2048 + (NCH + o + 1) * 128],
                        chunk(xn2, k), start=(k == 0), stop=(k == NCH - 1)),
                        inc="MM", waits=wts)
                ve = p.do("vector", lambda e, o2=chunk(phI, o), i=bank[:],
                          bc=bphi_c[:, NCH + o:NCH + o + 1]: e.tensor_scalar(
                              o2, i, bc, None, ALU.add), inc="VE",
                          waits=[("MM", mm), ("MM", stats2_mm)])
                pb01_free[b] = ("VE", ve)
                phI_ve[o] = ve

            sc = p.do("scalar", lambda e: e.activation(
                stat4[0:2, :], PB2[0:2, :], AF.Identity,
                scale=np.float32(1.0 / D)), inc="SC",
                waits=[("MM", stats2_mm), ("VE", rowcast_prev)])
            sc = p.do("scalar", lambda e: e.activation(
                rowf[0:2, :], PB7[0:2, :], AF.Identity,
                scale=np.float32(1.0 / D)), inc="SC")
            ve = p.do("vector", lambda e: e.tensor_mul(
                rowg[0:2, :], stat4[0:2, :], stat4[0:2, :]), inc="VE",
                waits=[("SC", sc)])
            ve = p.do("vector", lambda e: e.tensor_sub(
                rowf[0:2, :], rowf[0:2, :], rowg[0:2, :]), inc="VE")
            ve = p.do("vector", lambda e: e.tensor_single_scalar(
                rowf[0:2, :], rowf[0:2, :], 1e-30, ALU.max), inc="VE")
            sc = p.do("scalar", lambda e: e.activation(
                rowg[0:2, :], rowf[0:2, :], AF.Ln), inc="SC",
                waits=[("VE", ve)])
            sc = p.do("scalar", lambda e: e.activation(
                rowf[0:2, :], rowg[0:2, :], AF.Exp, scale=np.float32(0.5)),
                inc="SC")
            rows_sc = sc
            ve = p.do("vector", lambda e: e.tensor_scalar(
                rowf[0:2, :], rowf[0:2, :], 1e-6, None, ALU.add), inc="VE",
                waits=[("SC", sc)])
            ve = p.do("vector", lambda e: e.reciprocal(
                rowg[0:2, :], rowf[0:2, :]), inc="VE")
            ve = p.do("vector", lambda e: e.tensor_copy(
                rowm_b[0:2, :], stat4[0:2, :]), inc="VE")
            ve = p.do("vector", lambda e: e.tensor_copy(
                rowi_b[0:2, :], rowg[0:2, :]), inc="VE")
            rowcast = ve
            rowcast_prev = rowcast

            mmprev = {4: None, 7: None}
            mb_mm = None
            for src, j, dstrow, bank, bid in (
                    (rowm_b, 0, mb_re, PB4, 4), (rowm_b, 1, mb_im, PB7, 7),
                    (rowi_b, 0, ib_re, PB4, 4), (rowi_b, 1, ib_im, PB7, 7)):
                wts = [("VE", rowcast)]
                if mmprev[bid] is None:
                    wts.append(pb4_free if bid == 4 else ("SC", rows_sc))
                else:
                    wts.append(("VE", mmprev[bid]))
                mb_mm = p.do("tensor", lambda e, bk=bank, s=src, j=j: e.matmul(
                    bk[:], bsel_s[0:2, j * 128:(j + 1) * 128], s[0:2, :],
                    start=True, stop=True), inc="MM", waits=wts)
                cv = p.do("vector", lambda e, o=dstrow, bk=bank:
                          e.tensor_copy(o, bk[:]), inc="VE",
                          waits=[("MM", mb_mm)])
                mmprev[bid] = cv
            mb_mm_last = mb_mm
            pb4_free = ("VE", mmprev[4])

            mm = None
            for t in range(4):
                wts = [("VE", mmprev[7])] if t == 0 else []
                mm = p.do("tensor", lambda e, t=t,
                          i=ew_tm[:, st * 24 + t * 6:st * 24 + (t + 1) * 6]:
                          e.transpose(PBb[7][0:6, t * 128:(t + 1) * 128], i,
                                      identb_s), inc="MM", waits=wts)
            ewrow_ve = p.do("vector", lambda e: e.tensor_copy(
                ewrow[0:6, :], PBb[7][0:6, 0:512]), inc="VE",
                waits=[("MM", mm), ("MM", ewb_mm_prev)])
            pb7_free = ("VE", ewrow_ve)

            for cidx in range(NCH):
                hr = chunk(hJ_re, cidx); hi = chunk(hJ_im, cidx)
                wts = []
                if cidx == 0:
                    wts = [("MM", mb_mm_last), ("MM", res_mm_prev)]
                p.do("vector", lambda e, a=hr: e.tensor_sub(ta, a, mb_re),
                     inc="VE", waits=wts)
                p.do("vector", lambda e: e.tensor_mul(ta, ta, ib_re), inc="VE")
                p.do("vector", lambda e, o=chunk(attn_acc, cidx),
                     b=chunk(phR, cidx): e.tensor_mul(o, ta, b), inc="VE")
                p.do("vector", lambda e, a=hi: e.tensor_sub(tb, a, mb_im),
                     inc="VE")
                p.do("vector", lambda e: e.tensor_mul(tb, tb, ib_im), inc="VE")
                p.do("vector", lambda e, b=chunk(phI, cidx):
                     e.tensor_mul(tb, tb, b), inc="VE")
                p.do("vector", lambda e, o=chunk(attn_acc, cidx):
                     e.tensor_add(o, o, tb), inc="VE")
            attn_ve_prev = p.cnt["VE"]

            for eidx in range(NE):
                g = st * NE + eidx
                par = g % 2
                wts = [("VE", ewrow_ve), pb4_free]
                mmb = p.do("tensor", lambda e, ei=eidx: e.matmul(
                    PB4[:], bsel_s[0:6, ei * 128:(ei + 1) * 128],
                    ewrow[0:6, :], start=True, stop=True), inc="MM", waits=wts)
                he = he_b[par]
                he_ve = None
                for hk in range(NKH):
                    b = hk % 2
                    bank = PB[b]
                    mm = None
                    for k in range(NCH):
                        wts = []
                        if k == 0:
                            wts = [("WD", w1_rdy[g]), ("L", xn2_ready),
                                   pb01_free[b]]
                        mm = p.do("tensor",
                                  lambda e, bk=bank, hk=hk, k=k, par=par:
                                  e.matmul(
                                      bk[:],
                                      w1b[par][:, k * DH + hk * 128:
                                               k * DH + (hk + 1) * 128],
                                      chunk(xn2, k), start=(k == 0),
                                      stop=(k == NCH - 1)),
                                  inc="MM", waits=wts)
                    sc = p.do("scalar", lambda e, hk=hk, he=he, bk=bank:
                              e.activation(he[:, hk * SUB:(hk + 1) * SUB],
                                           bk[:], AF.Gelu_apprx_tanh),
                              inc="SC", waits=[("MM", mm), he_free[par]])
                    he_ve = p.do("vector", lambda e, hk=hk, he=he:
                                 e.tensor_tensor(he[:, hk * SUB:(hk + 1) * SUB],
                                                 he[:, hk * SUB:(hk + 1) * SUB],
                                                 PB4[:], ALU.mult),
                                 inc="VE", waits=[("SC", sc), ("MM", mmb)])
                    pb01_free[b] = ("SC", sc)
                h_mm_last = p.cnt["MM"]
                load_w1(g + 2, ("MM", h_mm_last))
                pb4_free = ("VE", he_ve)

                for o in range(NCH):
                    b = o % 2
                    bank = PB[2 + b]
                    mm = None
                    for hk in range(NKH):
                        wts = []
                        if hk == 0:
                            wts = [("VE", he_ve), ("WD", w2_rdy[g]),
                                   pb23_free[b]]
                            if eidx == 0 and o < 2:
                                wts.append(("SC", rows_sc))
                        mm = p.do("tensor",
                                  lambda e, bk=bank, o=o, hk=hk, par=par,
                                  he=he: e.matmul(
                                      bk[:],
                                      w2b[par][:, hk * 1024 + o * 128:
                                               hk * 1024 + (o + 1) * 128],
                                      he[:, hk * SUB:(hk + 1) * SUB],
                                      start=(hk == 0), stop=(hk == NKH - 1)),
                                  inc="MM", waits=wts)
                    yv = p.do("vector", lambda e, o2=chunk(attn_acc, o),
                              bk=bank: e.tensor_add(o2, o2, bk[:]), inc="VE",
                              waits=[("MM", mm)])
                    pb23_free[b] = ("VE", yv)
                y_mm = p.cnt["MM"]
                load_w2(g + 2, ("MM", y_mm))
                he_free[par] = ("MM", y_mm)
            y_all_ve = p.cnt["VE"]
            ewb_mm_prev = p.cnt["MM"]

            for t in range(4):
                for half in range(2):
                    bank = PB[5 + half]
                    mm = None
                    for ci in range(4):
                        cidx = half * 4 + ci
                        wts = [("VE", y_all_ve), pb56_free[half]] \
                            if ci == 0 else []
                        mm = p.do("tensor",
                                  lambda e, bi=5 + half, ci=ci, c=cidx, t=t:
                                  e.transpose(
                                      PB[bi][:, ci * 128:(ci + 1) * 128],
                                      chunk(attn_acc, c)[:, t * 128:
                                                         (t + 1) * 128],
                                      identf_s), inc="MM", waits=wts)
                    ve = p.do("vector", lambda e, bi=5 + half, t=t, h=half:
                              e.tensor_add(
                                  x2[:, t * 1024 + h * 512:
                                     t * 1024 + (h + 1) * 512],
                                  x2[:, t * 1024 + h * 512:
                                     t * 1024 + (h + 1) * 512],
                                  PB[bi][:, 0:512]), inc="VE",
                              waits=[("MM", mm), ("L", x_rdy)])
                    pb56_free[half] = ("VE", ve)
            res_ve = p.cnt["VE"]
            res_mm_prev = p.cnt["MM"]

            if st + 1 < NSUB:
                lv = dma("sync", "L", hJ_re, j_re[st + 1],
                         waits=[("VE", attn_ve_prev),
                                ("MM", stats2_last)])
                lv = dma("sync", "L", hJ_im, j_im[st + 1])
                j_ready = lv
                xn2_ready = dma("sync", "L", xn2, xn_sc[st + 1],
                                waits=[("MM", h_mm_last)])

            sv = p.do("sync", lambda e, st=st: e.dma_start(
                out=out_d[st * SUB:(st + 1) * SUB, :]
                .rearrange("(t p) d -> p t d", p=128),
                in_=x2.rearrange("p (t d) -> p t d", d=1024)),
                inc="S", waits=[("VE", res_ve)])

            emit_Z(st + 1)
            if st + 1 < NSUB:
                x_rdy = dma("sync", "L",
                            x2.rearrange("p (t d) -> p t d", d=1024),
                            dI["x"][(st + 1) * SUB:(st + 2) * SUB, :]
                            .rearrange("(t p) d -> p t d", p=128),
                            waits=[("S", sv)])

        p.w("sync", "S", p.cnt["S"])
        p.w("sync", "L", p.cnt["L"])
        AFa.pop(); AB.pop()

        @block.sync
        def _(e):
            for f in p.q["sync"]:
                f(e)

        @block.tensor
        def _(e):
            for f in p.q["tensor"]:
                f(e)

        @block.vector
        def _(e):
            for f in p.q["vector"]:
                f(e)

        @block.scalar
        def _(e):
            for f in p.q["scalar"]:
                f(e)

        @block.gpsimd
        def _(e):
            for f in p.q["gpsimd"]:
                f(e)

    return nc


def _host_constants():
    ehot = np.zeros((128, 16), dtype=ml_dtypes.bfloat16)
    for j in range(4):
        ehot[:, 4 * j + j] = 1
    bsel = np.zeros((6, 768), dtype=ml_dtypes.bfloat16)
    for j in range(6):
        bsel[j, j * 128:(j + 1) * 128] = 1
    return dict(
        identf=np.eye(128, dtype=np.float32),
        identb=np.eye(128, dtype=ml_dtypes.bfloat16),
        ident6=np.eye(6, dtype=np.float32),
        onesrow_b=np.ones((1, 128), dtype=ml_dtypes.bfloat16),
        ehot=ehot,
        bsel=bsel,
    )


def kernel(L0_attn_x, L0_attn_omega, L0_attn_log_gamma, L0_attn_dt,
           L0_attn_W_psi, L0_attn_b_psi, L0_attn_W_phi, L0_attn_b_phi,
           L0_attn_W_gate, L0_attn_b_gate, L0_ffn_proj_w, L0_ffn_anchors,
           L0_ffn_routing_temp, L0_ffn_w1, L0_ffn_w2):
    if "nc" not in _CACHE:
        _CACHE["nc"] = build_program()
    nc = _CACHE["nc"]

    f32 = np.float32
    x = np.asarray(L0_attn_x, f32)
    shared = dict(
        wpsiT=np.ascontiguousarray(np.asarray(L0_attn_W_psi, f32).T),
        wphiT=np.ascontiguousarray(np.asarray(L0_attn_W_phi, f32).T),
        wgateT=np.ascontiguousarray(np.asarray(L0_attn_W_gate, f32).T),
        bpsi=np.asarray(L0_attn_b_psi, f32),
        bphi=np.asarray(L0_attn_b_phi, f32),
        bgate=np.asarray(L0_attn_b_gate, f32),
        omega=np.asarray(L0_attn_omega, f32),
        loggam=np.asarray(L0_attn_log_gamma, f32),
        dtv=np.asarray(L0_attn_dt, f32).reshape(1),
        rtemp=np.asarray(L0_ffn_routing_temp, f32).reshape(1),
        projwT=np.ascontiguousarray(np.asarray(L0_ffn_proj_w, f32).T),
        anchT=np.ascontiguousarray(np.asarray(L0_ffn_anchors, f32).T),
        w1=np.ascontiguousarray(np.asarray(L0_ffn_w1, f32)),
        w2=np.ascontiguousarray(np.asarray(L0_ffn_w2, f32)),
        **_host_constants(),
    )
    in_maps = []
    for core in range(NCORE):
        seq, ch = divmod(core, CPS)
        sel = np.zeros((3, NCORE), f32)
        for m in range(3):
            j = ch - 1 - m
            if j >= 0:
                sel[m, seq * CPS + j] = 1.0
        im = dict(shared)
        im["x"] = np.ascontiguousarray(x[seq, ch * C:(ch + 1) * C])
        im["sel"] = np.ascontiguousarray(sel.reshape(1, 24))
        in_maps.append(im)

    res = run_bass_kernel_spmd(nc, in_maps, core_ids=list(range(NCORE)))
    out = np.empty((B, L, D), np.float32)
    for core in range(NCORE):
        seq, ch = divmod(core, CPS)
        out[seq, ch * C:(ch + 1) * C] = res.results[core]["out"]
    return out


# version 12 (final)
# speedup vs baseline: 1.7288x; 1.6351x over previous
import numpy as np
import ml_dtypes
import concourse.bass as bass
import concourse.mybir as mybir
from concourse.bass_utils import run_bass_kernel_spmd
from concourse.replica_groups import maybe_share_collective_output_space

F32 = mybir.dt.float32
BF16 = mybir.dt.bfloat16
I32 = mybir.dt.int32
AF = mybir.ActivationFunctionType
ALU = mybir.AluOpType

NCORE = 8
B, L, D = 2, 16384, 1024
CPS = NCORE // B
C = L // CPS
SUB = 512
NSUB = C // SUB
NCH = D // 128
NE, DH = 6, 512
NKH = DH // 128
TWO_PI = float(np.float32(2 * np.pi))
INV_2PI = float(np.float32(1.0 / (2 * np.pi)))
TP_HI = float(np.float32(2 * np.pi))
TP_LO = float(np.float32(2 * np.pi - np.float64(np.float32(2 * np.pi))))
PI_CL = float(np.float32(3.1415925))

_CACHE = {}


class Prog:
    ENGS = ("sync", "tensor", "vector", "scalar", "gpsimd")
    SEMS = ("L", "S", "WD", "MM", "VE", "SC", "GP", "CC", "PH", "CO")
    DMA_SEMS = ("L", "S", "WD", "CC")

    def __init__(self):
        self.q = {e: [] for e in self.ENGS}
        self.cnt = {s: 0 for s in self.SEMS}
        self.sems = {}
        self.lastwait = {}

    def w(self, eng, sem, val):
        if val is None or val <= 0:
            return
        k = (eng, sem)
        if self.lastwait.get(k, -1) >= val:
            return
        self.lastwait[k] = val

        def f(e, sem=sem, val=val):
            e.wait_ge(self.sems[sem], val)
        self.q[eng].append(f)

    def do(self, eng, build, inc=None, waits=(), attach=False):
        for wt in waits:
            if wt is None:
                continue
            sem, val = wt
            self.w(eng, sem, val)
        sem = None
        amt = 0
        if inc is not None:
            if isinstance(inc, tuple):
                sem, amt = inc
            else:
                sem = inc
                amt = 16 if sem in self.DMA_SEMS else 1
            self.cnt[sem] += amt

        def f(e, build=build, sem=sem, amt=amt, eng=eng, attach=attach):
            ins = build(e)
            if sem is not None:
                if attach or amt == 16 or eng in ("sync", "tensor"):
                    ins.then_inc(self.sems[sem], amt)
                else:
                    e.drain().then_inc(self.sems[sem], amt)
        self.q[eng].append(f)
        return self.cnt[sem] if sem is not None else None

    def barrier(self):
        comp = ("tensor", "vector", "scalar", "gpsimd")
        target = self.cnt["PH"] + len(comp)
        for eng in comp:
            def f(e):
                e.drain().then_inc(self.sems["PH"], 1)
            self.q[eng].append(f)
        self.cnt["PH"] = target
        for eng in self.ENGS:
            for sem in ("S", "L", "WD", "CC"):
                self.w(eng, sem, self.cnt[sem])
            self.w(eng, "PH", target)
        return target


class Alo:
    def __init__(self, t, total, name):
        self.t, self.total, self.name = t, total, name
        self.off = 0
        self.marks = []

    def take(self, n):
        o = self.off
        assert o + n <= self.total, f"{self.name} overflow {o}+{n}>{self.total}"
        self.off += n
        return self.t[:, o:o + n]

    def push(self):
        self.marks.append(self.off)

    def pop(self):
        self.off = self.marks.pop()


def build_program():
    nc = bass.Bass("TRN2", target_bir_lowering=False, debug=False, detect_race_conditions=False)

    dI = {}
    def din(name, shape, dt=F32):
        dI[name] = nc.dram_tensor(name, shape, dt, kind="ExternalInput").ap()
    din("x", [C, D])
    din("wpsiT", [D, 2 * D]); din("wphiT", [D, 2 * D]); din("wgateT", [D, D])
    din("bpsi", [2 * D]); din("bphi", [2 * D]); din("bgate", [D])
    din("omega", [D]); din("loggam", [D]); din("dtv", [1]); din("rtemp", [1])
    din("projwT", [D, NE]); din("anchT", [NE, NE])
    din("w1", [NE, D, DH]); din("w2", [NE, DH, D])
    din("identf", [128, 128]); din("identb", [128, 128], BF16)
    din("ident6", [6, 6])
    din("onesrow_b", [1, 128], BF16)
    din("ehot", [128, 16], BF16)
    din("bsel", [6, 768], BF16)
    din("sel", [1, 24])

    out_d = nc.dram_tensor("out", [C, D], F32, kind="ExternalOutput").ap()

    j_re = nc.dram_tensor("j_re", [NSUB, 128, NCH * SUB], BF16, kind="Internal").ap()
    j_im = nc.dram_tensor("j_im", [NSUB, 128, NCH * SUB], BF16, kind="Internal").ap()
    xn_sc = nc.dram_tensor("xn_sc", [NSUB, 128, NCH * SUB], BF16, kind="Internal").ap()
    w1sc = nc.dram_tensor("w1sc", [NE, D, DH], BF16, kind="Internal").ap()
    w2sc = nc.dram_tensor("w2sc", [NE, DH, D], BF16, kind="Internal").ap()
    cc_in = nc.dram_tensor("cc_in", [128 * 16], F32, kind="Internal").ap()
    rg = [list(range(NCORE))]
    cc_out = nc.dram_tensor(
        "cc_out", [NCORE, 128 * 16], F32, kind="Internal",
        addr_space=maybe_share_collective_output_space("AllGather", rg)).ap()

    AB_N = 75264
    AF_N = 11776

    import contextlib
    with contextlib.ExitStack() as _ctx:
        ec = _ctx.enter_context
        AB_t = ec(nc.sbuf_tensor("AB", [128, AB_N], BF16))
        AF_t = ec(nc.sbuf_tensor("AF", [128, AF_N], F32))
        PRM = ec(nc.sbuf_tensor("PRM", [128, 336], F32))
        PBt = [ec(nc.psum_tensor("PB%d" % i, [128, 512], F32)) for i in range(8)]
        (PB0, PB1, PB2, PB3, PB4, PB5, PB6, PB7) = PBt
        sems = {n: ec(nc.semaphore(n))
                for n in ("L", "S", "WD", "MM", "VE", "SC", "GP", "CC", "PH", "CO")}
        (sem_L, sem_S, sem_WD, sem_MM, sem_VE, sem_SC, sem_GP, sem_CC,
         sem_PH, sem_CO) = (sems[n] for n in ("L", "S", "WD", "MM", "VE", "SC",
                                              "GP", "CC", "PH", "CO"))
        block = ec(nc.Block())
        p = Prog()
        p.sems = dict(L=sem_L, S=sem_S, WD=sem_WD, MM=sem_MM, VE=sem_VE,
                      SC=sem_SC, GP=sem_GP, CC=sem_CC, PH=sem_PH, CO=sem_CO)
        PB = [PB0, PB1, PB2, PB3, PB4, PB5, PB6, PB7]
        PBb = [b[:].bitcast(BF16) for b in PB]

        AB = Alo(AB_t[:], AB_N, "AB")
        AFa = Alo(AF_t[:], AF_N, "AF")

        lg_c = PRM[:, 0:8]; om_c = PRM[:, 8:16]
        del_c = PRM[:, 16:24]; rho_c = PRM[:, 24:32]; th_c = PRM[:, 32:40]
        bpsi_c = PRM[:, 40:56]; bphi_c = PRM[:, 56:72]; bgate_c = PRM[:, 72:80]
        gc_c = PRM[:, 80:88]; gs_c = PRM[:, 88:96]
        ec_c = PRM[:, 96:104]; es_c = PRM[:, 104:112]
        seam_re = PRM[:, 112:120]; seam_im = PRM[:, 120:128]
        scols = PRM[:, 128:144]
        sin_re = PRM[:, 144:152]; sin_im = PRM[:, 152:160]
        P_b = PRM[:, 160:192]
        mcol = PRM[:, 192:200]
        selR = PRM[:, 200:208]
        colA = PRM[:, 208:216]; colB = PRM[:, 216:224]
        icol = PRM[:, 224:232].bitcast(I32)
        dcol = PRM[:, 232:240]
        accR = PRM[:, 240:248]; accI = PRM[:, 248:256]
        fr_c = PRM[:, 256:264]; fr_s = PRM[:, 264:272]
        selI = PRM[:, 272:280]
        ms0 = PRM[:, 280:281]; ms1 = PRM[:, 281:282]
        ms2 = PRM[:, 282:283]
        SFAC = PRM[:, 284:285]; NB6 = PRM[:, 285:286]
        rowstg = PRM[0:2, 286:318]
        DT_COL = P_b[:, 2:3]; RT_COL = P_b[:, 3:4]
        rsA = PRM[:, 320:328]

        def SEL_COL(m, r):
            return P_b[:, 4 + m * 8 + r:5 + m * 8 + r]

        identb_s = AB.take(128)
        onesr_b = AB.take(128)
        ehot_s = AB.take(16)
        bsel_s = AB.take(768)
        ew_tm = AB.take(NSUB * 24)
        zre_s = AB.take(64); zim_s = AB.take(64)

        identf_s = AFa.take(128)
        ident6_s = AFa.take(6)
        anch_s = AFa.take(6)
        s_all = AFa.take(128)
        ones_rf = AFa.take(128)

        def dma(eng, sem, out, in_, waits=(), slow=False):
            if slow:
                return p.do(eng, lambda e, o=out, i=in_: e.dma_start(
                    out=o, in_=i, allow_slow_non_contiguous=True),
                    inc=sem, waits=waits)
            return p.do(eng, lambda e, o=out, i=in_: e.dma_start(out=o, in_=i),
                        inc=sem, waits=waits)

        def chunk(ap, c, w=SUB):
            return ap[:, c * w:(c + 1) * w]

        def rr_sin(dst, ang, shift, sc_extra=()):
            p.do("vector", lambda e, o=colA, i=ang, s=float(shift):
                 e.tensor_scalar(o, i, INV_2PI, s, ALU.mult, ALU.add), inc="VE",
                 waits=[("SC", p.cnt["SC"])])
            p.do("vector", lambda e, o=icol, i=colA: e.tensor_copy(o, i), inc="VE")
            p.do("vector", lambda e, o=colB, i=icol: e.tensor_copy(o, i), inc="VE")
            p.do("vector", lambda e, o=colA, a=colA, b=colB: e.tensor_sub(o, a, b),
                 inc="VE")
            p.do("vector", lambda e, o=rsA, i=colA: e.tensor_single_scalar(
                o, i, 0.5, ALU.is_gt), inc="VE")
            p.do("vector", lambda e, o=colB, a=colB, b=rsA: e.tensor_add(o, a, b),
                 inc="VE")
            p.do("vector", lambda e, o=rsA, i=colA: e.tensor_single_scalar(
                o, i, -0.5, ALU.is_lt), inc="VE")
            p.do("vector", lambda e, o=colB, a=colB, b=rsA: e.tensor_sub(o, a, b),
                 inc="VE")
            if float(shift) != 0.0:
                p.do("vector", lambda e, o=colB, i=colB, s=-float(shift):
                     e.tensor_scalar(o, i, s, None, ALU.add), inc="VE")
            p.do("vector", lambda e, o=colA, k=colB, a=ang:
                 e.scalar_tensor_tensor(o, k, -TP_HI, a, ALU.mult, ALU.add), inc="VE")
            p.do("vector", lambda e, o=colA, k=colB, a=colA:
                 e.scalar_tensor_tensor(o, k, -TP_LO, a, ALU.mult, ALU.add), inc="VE")
            p.do("vector", lambda e, o=colA: e.tensor_single_scalar(
                o, colA, PI_CL, ALU.min), inc="VE")
            ve = p.do("vector", lambda e, o=colA: e.tensor_single_scalar(
                o, colA, -PI_CL, ALU.max), inc="VE")
            return p.do("scalar", lambda e, o=dst, i=colA: e.activation(
                o, i, AF.Sin), inc="SC",
                waits=[("VE", ve)] + list(sc_extra))

        def cs_cols(dst_c, dst_s, pimm):
            p.do("vector", lambda e, o=dcol, s=float(pimm): e.tensor_scalar(
                o, th_c, s, None, ALU.mult), inc="VE")
            rr_sin(dst_c, dcol, 0.25)
            return rr_sin(dst_s, dcol, 0.0)

        AB.push()
        tabC = AB.take(NCH * SUB)
        tabS = AB.take(NCH * SUB)
        tabR = AB.take(NCH * SUB)
        AB.push()
        w_psi = AB.take(NCH * 2048)
        w_gate = AB.take(NCH * 1024)
        pw_hi = AB.take(NCH * 8)
        pw_lo = AB.take(NCH * 8)

        AFa.push()
        stage = AFa.take(4096)
        sc_a = AFa.take(512); sc_b = AFa.take(512)
        sc_i = AFa.take(512)
        tau_row = AFa.take(SUB)
        assert AFa.off <= AF_N

        wd = 0
        for (dst, src) in (
            (lg_c, dI["loggam"].rearrange("(c p) -> p c", p=128)),
            (om_c, dI["omega"].rearrange("(c p) -> p c", p=128)),
            (bpsi_c, dI["bpsi"].rearrange("(c p) -> p c", p=128)),
            (bphi_c, dI["bphi"].rearrange("(c p) -> p c", p=128)),
            (bgate_c, dI["bgate"].rearrange("(c p) -> p c", p=128)),
            (rowstg[0:1, 0:1], dI["dtv"].rearrange("(a b) -> a b", a=1)),
            (rowstg[0:1, 1:2], dI["rtemp"].rearrange("(a b) -> a b", a=1)),
            (rowstg[0:1, 4:28], dI["sel"]),
            (identf_s, dI["identf"]),
            (identb_s, dI["identb"]),
            (ident6_s[0:6, :], dI["ident6"]),
            (anch_s[0:6, :], dI["anchT"]),
            (onesr_b[0:1, :], dI["onesrow_b"]),
            (ehot_s, dI["ehot"]),
            (bsel_s[0:6, :], dI["bsel"]),
        ):
            wd = dma("scalar", "WD", dst, src, slow=True)

        sc = p.do("scalar", lambda e: e.activation(
            rowstg[0:1, 2:3], rowstg[0:1, 0:1], AF.Abs), inc="SC",
            waits=[("WD", wd)])
        ve = p.do("vector", lambda e: e.tensor_copy(
            rowstg[0:1, 3:4], rowstg[0:1, 1:2]), inc="VE", waits=[("SC", sc)])
        ve = p.do("vector", lambda e: e.memset(ones_rf[0:1, :], 1.0), inc="VE")
        mm = p.do("tensor", lambda e: e.matmul(
            PB7[:, 0:28], ones_rf[0:1, :], rowstg[0:1, 0:28],
            start=True, stop=True), inc="MM", waits=[("VE", ve)])
        ve = p.do("vector", lambda e: e.tensor_copy(P_b[:, 0:28], PB7[:, 0:28]), inc="VE",
                  waits=[("MM", mm)])

        sc = p.do("scalar", lambda e: e.activation(del_c, lg_c, AF.Exp), inc="SC",
                  waits=[("VE", ve)])
        ve = p.do("vector", lambda e: e.tensor_scalar(
            del_c, del_c, DT_COL, None, ALU.mult), inc="VE", waits=[("SC", sc)])
        sc = p.do("scalar", lambda e: e.activation(
            rho_c, del_c, AF.Exp, scale=np.float32(-1.0)), inc="SC", waits=[("VE", ve)])
        ve_th = p.do("vector", lambda e: e.tensor_scalar(
            th_c, om_c, DT_COL, None, ALU.mult), inc="VE")

        gp = p.do("gpsimd", lambda e: e.iota(
            tau_row, pattern=[[1, SUB]], base=0, channel_multiplier=0,
            allow_small_or_imprecise_dtypes=True), inc="GP")

        sc_iI = sc_i.bitcast(I32)
        sc_d = stage[:, 512:1024]

        def rr_sin_big(dst, ang, shift, waits=()):
            p.do("vector", lambda e, o=sc_a, i=ang, s=float(shift):
                 e.tensor_scalar(o, i, INV_2PI, s, ALU.mult, ALU.add), inc="VE",
                 waits=list(waits) + [("SC", p.cnt["SC"])])
            p.do("vector", lambda e, o=sc_iI, i=sc_a: e.tensor_copy(o, i), inc="VE")
            p.do("vector", lambda e, o=sc_b, i=sc_iI: e.tensor_copy(o, i), inc="VE")
            p.do("vector", lambda e, o=sc_a, a=sc_a, b=sc_b: e.tensor_sub(o, a, b),
                 inc="VE")
            p.do("vector", lambda e, o=sc_d, i=sc_a: e.tensor_single_scalar(
                o, i, 0.5, ALU.is_gt), inc="VE")
            p.do("vector", lambda e, o=sc_b, a=sc_b, b=sc_d: e.tensor_add(o, a, b),
                 inc="VE")
            p.do("vector", lambda e, o=sc_d, i=sc_a: e.tensor_single_scalar(
                o, i, -0.5, ALU.is_lt), inc="VE")
            p.do("vector", lambda e, o=sc_b, a=sc_b, b=sc_d: e.tensor_sub(o, a, b),
                 inc="VE")
            if float(shift) != 0.0:
                p.do("vector", lambda e, o=sc_b, i=sc_b, s=-float(shift):
                     e.tensor_scalar(o, i, s, None, ALU.add), inc="VE")
            p.do("vector", lambda e, o=sc_a, k=sc_b, a=ang:
                 e.scalar_tensor_tensor(o, k, -TP_HI, a, ALU.mult, ALU.add), inc="VE")
            p.do("vector", lambda e, o=sc_a, k=sc_b, a=sc_a:
                 e.scalar_tensor_tensor(o, k, -TP_LO, a, ALU.mult, ALU.add), inc="VE")
            p.do("vector", lambda e, o=sc_a: e.tensor_single_scalar(
                o, sc_a, PI_CL, ALU.min), inc="VE")
            ve2 = p.do("vector", lambda e, o=sc_a: e.tensor_single_scalar(
                o, sc_a, -PI_CL, ALU.max), inc="VE")
            return p.do("scalar", lambda e, o=dst, i=sc_a: e.activation(
                o, i, AF.Sin), inc="SC",
                waits=[("VE", ve2)])

        sc_tab = 0
        ang = stage[:, 0:SUB]
        for cidx in range(NCH):
            ve = p.do("vector", lambda e, o=ang, c=cidx: e.tensor_scalar(
                o, tau_row, th_c[:, c:c + 1], None, ALU.mult), inc="VE",
                waits=[("GP", gp), ("VE", ve_th), ("SC", p.cnt["SC"])])
            rr_sin_big(chunk(tabC, cidx), ang, 0.25)
            sc_tab = rr_sin_big(chunk(tabS, cidx), ang, 0.0)
            ve = p.do("vector", lambda e, o=ang, c=cidx: e.tensor_scalar(
                o, tau_row, del_c[:, c:c + 1], None, ALU.mult), inc="VE",
                waits=[("SC", p.cnt["SC"])])
            sc_tab = p.do("scalar", lambda e, o=chunk(tabR, cidx), i=ang:
                          e.activation(o, i, AF.Exp, scale=np.float32(-1.0)), inc="SC",
                          waits=[("VE", ve)])

        cs_cols(gc_c, gs_c, SUB)
        cs_cols(ec_c, es_c, SUB - 1)

        p.do("vector", lambda e: e.memset(seam_re, 0.0), inc="VE")
        p.do("vector", lambda e: e.memset(seam_im, 0.0), inc="VE")

        last_cast = ("VE", p.cnt["VE"])
        p.w("scalar", "SC", sc_tab)
        for k in range(NCH):
            src = dI["wpsiT"][k * 128:(k + 1) * 128, :]
            wdv = dma("scalar", "WD", stage[:, 0:2048], src, waits=[last_cast])
            cv = p.do("vector", lambda e, o=w_psi[:, k * 2048:(k + 1) * 2048],
                      i=stage[:, 0:2048]: e.tensor_copy(o, i), inc="VE",
                      waits=[("WD", wdv)])
            last_cast = ("VE", cv)
        for k in range(NCH):
            src = dI["wgateT"][k * 128:(k + 1) * 128, :]
            wdv = dma("scalar", "WD", stage[:, 0:1024], src, waits=[last_cast])
            cv = p.do("vector", lambda e, o=w_gate[:, k * 1024:(k + 1) * 1024],
                      i=stage[:, 0:1024]: e.tensor_copy(o, i), inc="VE",
                      waits=[("WD", wdv)])
            last_cast = ("VE", cv)
        for k in range(NCH):
            wdv = dma("scalar", "WD", stage[:, k * NE:(k + 1) * NE],
                      dI["projwT"][k * 128:(k + 1) * 128, :],
                      waits=[last_cast] if k == 0 else ())
        for k in range(NCH):
            sl = stage[:, k * NE:(k + 1) * NE]
            hi = pw_hi[:, k * 8:k * 8 + NE]
            lo = pw_lo[:, k * 8:k * 8 + NE]
            p.do("vector", lambda e, o=hi, i=sl: e.tensor_copy(o, i), inc="VE",
                 waits=[("WD", wdv)])
            p.do("vector", lambda e, o=lo, a=sl, b=hi: e.tensor_sub(o, a, b),
                 inc="VE")

        AFa.pop()
        p.w("sync", "S", p.cnt["S"]); p.w("sync", "L", p.cnt["L"])
        p.barrier()

        AFa.push()
        x_tm = [AFa.take(4096), AFa.take(4096)]
        sq_f32 = AFa.take(1024)
        rtile = AFa.take(512)
        lb_tm = AFa.take(24)
        mlm_f = AFa.take(24)
        m14 = AFa.take(16)
        ln_red = AFa.take(8)
        assert AFa.off <= AF_N

        AB.push()
        xn_bf = AB.take(NCH * SUB)
        xn_lo = AB.take(NCH * SUB)
        g_bf = AB.take(NCH * SUB)
        u_re = AB.take(NCH * SUB)
        u_im = AB.take(NCH * SUB)
        w_re_t = AB.take(NCH * SUB)
        w_im_t = AB.take(NCH * SUB)
        t14 = AB.take(4 * SUB)
        m24 = AB.take(48)
        stg32 = [AB.take(2048).bitcast(F32), AB.take(2048).bitcast(F32)]
        stg16 = [AB.take(1024), AB.take(1024)]
        assert AB.off <= AB_N, AB.off
        jt_re = u_re
        jt_im = u_im

        last_S_xn = 0
        last_S_j = 0
        sq_free = ("VE", 0)
        x_free = [("VE", 0), ("VE", 0)]
        psi_free = [("VE", 0), ("VE", 0)]
        gate_free = [("SC", 0), ("SC", 0)]
        tr_free = [("VE", 0), ("VE", 0)]
        rtr_free = ("VE", 0)
        pb6_free = ("SC", 0)
        rot_gp_prev = 0
        u_prev_ve = 0
        router_mm_prev = 0
        scans_prev = 0
        ln_last_sc = 0
        sfac_ve = None

        wjobs = []
        for eidx in range(NE):
            for j in range(NCH // 2):
                wjobs.append((
                    dI["w1"][eidx, 2 * j * 128:(2 * j + 2) * 128, :]
                    .rearrange("(k p) d -> p k d", p=128),
                    w1sc[eidx, 2 * j * 128:(2 * j + 2) * 128, :]
                    .rearrange("(k p) d -> p k d", p=128), True))
            for hk in range(NKH):
                wjobs.append((dI["w2"][eidx, hk * 128:(hk + 1) * 128, :],
                              w2sc[eidx, hk * 128:(hk + 1) * 128, :], False))
        wst_free = [("VE", 0), ("VE", 0)]
        wst16_S = [0, 0]

        def emit_wjob(i):
            src, dst, pair = wjobs[i]
            par = i % 2
            s32 = stg32[par]
            s16 = stg16[par]
            if pair:
                s32 = s32.rearrange("p (k d) -> p k d", d=512)
                s16 = s16.rearrange("p (k d) -> p k d", d=512)
            wd = dma("scalar", "WD", s32, src, waits=[wst_free[par]])
            cv = p.do("vector", lambda e, o=stg16[par], i2=stg32[par]:
                      e.tensor_copy(o, i2), inc="VE",
                      waits=[("WD", wd), ("S", wst16_S[par])])
            wst_free[par] = ("VE", cv)
            wst16_S[par] = dma("sync", "S", dst, s16,
                               waits=[("VE", cv)])

        for st in range(NSUB):
            for i in range(st * 6, min(st * 6 + 6, len(wjobs))):
                emit_wjob(i)
            xb = x_tm[st % 2]
            p.w("sync", x_free[st % 2][0], x_free[st % 2][1])
            lv = dma("sync", "L", xb.rearrange("p (t d) -> p t d", d=1024),
                     dI["x"][st * SUB:(st + 1) * SUB, :]
                     .rearrange("(t p) d -> p t d", p=128))

            ln_sc = []
            for t in range(4):
                tile = xb[:, t * 1024:(t + 1) * 1024]
                rcol = ln_red[:, 2 * t:2 * t + 1]
                scol = ln_red[:, 2 * t + 1:2 * t + 2]
                ve = p.do("vector", lambda e, o=rcol, i=tile: e.reduce_sum(
                    o, i, axis=mybir.AxisListType.X), inc="VE", waits=[("L", lv)])
                sc = p.do("scalar", lambda e, o=sq_f32, i=tile:
                          e.activation(o, i, AF.Square), inc="SC",
                          waits=[("L", lv), sq_free])
                ve_s2 = p.do("vector", lambda e, o=scol, i=sq_f32: e.reduce_sum(
                    o, i, axis=mybir.AxisListType.X), inc="VE",
                    waits=[("SC", sc)])
                sq_free = ("VE", ve_s2)
                mc = mcol[:, 2 * t:2 * t + 1]
                nmi = mcol[:, 2 * t + 1:2 * t + 2]
                ve = p.do("vector", lambda e, o=mc, i=rcol: e.tensor_scalar(
                    o, i, 1.0 / D, None, ALU.mult), inc="VE",
                    waits=[("SC", ln_last_sc)])
                ve = p.do("vector", lambda e, o=ms0, i=scol: e.tensor_scalar(
                    o, i, 1.0 / D, None, ALU.mult), inc="VE", waits=[("VE", ve_s2)])
                ve = p.do("vector", lambda e, o=ms1, a=mc, b=mc:
                          e.tensor_mul(o, a, b), inc="VE")
                ve = p.do("vector", lambda e, o=ms0, a=ms0, b=ms1:
                          e.tensor_sub(o, a, b), inc="VE")
                sc = p.do("scalar", lambda e, o=ms1, i=ms0:
                          e.activation(o, i, AF.Ln, bias=np.float32(1e-5)), inc="SC",
                          waits=[("VE", ve)])
                sc = p.do("scalar", lambda e, o=ms0, i=ms1:
                          e.activation(o, i, AF.Exp, scale=np.float32(-0.5)), inc="SC")
                ve = p.do("vector", lambda e, o=nmi, m=mc, i=ms0:
                          e.scalar_tensor_tensor(o, m, -1.0, i, ALU.mult, ALU.mult),
                          inc="VE", waits=[("SC", sc)])
                ve = p.do("vector", lambda e, o=mc, i=ms0: e.tensor_copy(o, i),
                          inc="VE")
                sc = p.do("scalar", lambda e, o=tile, i=tile, s=mc, b=nmi:
                          e.activation(o, i, AF.Identity, bias=b, scale=s),
                          inc="SC", waits=[("VE", ve)])
                ln_sc.append(sc)
            ln_last_sc = ln_sc[-1]

            for cidx in range(NCH):
                bank = PB[cidx % 2]
                mm = None
                for t in range(4):
                    wts = [("SC", ln_sc[t])]
                    if t == 0:
                        wts.append(tr_free[cidx % 2])
                    mm = p.do("tensor", lambda e, b=bank, t=t, c=cidx, xb=xb:
                              e.transpose(b[:, t * 128:(t + 1) * 128],
                                          xb[:, t * 1024 + c * 128:
                                             t * 1024 + (c + 1) * 128],
                                          identf_s), inc="MM", waits=wts)
                sc = p.do("scalar", lambda e, o=chunk(xn_bf, cidx), i=bank[:]:
                          e.activation(o, i, AF.Identity), inc="SC",
                          waits=[("MM", mm), ("S", last_S_xn),
                                 ("MM", router_mm_prev)])
                ve = p.do("vector", lambda e, o=chunk(xn_lo, cidx), a=bank[:],
                          b=chunk(xn_bf, cidx): e.tensor_sub(o, a, b), inc="VE",
                          waits=[("SC", sc), ("MM", router_mm_prev)])
                tr_free[cidx % 2] = ("VE", ve)
            xn_all_sc = p.cnt["SC"]
            xn_all_ve = p.cnt["VE"]

            sxn = dma("sync", "S", xn_sc[st], xn_bf, waits=[("SC", xn_all_sc)])
            last_S_xn = sxn

            g_sc = []
            for o in range(NCH):
                bank = PB[2 if o % 2 == 0 else 5]
                mm = None
                for k in range(NCH):
                    wts = []
                    if k == 0:
                        wts = [("SC", xn_all_sc), gate_free[o % 2]]
                    mm = p.do("tensor", lambda e, b=bank, o=o, k=k: e.matmul(
                        b[:], w_gate[:, k * 1024 + o * 128:k * 1024 + (o + 1) * 128],
                        chunk(xn_bf, k), start=(k == 0), stop=(k == NCH - 1)),
                        inc="MM", waits=wts)
                sc = p.do("scalar", lambda e, o2=chunk(g_bf, o), i=bank[:],
                          bc=bgate_c[:, o:o + 1]: e.activation(
                              o2, i, AF.Sigmoid, bias=bc), inc="SC",
                          waits=[("MM", mm), ("VE", u_prev_ve)])
                gate_free[o % 2] = ("SC", sc)
                g_sc.append(sc)

            for comp in range(2):
                udst = u_re if comp == 0 else u_im
                for cidx in range(NCH):
                    o = comp * NCH + cidx
                    bank = PB[3 + (o % 2)]
                    mm = None
                    for k in range(NCH):
                        wts = [psi_free[o % 2]] if k == 0 else []
                        mm = p.do("tensor", lambda e, b=bank, o=o, k=k: e.matmul(
                            b[:], w_psi[:, k * 2048 + o * 128:
                                        k * 2048 + (o + 1) * 128],
                            chunk(xn_bf, k), start=(k == 0), stop=(k == NCH - 1)),
                            inc="MM", waits=wts)
                    ve = p.do("vector", lambda e, o2=chunk(udst, cidx), i=bank[:],
                              bc=bpsi_c[:, o:o + 1], g=chunk(g_bf, cidx):
                              e.scalar_tensor_tensor(o2, i, bc, g, ALU.add,
                                                     ALU.mult),
                              inc="VE", waits=[("MM", mm), ("SC", g_sc[cidx]),
                                               ("GP", rot_gp_prev),
                                               ("S", last_S_j)])
                    psi_free[o % 2] = ("VE", ve)
            u_all_ve = p.cnt["VE"]
            u_prev_ve = u_all_ve

            t1 = t14[:, 0:SUB]; t3 = t14[:, 2 * SUB:3 * SUB]
            for cidx in range(NCH):
                cre = chunk(u_re, cidx); cim = chunk(u_im, cidx)
                ctab = chunk(tabC, cidx); stab = chunk(tabS, cidx)
                uw = [("VE", u_all_ve)] if cidx == 0 else []
                p.do("vector", lambda e, o=chunk(w_re_t, cidx), a=cre, b=ctab:
                     e.tensor_mul(o, a, b), inc="VE", waits=uw)
                p.do("vector", lambda e, o=t1, a=cim, b=stab: e.tensor_mul(o, a, b),
                     inc="VE")
                ve_wr = p.do("vector", lambda e, o=chunk(w_re_t, cidx), a=t1:
                             e.tensor_add(o, o, a), inc="VE")
                gw = [("VE", u_all_ve), ("VE", scans_prev)] if cidx == 0 else []
                p.do("gpsimd", lambda e, o=chunk(w_im_t, cidx), a=cim, b=ctab:
                     e.tensor_mul(o, a, b), inc="GP", waits=gw)
                p.do("gpsimd", lambda e, o=t3, a=cre, b=stab: e.tensor_mul(o, a, b),
                     inc="GP")
                gp_wi = p.do("gpsimd", lambda e, o=chunk(w_im_t, cidx), a=t3:
                             e.tensor_sub(o, o, a), inc="GP")
                p.do("vector", lambda e, o=chunk(jt_re, cidx),
                     d0=rho_c[:, cidx:cidx + 1].broadcast_to([128, SUB]),
                     d1=chunk(w_re_t, cidx), ini=seam_re[:, cidx:cidx + 1]:
                     e.tensor_tensor_scan(o, d0, d1, ini, ALU.mult, ALU.add),
                     inc="VE", waits=[("GP", gp_wi)])
                p.do("vector", lambda e, o=chunk(jt_im, cidx),
                     d0=rho_c[:, cidx:cidx + 1].broadcast_to([128, SUB]),
                     d1=chunk(w_im_t, cidx), ini=seam_im[:, cidx:cidx + 1]:
                     e.tensor_tensor_scan(o, d0, d1, ini, ALU.mult, ALU.add),
                     inc="VE")
            rot_gp_prev = p.cnt["GP"]
            scans_ve = p.cnt["VE"]
            scans_prev = scans_ve

            jre_end = jt_re[:, SUB - 1::SUB]
            jim_end = jt_im[:, SUB - 1::SUB]
            p.do("vector", lambda e: e.tensor_mul(accR, gc_c, jre_end), inc="VE")
            p.do("vector", lambda e: e.tensor_mul(accI, gs_c, jim_end), inc="VE")
            p.do("vector", lambda e: e.tensor_sub(seam_re, accR, accI), inc="VE")
            p.do("vector", lambda e: e.tensor_mul(accR, gc_c, jim_end), inc="VE")
            p.do("vector", lambda e: e.tensor_mul(accI, gs_c, jre_end), inc="VE")
            seam_ve = p.do("vector", lambda e: e.tensor_add(seam_im, accR, accI),
                           inc="VE")
            if st == NSUB - 1:
                p.do("vector", lambda e: e.tensor_mul(accR, ec_c, jre_end), inc="VE")
                p.do("vector", lambda e: e.tensor_mul(accI, es_c, jim_end), inc="VE")
                p.do("vector", lambda e: e.tensor_sub(scols[:, 0:8], accR, accI),
                     inc="VE")
                p.do("vector", lambda e: e.tensor_mul(accR, ec_c, jim_end), inc="VE")
                p.do("vector", lambda e: e.tensor_mul(accI, es_c, jre_end), inc="VE")
                seam_ve = p.do("vector", lambda e: e.tensor_add(
                    scols[:, 8:16], accR, accI), inc="VE")

            sj = dma("sync", "S", j_re[st], jt_re, waits=[("VE", scans_ve)])
            sj = dma("sync", "S", j_im[st], jt_im, waits=[("VE", seam_ve)])
            last_S_j = sj

            pairs = []
            for (wsta, mov) in ((pw_hi, xn_bf), (pw_hi, xn_lo), (pw_lo, xn_bf)):
                for k in range(NCH):
                    pairs.append((wsta[:, k * 8:k * 8 + NE], chunk(mov, k)))
            mm = None
            for i, (wst, mv) in enumerate(pairs):
                wts = [("VE", xn_all_ve), pb6_free] if i == 0 else []
                mm = p.do("tensor", lambda e, w=wst, m=mv, f=(i == 0),
                          l=(i == len(pairs) - 1): e.matmul(
                              PB6[0:NE, :], w, m, start=f, stop=l),
                          inc="MM", waits=wts)
            sc = p.do("scalar", lambda e: e.activation(
                rtile[0:NE, :], PB6[0:NE, :], AF.Tanh,
                scale=np.float32(1.0) / np.float32(0.3)), inc="SC",
                waits=[("MM", mm)])
            mm = p.do("tensor", lambda e: e.matmul(
                PB6[0:NE, :], anch_s[0:6, 0:6], rtile[0:NE, :],
                start=True, stop=True), inc="MM", waits=[("SC", sc)])
            if sfac_ve is None:
                ve = p.do("vector", lambda e: e.tensor_scalar(
                    ms2, RT_COL, 0.1, 2.0, ALU.max, ALU.mult), inc="VE")
                ve = p.do("vector", lambda e: e.reciprocal(SFAC, ms2), inc="VE")
                sfac_ve = p.do("vector", lambda e: e.tensor_scalar(
                    NB6, SFAC, -6.0, None, ALU.mult), inc="VE")
            sc_lg = p.do("scalar", lambda e: e.activation(
                rtile[0:NE, :], PB6[0:NE, :], AF.Identity,
                bias=NB6[0:NE, :], scale=SFAC[0:NE, :]), inc="SC",
                waits=[("MM", mm), ("VE", sfac_ve)])
            pb6_free = ("SC", sc_lg)
            router_mm_prev = p.cnt["MM"]
            mm = None
            for t in range(4):
                wts = [("SC", sc_lg)]
                if t == 0:
                    wts.append(rtr_free)
                mm = p.do("tensor", lambda e, t=t: e.transpose(
                    PB7[:, t * 6:(t + 1) * 6],
                    rtile[0:NE, t * 128:(t + 1) * 128], ident6_s[0:6, 0:6]),
                    inc="MM", waits=wts)
            ve = p.do("vector", lambda e: e.tensor_copy(lb_tm, PB7[:, 0:24]),
                      inc="VE", waits=[("MM", mm)])
            rtr_free = ("VE", ve)
            mask1 = m24[:, 0:24]; mask2 = m24[:, 24:48]; mlm = mlm_f
            m1p = m14[:, 0:4]; m2p = m14[:, 4:8]
            wa = m14[:, 8:12]; wb = m14[:, 12:16]

            def g6(a):
                return a.rearrange("p (g e) -> p g e", e=6)
            p.do("vector", lambda e: e.reduce_max(
                m1p.unsqueeze(2), g6(lb_tm), axis=mybir.AxisListType.X), inc="VE")
            p.do("vector", lambda e: e.tensor_tensor(
                g6(mask1), g6(lb_tm),
                m1p.unsqueeze(2).broadcast_to([128, 4, 6]), ALU.is_equal),
                inc="VE")
            p.do("vector", lambda e: e.scalar_tensor_tensor(
                mlm, mask1, -1e30, lb_tm, ALU.mult, ALU.add), inc="VE")
            p.do("vector", lambda e: e.reduce_max(
                m2p.unsqueeze(2), g6(mlm), axis=mybir.AxisListType.X), inc="VE")
            p.do("vector", lambda e: e.tensor_tensor(
                g6(mask2), g6(mlm),
                m2p.unsqueeze(2).broadcast_to([128, 4, 6]), ALU.is_equal),
                inc="VE")
            ve = p.do("vector", lambda e: e.tensor_sub(m1p, m1p, m2p), inc="VE")
            sc = p.do("scalar", lambda e: e.activation(wa, m1p, AF.Sigmoid),
                      inc="SC", waits=[("VE", ve)])
            p.do("vector", lambda e: e.tensor_scalar(
                wb, wa, -1.0, 1.0, ALU.mult, ALU.add), inc="VE",
                waits=[("SC", sc)])
            p.do("vector", lambda e: e.tensor_tensor(
                g6(mask1), g6(mask1),
                wa.unsqueeze(2).broadcast_to([128, 4, 6]), ALU.mult), inc="VE")
            p.do("vector", lambda e: e.tensor_tensor(
                g6(mask2), g6(mask2),
                wb.unsqueeze(2).broadcast_to([128, 4, 6]), ALU.mult), inc="VE")
            p.do("vector", lambda e, o=ew_tm[:, st * 24:(st + 1) * 24]:
                 e.tensor_add(o, mask1, mask2), inc="VE")

            x_free[st % 2] = ("MM", p.cnt["MM"])

        AFa.pop()
        AB.pop()
        AB.pop()
        AB.push()
        p.w("sync", "S", p.cnt["S"]); p.w("sync", "L", p.cnt["L"])
        p.barrier()

        AFa.push()
        stage2 = AFa.take(2048)
        x2 = AFa.take(4096)
        attn_acc = AFa.take(4096)
        stat4 = stage2[:, 0:512]
        rowf = stage2[:, 512:1024]
        rowg = stage2[:, 1024:1536]
        assert AFa.off <= AF_N

        w_phi = AB.take(NCH * 2048)
        hJ_re = AB.take(NCH * SUB)
        hJ_im = AB.take(NCH * SUB)
        xn2 = AB.take(NCH * SUB)
        phR = AB.take(NCH * SUB)
        phI = AB.take(NCH * SUB)
        hsq_b = phI
        he_all = AB.take(2 * NKH * SUB)
        he_b = [he_all[:, 0:NKH * SUB], he_all[:, NKH * SUB:2 * NKH * SUB]]
        w1b = [AB.take(NCH * DH), AB.take(NCH * DH)]
        w2b = [AB.take(NKH * 1024), AB.take(NKH * 1024)]
        rowsb = AB.take(4 * SUB)
        ewrow = AB.take(SUB)
        ta = AB.take(SUB)
        tb = AB.take(SUB)
        assert AB.off <= AB_N, AB.off
        mb_re = rowsb[:, 0:SUB]; ib_re = rowsb[:, SUB:2 * SUB]
        mb_im = rowsb[:, 2 * SUB:3 * SUB]; ib_im = rowsb[:, 3 * SUB:4 * SUB]
        rowm_b = ta
        rowi_b = tb

        half_free = [("VE", 0), ("VE", 0)]
        for k in range(NCH):
            for hh in range(2):
                par = hh
                sl = stage2[:, par * 1024:(par + 1) * 1024]
                wdv = dma("scalar", "WD", sl,
                          dI["wphiT"][k * 128:(k + 1) * 128,
                                      hh * 1024:(hh + 1) * 1024],
                          waits=[half_free[par]])
                cv = p.do("vector", lambda e,
                          o=w_phi[:, k * 2048 + hh * 1024:
                                  k * 2048 + (hh + 1) * 1024],
                          i=sl: e.tensor_copy(o, i), inc="VE",
                          waits=[("WD", wdv)])
                half_free[par] = ("VE", cv)
        wphi_ready = max(half_free[0][1], half_free[1][1])

        lv = dma("sync", "L", hJ_re, j_re[0])
        lv = dma("sync", "L", hJ_im, j_im[0])
        j_ready = lv
        xn2_ready = dma("sync", "L", xn2, xn_sc[0])
        x_rdy = dma("sync", "L", x2.rearrange("p (t d) -> p t d", d=1024),
                    dI["x"][0:SUB, :].rearrange("(t p) d -> p t d", p=128))

        w1_rdy = {}
        w2_rdy = {}

        def load_w1(g, wait):
            if g >= NSUB * NE:
                return
            w1_rdy[g] = dma("scalar", "WD",
                            w1b[g % 2].rearrange("p (k d) -> p k d", d=DH),
                            w1sc[g % NE].rearrange("(k p) d -> p k d", p=128),
                            waits=[wait])

        def load_w2(g, wait):
            if g >= NSUB * NE:
                return
            w2_rdy[g] = dma("scalar", "WD",
                            w2b[g % 2].rearrange("p (k d) -> p k d", d=1024),
                            w2sc[g % NE].rearrange("(k p) d -> p k d", p=128),
                            waits=[wait])

        load_w1(0, None)
        load_w1(1, None)
        load_w2(0, None)
        load_w2(1, None)

        cc = p.do("gpsimd", lambda e: e.dma_start(
            out=cc_in.rearrange("(p f) -> p f", p=128), in_=scols), inc="CC")
        co = p.do("gpsimd", lambda e: e.collective_compute(
            "AllGather", ALU.bypass, replica_groups=rg,
            ins=[cc_in], outs=[cc_out]), inc=("CO", 1), waits=[("CC", cc)],
            attach=True)
        for r in range(NCORE):
            cc = p.do("gpsimd", lambda e, r=r: e.dma_start(
                out=s_all[:, r * 16:(r + 1) * 16],
                in_=cc_out[r].rearrange("(p f) -> p f", p=128)),
                inc="CC", waits=[("CO", co)])
        CC_ALL = cc

        p.do("vector", lambda e: e.memset(sin_re, 0.0), inc="VE",
             waits=[("CC", CC_ALL)])
        p.do("vector", lambda e: e.memset(sin_im, 0.0), inc="VE")
        for m in range(3):
            p.do("vector", lambda e: e.memset(selR, 0.0), inc="VE")
            p.do("vector", lambda e: e.memset(selI, 0.0), inc="VE")
            for r in range(NCORE):
                p.do("vector", lambda e, m=m, r=r: e.scalar_tensor_tensor(
                    selR, s_all[:, r * 16:r * 16 + 8], SEL_COL(m, r), selR,
                    ALU.mult, ALU.add), inc="VE")
                p.do("vector", lambda e, m=m, r=r: e.scalar_tensor_tensor(
                    selI, s_all[:, r * 16 + 8:r * 16 + 16], SEL_COL(m, r), selI,
                    ALU.mult, ALU.add), inc="VE")
            if m == 0:
                p.do("vector", lambda e: e.tensor_add(sin_re, sin_re, selR),
                     inc="VE")
                p.do("vector", lambda e: e.tensor_add(sin_im, sin_im, selI),
                     inc="VE")
            else:
                sc = cs_cols(fr_c, fr_s, C * m)
                ve = p.do("vector", lambda e, pi=float(C * m): e.tensor_scalar(
                    accR, del_c, pi, None, ALU.mult), inc="VE",
                    waits=[("SC", sc)])
                sc = p.do("scalar", lambda e: e.activation(
                    accR, accR, AF.Exp, scale=np.float32(-1.0)), inc="SC",
                    waits=[("VE", ve)])
                p.do("vector", lambda e: e.tensor_mul(fr_c, fr_c, accR), inc="VE",
                     waits=[("SC", sc)])
                p.do("vector", lambda e: e.tensor_mul(fr_s, fr_s, accR), inc="VE")
                p.do("vector", lambda e: e.tensor_mul(accR, fr_c, selR), inc="VE")
                p.do("vector", lambda e: e.tensor_mul(accI, fr_s, selI), inc="VE")
                p.do("vector", lambda e: e.tensor_sub(accR, accR, accI), inc="VE")
                p.do("vector", lambda e: e.tensor_add(sin_re, sin_re, accR),
                     inc="VE")
                p.do("vector", lambda e: e.tensor_mul(accR, fr_c, selI), inc="VE")
                p.do("vector", lambda e: e.tensor_mul(accI, fr_s, selR), inc="VE")
                p.do("vector", lambda e: e.tensor_add(accR, accR, accI), inc="VE")
                p.do("vector", lambda e: e.tensor_add(sin_im, sin_im, accR),
                     inc="VE")

        def emit_Z(b):
            if b >= NSUB:
                return
            pimm = b * SUB + 1
            sc = cs_cols(fr_c, fr_s, pimm)
            ve = p.do("vector", lambda e, pi=float(pimm): e.tensor_scalar(
                accR, del_c, pi, None, ALU.mult), inc="VE", waits=[("SC", sc)])
            sc = p.do("scalar", lambda e: e.activation(
                accR, accR, AF.Exp, scale=np.float32(-1.0)), inc="SC",
                waits=[("VE", ve)])
            p.do("vector", lambda e: e.tensor_mul(fr_c, fr_c, accR), inc="VE",
                 waits=[("SC", sc)])
            p.do("vector", lambda e: e.tensor_mul(fr_s, fr_s, accR), inc="VE")
            zr = zre_s[:, b * 8:(b + 1) * 8]
            zi = zim_s[:, b * 8:(b + 1) * 8]
            p.do("vector", lambda e: e.tensor_mul(accR, fr_c, sin_re), inc="VE")
            p.do("vector", lambda e: e.tensor_mul(accI, fr_s, sin_im), inc="VE")
            p.do("vector", lambda e, o=zr: e.tensor_sub(o, accR, accI),
                 inc="VE")
            p.do("vector", lambda e: e.tensor_mul(accR, fr_c, sin_im), inc="VE")
            p.do("vector", lambda e: e.tensor_mul(accI, fr_s, sin_re), inc="VE")
            p.do("vector", lambda e, o=zi: e.tensor_add(o, accR, accI),
                 inc="VE")

        emit_Z(0)


        pb01_free = [("VE", 0), ("VE", 0)]
        pb23_free = [("VE", 0), ("VE", 0)]
        pb4_free = ("VE", 0)
        pb56_free = [("VE", 0), ("VE", 0)]
        pb7_free = ("VE", 0)
        he_free = [("MM", 0), ("MM", 0)]
        attn_ve_prev = 0
        attn_gp_prev = 0
        res_mm_prev = 0
        ewb_mm_prev = 0
        rowcast_prev = 0
        stats2_last = 0
        h_mm_last = 0

        for st in range(NSUB):
            first = True
            for cidx in range(NCH):
                hr = chunk(hJ_re, cidx); hi = chunk(hJ_im, cidx)
                rtab = chunk(tabR, cidx)
                zr = zre_s[:, st * 8 + cidx:st * 8 + cidx + 1]
                zi = zim_s[:, st * 8 + cidx:st * 8 + cidx + 1]
                wts = [("L", j_ready), ("MM", ewb_mm_prev)] if first else []
                p.do("vector", lambda e, i=zr: e.tensor_copy(ms0, i), inc="VE",
                     waits=wts)
                p.do("vector", lambda e, i=zi: e.tensor_copy(ms1, i), inc="VE")
                p.do("vector", lambda e, o=hr, t=rtab:
                     e.scalar_tensor_tensor(o, t, ms0, o, ALU.mult, ALU.add),
                     inc="VE")
                p.do("vector", lambda e, o=hi, t=rtab:
                     e.scalar_tensor_tensor(o, t, ms1, o, ALU.mult, ALU.add),
                     inc="VE")
                first = False
            p.do("vector", lambda e: e.tensor_mul(phR, hJ_re, tabC), inc="VE")
            p.do("vector", lambda e: e.tensor_mul(he_all, hJ_im, tabS),
                 inc="VE")
            p.do("vector", lambda e: e.tensor_sub(phR, phR, he_all), inc="VE")
            p.do("vector", lambda e: e.tensor_mul(hJ_im, hJ_im, tabC),
                 inc="VE")
            p.do("vector", lambda e: e.tensor_mul(he_all, hJ_re, tabS),
                 inc="VE")
            p.do("vector", lambda e: e.tensor_add(hJ_im, hJ_im, he_all),
                 inc="VE")
            p.do("vector", lambda e: e.tensor_copy(hJ_re, phR), inc="VE")
            p.do("vector", lambda e: e.tensor_mul(phI, hJ_re, hJ_re),
                 inc="VE")
            p.do("vector", lambda e: e.tensor_mul(he_all, hJ_im, hJ_im),
                 inc="VE")
            rot_ve = p.cnt["VE"]

            for o in range(NCH):
                b = o % 2
                bank = PB[b]
                mm = None
                for k in range(NCH):
                    wts = []
                    if k == 0:
                        wts = [("L", xn2_ready), ("VE", wphi_ready),
                               pb01_free[b]]
                    mm = p.do("tensor", lambda e, bk=bank, o=o, k=k: e.matmul(
                        bk[:], w_phi[:, k * 2048 + o * 128:
                                     k * 2048 + (o + 1) * 128],
                        chunk(xn2, k), start=(k == 0), stop=(k == NCH - 1)),
                        inc="MM", waits=wts)
                ve = p.do("vector", lambda e, o2=chunk(phR, o), i=bank[:],
                          bc=bphi_c[:, o:o + 1]: e.tensor_scalar(
                              o2, i, bc, None, ALU.add), inc="VE",
                          waits=[("MM", mm)])
                pb01_free[b] = ("VE", ve)

            mm_stats = None
            for cidx in range(NCH):
                wts = []
                if cidx == 0:
                    wts = [("VE", rot_ve), pb23_free[0], pb7_free]
                l = (cidx == NCH - 1)
                mm_stats = p.do("tensor", lambda e, f=(cidx == 0),
                                m=chunk(hJ_re, cidx): e.matmul(
                                    PB2[0:2, :], ehot_s[:, 0:2], m,
                                    start=f, stop=False), inc="MM", waits=wts)
                mm_stats = p.do("tensor", lambda e, f=(cidx == 0),
                                m=chunk(phI, cidx): e.matmul(
                                    PB7[0:2, :], ehot_s[:, 0:2], m,
                                    start=f, stop=False), inc="MM")
                mm_stats = p.do("tensor", lambda e, m=chunk(hJ_im, cidx),
                                l=l: e.matmul(
                                    PB2[0:2, :], ehot_s[:, 4:6], m,
                                    start=False, stop=l), inc="MM")
                mm_stats = p.do("tensor", lambda e, m=chunk(he_all, cidx),
                                l=l: e.matmul(
                                    PB7[0:2, :], ehot_s[:, 4:6], m,
                                    start=False, stop=l), inc="MM")
            stats2_mm = mm_stats
            stats2_last = stats2_mm

            phI_ve = {}
            for o in range(NCH):
                b = o % 2
                bank = PB[b]
                mm = None
                for k in range(NCH):
                    wts = [pb01_free[b]] if k == 0 else []
                    mm = p.do("tensor", lambda e, bk=bank, o=o, k=k: e.matmul(
                        bk[:], w_phi[:, k * 2048 + (NCH + o) * 128:
                                     k * 2048 + (NCH + o + 1) * 128],
                        chunk(xn2, k), start=(k == 0), stop=(k == NCH - 1)),
                        inc="MM", waits=wts)
                ve = p.do("vector", lambda e, o2=chunk(phI, o), i=bank[:],
                          bc=bphi_c[:, NCH + o:NCH + o + 1]: e.tensor_scalar(
                              o2, i, bc, None, ALU.add), inc="VE",
                          waits=[("MM", mm), ("MM", stats2_mm)])
                pb01_free[b] = ("VE", ve)
                phI_ve[o] = ve

            sc = p.do("scalar", lambda e: e.activation(
                stat4[0:2, :], PB2[0:2, :], AF.Identity,
                scale=np.float32(1.0 / D)), inc="SC",
                waits=[("MM", stats2_mm), ("VE", rowcast_prev)])
            sc = p.do("scalar", lambda e: e.activation(
                rowf[0:2, :], PB7[0:2, :], AF.Identity,
                scale=np.float32(1.0 / D)), inc="SC")
            ve = p.do("vector", lambda e: e.tensor_mul(
                rowg[0:2, :], stat4[0:2, :], stat4[0:2, :]), inc="VE",
                waits=[("SC", sc)])
            ve = p.do("vector", lambda e: e.tensor_sub(
                rowf[0:2, :], rowf[0:2, :], rowg[0:2, :]), inc="VE")
            ve = p.do("vector", lambda e: e.tensor_single_scalar(
                rowf[0:2, :], rowf[0:2, :], 1e-30, ALU.max), inc="VE")
            sc = p.do("scalar", lambda e: e.activation(
                rowg[0:2, :], rowf[0:2, :], AF.Ln), inc="SC",
                waits=[("VE", ve)])
            sc = p.do("scalar", lambda e: e.activation(
                rowf[0:2, :], rowg[0:2, :], AF.Exp, scale=np.float32(0.5)),
                inc="SC")
            rows_sc = sc
            ve = p.do("vector", lambda e: e.tensor_scalar(
                rowf[0:2, :], rowf[0:2, :], 1e-6, None, ALU.add), inc="VE",
                waits=[("SC", sc)])
            ve = p.do("vector", lambda e: e.reciprocal(
                rowg[0:2, :], rowf[0:2, :]), inc="VE")
            ve = p.do("vector", lambda e: e.tensor_copy(
                rowm_b[0:2, :], stat4[0:2, :]), inc="VE")
            ve = p.do("vector", lambda e: e.tensor_copy(
                rowi_b[0:2, :], rowg[0:2, :]), inc="VE")
            rowcast = ve
            rowcast_prev = rowcast

            mmprev = {4: None, 7: None}
            mb_mm = None
            for src, j, dstrow, bank, bid in (
                    (rowm_b, 0, mb_re, PB4, 4), (rowm_b, 1, mb_im, PB7, 7),
                    (rowi_b, 0, ib_re, PB4, 4), (rowi_b, 1, ib_im, PB7, 7)):
                wts = [("VE", rowcast)]
                if mmprev[bid] is None:
                    wts.append(pb4_free if bid == 4 else ("SC", rows_sc))
                else:
                    wts.append(("VE", mmprev[bid]))
                mb_mm = p.do("tensor", lambda e, bk=bank, s=src, j=j: e.matmul(
                    bk[:], bsel_s[0:2, j * 128:(j + 1) * 128], s[0:2, :],
                    start=True, stop=True), inc="MM", waits=wts)
                cv = p.do("vector", lambda e, o=dstrow, bk=bank:
                          e.tensor_copy(o, bk[:]), inc="VE",
                          waits=[("MM", mb_mm)])
                mmprev[bid] = cv
            mb_mm_last = mb_mm
            pb4_free = ("VE", mmprev[4])

            mm = None
            for t in range(4):
                wts = [("VE", mmprev[7])] if t == 0 else []
                mm = p.do("tensor", lambda e, t=t,
                          i=ew_tm[:, st * 24 + t * 6:st * 24 + (t + 1) * 6]:
                          e.transpose(PBb[7][0:6, t * 128:(t + 1) * 128], i,
                                      identb_s), inc="MM", waits=wts)
            ewrow_ve = p.do("vector", lambda e: e.tensor_copy(
                ewrow[0:6, :], PBb[7][0:6, 0:512]), inc="VE",
                waits=[("MM", mm), ("MM", ewb_mm_prev)])
            pb7_free = ("VE", ewrow_ve)

            for cidx in range(NCH):
                hr = chunk(hJ_re, cidx); hi = chunk(hJ_im, cidx)
                wts = []
                if cidx == 0:
                    wts = [("MM", mb_mm_last), ("MM", res_mm_prev)]
                p.do("vector", lambda e, a=hr: e.tensor_sub(ta, a, mb_re),
                     inc="VE", waits=wts)
                p.do("vector", lambda e: e.tensor_mul(ta, ta, ib_re), inc="VE")
                p.do("vector", lambda e, o=chunk(attn_acc, cidx),
                     b=chunk(phR, cidx): e.tensor_mul(o, ta, b), inc="VE")
                p.do("vector", lambda e, a=hi: e.tensor_sub(tb, a, mb_im),
                     inc="VE")
                p.do("vector", lambda e: e.tensor_mul(tb, tb, ib_im), inc="VE")
                p.do("vector", lambda e, b=chunk(phI, cidx):
                     e.tensor_mul(tb, tb, b), inc="VE")
                p.do("vector", lambda e, o=chunk(attn_acc, cidx):
                     e.tensor_add(o, o, tb), inc="VE")
            attn_ve_prev = p.cnt["VE"]

            for eidx in range(NE):
                g = st * NE + eidx
                par = g % 2
                wts = [("VE", ewrow_ve), pb4_free]
                mmb = p.do("tensor", lambda e, ei=eidx: e.matmul(
                    PB4[:], bsel_s[0:6, ei * 128:(ei + 1) * 128],
                    ewrow[0:6, :], start=True, stop=True), inc="MM", waits=wts)
                he = he_b[par]
                he_ve = None
                for hk in range(NKH):
                    b = hk % 2
                    bank = PB[b]
                    mm = None
                    for k in range(NCH):
                        wts = []
                        if k == 0:
                            wts = [("WD", w1_rdy[g]), ("L", xn2_ready),
                                   pb01_free[b]]
                        mm = p.do("tensor",
                                  lambda e, bk=bank, hk=hk, k=k, par=par:
                                  e.matmul(
                                      bk[:],
                                      w1b[par][:, k * DH + hk * 128:
                                               k * DH + (hk + 1) * 128],
                                      chunk(xn2, k), start=(k == 0),
                                      stop=(k == NCH - 1)),
                                  inc="MM", waits=wts)
                    sc = p.do("scalar", lambda e, hk=hk, he=he, bk=bank:
                              e.activation(he[:, hk * SUB:(hk + 1) * SUB],
                                           bk[:], AF.Gelu_apprx_tanh),
                              inc="SC", waits=[("MM", mm), he_free[par]])
                    he_ve = p.do("vector", lambda e, hk=hk, he=he:
                                 e.tensor_tensor(he[:, hk * SUB:(hk + 1) * SUB],
                                                 he[:, hk * SUB:(hk + 1) * SUB],
                                                 PB4[:], ALU.mult),
                                 inc="VE", waits=[("SC", sc), ("MM", mmb)])
                    pb01_free[b] = ("SC", sc)
                h_mm_last = p.cnt["MM"]
                load_w1(g + 2, ("MM", h_mm_last))
                pb4_free = ("VE", he_ve)

                for o in range(NCH):
                    b = o % 2
                    bank = PB[2 + b]
                    mm = None
                    for hk in range(NKH):
                        wts = []
                        if hk == 0:
                            wts = [("VE", he_ve), ("WD", w2_rdy[g]),
                                   pb23_free[b]]
                            if eidx == 0 and o < 2:
                                wts.append(("SC", rows_sc))
                        mm = p.do("tensor",
                                  lambda e, bk=bank, o=o, hk=hk, par=par,
                                  he=he: e.matmul(
                                      bk[:],
                                      w2b[par][:, hk * 1024 + o * 128:
                                               hk * 1024 + (o + 1) * 128],
                                      he[:, hk * SUB:(hk + 1) * SUB],
                                      start=(hk == 0), stop=(hk == NKH - 1)),
                                  inc="MM", waits=wts)
                    yv = p.do("vector", lambda e, o2=chunk(attn_acc, o),
                              bk=bank: e.tensor_add(o2, o2, bk[:]), inc="VE",
                              waits=[("MM", mm)])
                    pb23_free[b] = ("VE", yv)
                y_mm = p.cnt["MM"]
                load_w2(g + 2, ("MM", y_mm))
                he_free[par] = ("MM", y_mm)
            y_all_ve = p.cnt["VE"]
            ewb_mm_prev = p.cnt["MM"]

            for t in range(4):
                for half in range(2):
                    bank = PB[5 + half]
                    mm = None
                    for ci in range(4):
                        cidx = half * 4 + ci
                        wts = [("VE", y_all_ve), pb56_free[half]] \
                            if ci == 0 else []
                        mm = p.do("tensor",
                                  lambda e, bi=5 + half, ci=ci, c=cidx, t=t:
                                  e.transpose(
                                      PB[bi][:, ci * 128:(ci + 1) * 128],
                                      chunk(attn_acc, c)[:, t * 128:
                                                         (t + 1) * 128],
                                      identf_s), inc="MM", waits=wts)
                    ve = p.do("vector", lambda e, bi=5 + half, t=t, h=half:
                              e.tensor_add(
                                  x2[:, t * 1024 + h * 512:
                                     t * 1024 + (h + 1) * 512],
                                  x2[:, t * 1024 + h * 512:
                                     t * 1024 + (h + 1) * 512],
                                  PB[bi][:, 0:512]), inc="VE",
                              waits=[("MM", mm), ("L", x_rdy)])
                    pb56_free[half] = ("VE", ve)
            res_ve = p.cnt["VE"]
            res_mm_prev = p.cnt["MM"]

            if st + 1 < NSUB:
                lv = dma("sync", "L", hJ_re, j_re[st + 1],
                         waits=[("VE", attn_ve_prev),
                                ("MM", stats2_last)])
                lv = dma("sync", "L", hJ_im, j_im[st + 1])
                j_ready = lv
                xn2_ready = dma("sync", "L", xn2, xn_sc[st + 1],
                                waits=[("MM", h_mm_last)])

            sv = p.do("sync", lambda e, st=st: e.dma_start(
                out=out_d[st * SUB:(st + 1) * SUB, :]
                .rearrange("(t p) d -> p t d", p=128),
                in_=x2.rearrange("p (t d) -> p t d", d=1024)),
                inc="S", waits=[("VE", res_ve)])

            emit_Z(st + 1)
            if st + 1 < NSUB:
                x_rdy = dma("sync", "L",
                            x2.rearrange("p (t d) -> p t d", d=1024),
                            dI["x"][(st + 1) * SUB:(st + 2) * SUB, :]
                            .rearrange("(t p) d -> p t d", p=128),
                            waits=[("S", sv)])

        p.w("sync", "S", p.cnt["S"])
        p.w("sync", "L", p.cnt["L"])
        AFa.pop(); AB.pop()

        @block.sync
        def _(e):
            for f in p.q["sync"]:
                f(e)

        @block.tensor
        def _(e):
            for f in p.q["tensor"]:
                f(e)

        @block.vector
        def _(e):
            for f in p.q["vector"]:
                f(e)

        @block.scalar
        def _(e):
            for f in p.q["scalar"]:
                f(e)

        @block.gpsimd
        def _(e):
            for f in p.q["gpsimd"]:
                f(e)

    return nc


def _host_constants():
    ehot = np.zeros((128, 16), dtype=ml_dtypes.bfloat16)
    for j in range(4):
        ehot[:, 4 * j + j] = 1
    bsel = np.zeros((6, 768), dtype=ml_dtypes.bfloat16)
    for j in range(6):
        bsel[j, j * 128:(j + 1) * 128] = 1
    return dict(
        identf=np.eye(128, dtype=np.float32),
        identb=np.eye(128, dtype=ml_dtypes.bfloat16),
        ident6=np.eye(6, dtype=np.float32),
        onesrow_b=np.ones((1, 128), dtype=ml_dtypes.bfloat16),
        ehot=ehot,
        bsel=bsel,
    )


def kernel(L0_attn_x, L0_attn_omega, L0_attn_log_gamma, L0_attn_dt,
           L0_attn_W_psi, L0_attn_b_psi, L0_attn_W_phi, L0_attn_b_phi,
           L0_attn_W_gate, L0_attn_b_gate, L0_ffn_proj_w, L0_ffn_anchors,
           L0_ffn_routing_temp, L0_ffn_w1, L0_ffn_w2):
    if "nc" not in _CACHE:
        _CACHE["nc"] = build_program()
    nc = _CACHE["nc"]

    f32 = np.float32
    x = np.asarray(L0_attn_x, f32)
    shared = dict(
        wpsiT=np.ascontiguousarray(np.asarray(L0_attn_W_psi, f32).T),
        wphiT=np.ascontiguousarray(np.asarray(L0_attn_W_phi, f32).T),
        wgateT=np.ascontiguousarray(np.asarray(L0_attn_W_gate, f32).T),
        bpsi=np.asarray(L0_attn_b_psi, f32),
        bphi=np.asarray(L0_attn_b_phi, f32),
        bgate=np.asarray(L0_attn_b_gate, f32),
        omega=np.asarray(L0_attn_omega, f32),
        loggam=np.asarray(L0_attn_log_gamma, f32),
        dtv=np.asarray(L0_attn_dt, f32).reshape(1),
        rtemp=np.asarray(L0_ffn_routing_temp, f32).reshape(1),
        projwT=np.ascontiguousarray(np.asarray(L0_ffn_proj_w, f32).T),
        anchT=np.ascontiguousarray(np.asarray(L0_ffn_anchors, f32).T),
        w1=np.ascontiguousarray(np.asarray(L0_ffn_w1, f32)),
        w2=np.ascontiguousarray(np.asarray(L0_ffn_w2, f32)),
        **_host_constants(),
    )
    in_maps = []
    for core in range(NCORE):
        seq, ch = divmod(core, CPS)
        sel = np.zeros((3, NCORE), f32)
        for m in range(3):
            j = ch - 1 - m
            if j >= 0:
                sel[m, seq * CPS + j] = 1.0
        im = dict(shared)
        im["x"] = np.ascontiguousarray(x[seq, ch * C:(ch + 1) * C])
        im["sel"] = np.ascontiguousarray(sel.reshape(1, 24))
        in_maps.append(im)

    res = run_bass_kernel_spmd(nc, in_maps, core_ids=list(range(NCORE)))
    out = np.empty((B, L, D), np.float32)
    for core in range(NCORE):
        seq, ch = divmod(core, CPS)
        out[seq, ch * C:(ch + 1) * C] = res.results[core]["out"]
    return out
